# version 08
# baseline (speedup 1.0000x reference)
import numpy as np
import concourse.bass as bass
import concourse.mybir as mybir
from concourse.bass_utils import run_bass_kernel_spmd

F32 = mybir.dt.float32
BF16 = mybir.dt.bfloat16
I32 = mybir.dt.int32
AF = mybir.ActivationFunctionType
OP = mybir.AluOpType
AX = mybir.AxisListType

B, S, D, H, DK, HID = 4, 4096, 1024, 16, 64, 3072
T = 2048
NQT = 4
EPS = 1e-5
DEQ = 1.0 / 127.0
DEQ2 = DEQ * DEQ

ENGS = ("sync", "gpsimd", "tensor", "scalar", "vector")
DMA_SEMS = ("ds", "dg")


class Sched:

    def __init__(self, limit=None):
        self.limit = limit
        self.total = 0
        self.q = {e: [] for e in ENGS}
        self.n = {s: 0 for s in ("ds", "dg", "pe", "act", "dve")}
        self.meta = {e: [] for e in ENGS}

    def check(self):
        ptr = {e: 0 for e in ENGS}
        cnt = {s: 0 for s in self.n}
        progress = True
        while progress:
            progress = False
            for e in ENGS:
                while ptr[e] < len(self.meta[e]):
                    waits, inc, tag = self.meta[e][ptr[e]]
                    if any(cnt[s] < t for (s, t) in waits):
                        break
                    if inc:
                        cnt[inc] += 16 if inc in DMA_SEMS else 1
                    ptr[e] += 1
                    progress = True
        stuck = {e: ptr[e] for e in ENGS if ptr[e] < len(self.meta[e])}
        if stuck:
            detail = {}
            for e in stuck:
                waits, inc, tag = self.meta[e][ptr[e]]
                detail[e] = (ptr[e], tag, [(s, t, cnt[s]) for (s, t) in waits if cnt[s] < t])
            raise RuntimeError(f"DEADLOCK: stuck steps {detail}")

    def step(self, eng, emit, waits=(), inc=None, tag=""):
        if self.limit is not None and self.total >= self.limit:
            return self.n.get(inc)
        self.total += 1
        waits = tuple((s, t) for (s, t) in waits if t)
        for s, t in waits:
            assert t <= self.n[s], f"wait target {s}={t} not yet emitted ({self.n[s]})"
        tgt = None
        if inc is not None:
            self.n[inc] += 16 if inc in DMA_SEMS else 1
            tgt = self.n[inc]

        def run(e, SM, waits=waits, emit=emit, inc=inc):
            for s, t in waits:
                e.wait_ge(SM[s], t)
            r = emit(e, SM)
            if inc is not None:
                assert r is not None, "inc requires a real instruction"
                r.then_inc(SM[inc], 16 if inc in DMA_SEMS else 1)
        self.q[eng].append(run)
        self.meta[eng].append((waits, inc, tag))
        return tgt


def build_nc(n_quads=4, n_heads=4, do_p3=True, do_p4=True, limit=None):
    nc = bass.Bass("TRN2", target_bir_lowering=False)

    x_in = nc.dram_tensor("x_in", [S, D], F32, kind="ExternalInput").ap()
    wq = nc.dram_tensor("wq", [D, D], F32, kind="ExternalInput").ap()
    wk = nc.dram_tensor("wk", [D, D], F32, kind="ExternalInput").ap()
    wv = nc.dram_tensor("wv", [D, D], F32, kind="ExternalInput").ap()
    wo = nc.dram_tensor("wo", [D, D], F32, kind="ExternalInput").ap()
    bq = nc.dram_tensor("bq", [D], F32, kind="ExternalInput").ap()
    bk = nc.dram_tensor("bk", [D], F32, kind="ExternalInput").ap()
    bv = nc.dram_tensor("bv", [D], F32, kind="ExternalInput").ap()
    bo = nc.dram_tensor("bo", [D], F32, kind="ExternalInput").ap()
    wg = nc.dram_tensor("wg", [HID, D], I32, kind="ExternalInput").ap()
    wu = nc.dram_tensor("wu", [HID, D], I32, kind="ExternalInput").ap()
    wd = nc.dram_tensor("wd", [D, HID], I32, kind="ExternalInput").ap()
    out = nc.dram_tensor("out", [T, D], F32, kind="ExternalOutput").ap()

    s_xbf = nc.dram_tensor("s_xbf", [S, D], BF16, kind="Internal").ap()
    s_wg = nc.dram_tensor("s_wg", [HID, D], BF16, kind="Internal").ap()
    s_wu = nc.dram_tensor("s_wu", [HID, D], BF16, kind="Internal").ap()
    s_wd = nc.dram_tensor("s_wd", [D, HID], BF16, kind="Internal").ap()
    s_x1b = nc.dram_tensor("s_x1b", [T, D], BF16, kind="Internal").ap()
    s_x1f = nc.dram_tensor("s_x1f", [T, D], F32, kind="Internal").ap()

    ident_dram = nc.inline_tensor(np.eye(128, dtype=np.float32), name="ident128")

    SBASE = 16512
    A = lambda name, shape, dt, off: nc.alloc_sbuf_tensor_at(name, shape, dt, offset=SBASE + off)
    XT = [A(f"XT{c}", [128, S], BF16, 8192 * c) for c in range(8)]
    WQq = A("WQq", [128, 2048], BF16, 65536)
    WKq = A("WKq", [128, 2048], BF16, 69632)
    WVq = A("WVq", [128, 2048], BF16, 73728)
    KTq = [A(f"KTq{p}", [128, S], BF16, 77824 + 8192 * p) for p in range(2)]
    QTq = [A(f"QTq{p}", [128, T], BF16, 94208 + 4096 * p) for p in range(2)]
    Vq = [A(f"Vq{t}", [128, 256], BF16, 102400 + 512 * t) for t in range(32)]
    SCBr = [A(f"SCBr{r}", [128, 1024], BF16, 118784 + 2048 * r) for r in range(4)]
    Kf32 = A("Kf32", [128, S], F32, 126976)
    AOT = [A(f"AOT{c}", [128, T], BF16, 151552 + 4096 * c) for c in range(8)]
    av_sb = A("av_sb", [128, 512], F32, 184320)
    rv = A("rv", [65, 512], F32, 186368)
    rvh = A("rvh", [65, 512], BF16, 188416)
    rvl = A("rvl", [65, 512], BF16, 189440)
    rvd = A("rvd", [65, 512], F32, 190464)
    Wo_sb = [A(f"Wo{c}", [128, 1024], BF16, 2048 * c) for c in range(8)]
    o_sb = [A(f"osb{i}", [128, 512], F32, 16384 + 2048 * i) for i in range(8)]
    xres = [A(f"xres{i}", [128, 1024], F32, 32768 + 4096 * i) for i in range(2)]
    xpre = [A(f"xpre{i}", [128, 1024], F32, 40960 + 4096 * i) for i in range(2)]
    sq1 = A("sq1", [128, 1024], F32, 49152)
    x1t = [A(f"x1t{i}", [128, 1024], F32, 53248 + 4096 * i) for i in range(2)]
    x1tb = [A(f"x1tb{i}", [128, 1024], BF16, 61440 + 2048 * i) for i in range(2)]
    X1F = [A(f"X1F{c}", [128, T], BF16, 4096 * c) for c in range(8)]
    WGT = [A(f"WGT{c}", [128, HID], BF16, 32768 + 6144 * c) for c in range(8)]
    WUT = [A(f"WUT{c}", [128, HID], BF16, 81920 + 6144 * c) for c in range(8)]
    WDT = [A(f"WDT{j}", [128, 1024], BF16, 131072 + 2048 * j) for j in range(24)]
    s_sb = A("s_sb", [128, 256], BF16, 180224)
    h_sb = [A(f"hsb{i}", [128, 256], BF16, 180736 + 512 * i) for i in range(2)]
    dn_sb = A("dn_sb", [128, 2048], F32, 181760)
    x2pre = [A(f"x2pre{i}", [128, 1024], F32, 189952 + 4096 * i) for i in range(2)]
    sq2 = A("sq2", [128, 1024], F32, 198144)
    x1re = A("x1re", [128, 1024], F32, 202240)
    P = lambda name, shape, dt: nc.alloc_sbuf_tensor(name, shape, dt, side="right")
    ident = P("ident", [128, 128], F32)
    ones_col = P("ones_col", [128, 1], BF16)
    ones_bc = P("ones_bc", [65, 64], BF16)
    ones_row = P("ones_row", [1, 128], BF16)
    zeros_row = P("zeros_row", [1, 128], BF16)
    bvrow = P("bvrow", [1, 1024], BF16)
    bq_sb = P("bq_sb", [128, 8], F32)
    bk_sb = P("bk_sb", [128, 8], F32)
    bo_sb = P("bo_sb", [128, 8], F32)
    ln_sum = P("ln_sum", [128, 1], F32)
    ln_ssq = P("ln_ssq", [128, 1], F32)
    ln_mean = P("ln_mean", [128, 1], F32)
    ln_var = P("ln_var", [128, 1], F32)
    ln_m2 = P("ln_m2", [128, 1], F32)
    ln_sd = P("ln_sd", [128, 1], F32)
    ln_rstd = P("ln_rstd", [128, 1], F32)
    kmean = P("kmean", [128, 1], F32)

    ps_big = nc.alloc_psum_tensor("ps_big", [128, 2048], F32)
    ps_b5 = nc.alloc_psum_tensor("ps_b5", [128, 512], F32)
    ps_bc = nc.alloc_psum_tensor("ps_bc", [128, 512], F32)
    ps_tok = nc.alloc_psum_tensor("ps_tok", [128, 1024], F32)

    sc = Sched(limit)
    st = sc.step

    m_xc = []
    for rq in range(4):
        m_xc.append(st("gpsimd", lambda e, M, rq=rq: e.dma_start(
            out=s_xbf[rq * 1024:(rq + 1) * 1024, :], in_=x_in[rq * 1024:(rq + 1) * 1024, :]), inc="dg"))
    m_bv = st("gpsimd", lambda e, M: e.dma_start(out=bvrow[0:1, :], in_=bv[None, :]), inc="dg")
    st("sync", lambda e, M: e.dma_start(out=ident[:], in_=ident_dram.ap()), inc="ds")
    for c in range(8):
        st("sync", lambda e, M, c=c: e.dma_start(out=bq_sb[:, c], in_=bq[c * 128:(c + 1) * 128]), inc="ds")
        st("sync", lambda e, M, c=c: e.dma_start(out=bk_sb[:, c], in_=bk[c * 128:(c + 1) * 128]), inc="ds")
        st("sync", lambda e, M, c=c: e.dma_start(out=bo_sb[:, c], in_=bo[c * 128:(c + 1) * 128]), inc="ds")
    m_bias = sc.n["ds"]
    st("vector", lambda e, M: e.memset(ones_col[:], 1.0), inc="dve")
    st("vector", lambda e, M: e.memset(ones_bc[:], 1.0), inc="dve")
    st("vector", lambda e, M: e.memset(ones_row[:], 1.0), inc="dve")
    m_ones = st("vector", lambda e, M: e.memset(zeros_row[:], 0.0), inc="dve")
    m_xt = 0
    for half in range(2):
        for c in range(8):
            m_xt = st("sync", lambda e, M, c=c, half=half: e.dma_start_transpose(
                XT[c][:, half * 2048:(half + 1) * 2048],
                s_xbf[half * 2048:(half + 1) * 2048, c * 128:(c + 1) * 128]),
                waits=[("dg", m_xc[2 * half + 1])], inc="ds")

    quarter_free = [0, 0, 0, 0]
    m_exp = [0] * 8
    m_b5_free = []
    m_bc_read = 0
    m_bc_done = 0
    m_avsb = 0
    m_mult = 0
    m_av_done = 0
    m_kcent_all = 0
    m_vq = [0] * 32
    m_wslice_read = 0
    m_rvchain = 0
    m_avmm_prev = {}
    m_norm_prev = 0

    m_wgc = m_wuc = m_wdc = 0
    for g in range(n_quads):
        if g == 1:
            m_wgc = st("gpsimd", lambda e, M: e.dma_start(out=s_wg, in_=wg), inc="dg")
            m_wuc = st("gpsimd", lambda e, M: e.dma_start(out=s_wu, in_=wu), inc="dg")
            m_wdc = st("gpsimd", lambda e, M: e.dma_start(out=s_wd, in_=wd), inc="dg")
        cols = slice(g * 256, (g + 1) * 256)
        m_wqs = st("gpsimd", lambda e, M, v=wq.rearrange("(c p) n -> p c n", p=128)[:, :, cols]:
                   e.dma_start(out=WQq.ap().rearrange("p (c n) -> p c n", c=8), in_=v),
                   waits=[("pe", m_wslice_read)], inc="dg")
        m_wks = st("gpsimd", lambda e, M, v=wk.rearrange("(c p) n -> p c n", p=128)[:, :, cols]:
                   e.dma_start(out=WKq.ap().rearrange("p (c n) -> p c n", c=8), in_=v),
                   waits=[("pe", m_wslice_read)], inc="dg")
        m_wvs = st("gpsimd", lambda e, M, v=wv.rearrange("(c p) n -> p c n", p=128)[:, :, cols]:
                   e.dma_start(out=WVq.ap().rearrange("p (c n) -> p c n", c=8), in_=v),
                   waits=[("pe", m_wslice_read)], inc="dg")

        m_kcent = [0, 0]
        for pc in range(2):
            m_kevac = [0] * 8
            for tcol in range(8):
                qb = tcol % 4
                w0 = [("ds", m_xt), ("dg", m_wks), ("act", quarter_free[qb]),
                      ("pe", m_av_done), ("dve", m_mult)]
                mm_last = None
                for ci in range(8):
                    def mmk(e, M, ci=ci, pc=pc, tcol=tcol, qb=qb):
                        return e.matmul(ps_big[:, qb * 512:qb * 512 + 512],
                                        WKq[:, ci * 256 + pc * 128: ci * 256 + pc * 128 + 128],
                                        XT[ci][:, tcol * 512:(tcol + 1) * 512],
                                        start=(ci == 0), stop=(ci == 7))
                    mm_last = st("tensor", mmk, waits=(w0 if ci == 0 else ()), inc=("pe" if ci == 7 else None))
                def kev(e, M, pc=pc, tcol=tcol, qb=qb, g=g):
                    return e.activation(Kf32[:, tcol * 512:(tcol + 1) * 512], ps_big[:, qb * 512:qb * 512 + 512],
                                        AF.Identity, bias=bk_sb[:, 2 * g + pc:2 * g + pc + 1])
                m_kevac[tcol] = st("scalar", kev,
                                   waits=[("pe", mm_last), ("ds", m_bias),
                                          ("dve", m_kcent[0] if pc == 1 else m_kcent_all)],
                                   inc="act")
                quarter_free[qb] = m_kevac[tcol]
            m_red = st("vector", lambda e, M: e.tensor_reduce(kmean[:], Kf32[:], AX.X, OP.add),
                       waits=[("act", m_kevac[7]), ("dve", m_kcent_all)], inc="dve")
            m_scl = st("vector", lambda e, M: e.tensor_scalar(kmean[:], kmean[:], 1.0 / S, None, OP.mult),
                       waits=[("dve", m_red)], inc="dve")
            m_kcent[pc] = st("vector", lambda e, M, pc=pc: e.tensor_scalar(
                KTq[pc][:], Kf32[:], kmean[:, 0:1], None, OP.subtract),
                waits=[("dve", m_scl), ("pe", m_av_done)], inc="dve")
        m_kcent_all = m_kcent[1]

        m_qevac = [[0] * NQT, [0] * NQT]
        for pc in range(2):
            for qt in range(NQT):
                qb = qt % 4
                w0 = [("act", quarter_free[qb]), ("dg", m_wqs)]
                mm_last = None
                for ci in range(8):
                    def mmq(e, M, ci=ci, pc=pc, qt=qt, qb=qb):
                        return e.matmul(ps_big[:, qb * 512:qb * 512 + 512],
                                        WQq[:, ci * 256 + pc * 128: ci * 256 + pc * 128 + 128],
                                        XT[ci][:, qt * 512:(qt + 1) * 512],
                                        start=(ci == 0), stop=(ci == 7))
                    mm_last = st("tensor", mmq, waits=(w0 if ci == 0 else ()), inc=("pe" if ci == 7 else None))
                def qev(e, M, pc=pc, qt=qt, qb=qb, g=g):
                    return e.activation(QTq[pc][:, qt * 512:(qt + 1) * 512], ps_big[:, qb * 512:qb * 512 + 512],
                                        AF.Identity, bias=bq_sb[:, 2 * g + pc:2 * g + pc + 1])
                m_qevac[pc][qt] = st("scalar", qev, waits=[("pe", mm_last)], inc="act")
                quarter_free[qb] = m_qevac[pc][qt]

        for tt in range(32):
            pjv = ps_b5 if tt % 2 == 0 else ps_bc
            w0 = [("dg", m_wvs), ("dve", m_ones), ("dg", m_bv)]
            if tt >= 2:
                w0.append(("dve", m_vq[tt - 2]))
            else:
                w0 += m_b5_free
            mm_last = None
            for ci in range(8):
                def mmv(e, M, ci=ci, tt=tt, pjv=pjv):
                    return e.matmul(pjv[:, 0:256],
                                    XT[ci][:, tt * 128:(tt + 1) * 128],
                                    WVq[:, ci * 256:(ci + 1) * 256],
                                    start=(ci == 0), stop=False)
                st("tensor", mmv, waits=(w0 if ci == 0 else ()))
            m_vmm = st("tensor", lambda e, M, pjv=pjv, g=g: e.matmul(
                pjv[:, 0:256], ones_row[:], bvrow[0:1, g * 256:(g + 1) * 256],
                start=False, stop=True), inc="pe")
            m_vq[tt] = st("vector", lambda e, M, tt=tt, pjv=pjv: e.tensor_copy(
                Vq[tt][:], pjv[:, 0:256]), waits=[("pe", m_vmm)], inc="dve")
        m_wslice_read = sc.n["pe"]
        m_b5_free = [("dve", m_vq[30]), ("dve", m_vq[31])]

        for pc in range(2):
            ch = 2 * g + pc
            for qt in range(NQT):
                m_avmm = dict(m_avmm_prev)
                for c in range(32):
                    i2 = c % 2
                    r4 = c % 4
                    w0 = [("dve", m_kcent[pc]), ("act", m_qevac[pc][qt]),
                          ("act", quarter_free[2 * i2]), ("act", quarter_free[2 * i2 + 1])]
                    def mmse(e, M, c=c, i2=i2, pc=pc, qt=qt):
                        return e.matmul(ps_big[:, i2 * 1024: i2 * 1024 + 512],
                                        KTq[pc][0:64, c * 128:(c + 1) * 128],
                                        QTq[pc][0:64, qt * 512:(qt + 1) * 512],
                                        start=True, stop=True, tile_position=(0, 0))
                    st("tensor", mmse, waits=w0)
                    def mmso(e, M, c=c, i2=i2, pc=pc, qt=qt):
                        return e.matmul(ps_big[:, i2 * 1024 + 512: i2 * 1024 + 1024],
                                        KTq[pc][64:128, c * 128:(c + 1) * 128],
                                        QTq[pc][64:128, qt * 512:(qt + 1) * 512],
                                        start=True, stop=True, tile_position=(64, 0))
                    m_smm = st("tensor", mmso, inc="pe")
                    m_e = st("scalar", lambda e, M, c=c, i2=i2, r4=r4: e.activation(
                        SCBr[r4][:], ps_big[:, i2 * 1024:(i2 + 1) * 1024], AF.Exp, scale=0.125),
                        waits=[("pe", m_smm), ("pe", m_avmm.get(c - 4, 0))], inc="act")
                    quarter_free[2 * i2] = m_e
                    quarter_free[2 * i2 + 1] = m_e
                    strt, stop = (c == 0), (c == 31)
                    wav = [("act", m_e)]
                    if strt:
                        wav += [("dve", m_norm_prev), ("dve", m_vq[31])]
                    def av_e(e, M, c=c, r4=r4, pc=pc, strt=strt, stop=stop):
                        return e.matmul(ps_b5[0:64, :], Vq[c][:, (2 * pc) * 64:(2 * pc) * 64 + 64],
                                        SCBr[r4][:, 0:512], start=strt, stop=stop, tile_position=(0, 0))
                    st("tensor", av_e, waits=wav)
                    def den_e(e, M, c=c, r4=r4, strt=strt, stop=stop):
                        return e.matmul(ps_b5[64:65, :], ones_col[:], SCBr[r4][:, 0:512],
                                        start=strt, stop=stop, tile_position=(0, 64))
                    st("tensor", den_e)
                    def av_o(e, M, c=c, r4=r4, pc=pc, strt=strt, stop=stop):
                        return e.matmul(ps_bc[64:128, :], Vq[c][:, (2 * pc + 1) * 64:(2 * pc + 1) * 64 + 64],
                                        SCBr[r4][:, 512:1024], start=strt, stop=stop, tile_position=(0, 64))
                    st("tensor", av_o)
                    def den_o(e, M, c=c, r4=r4, strt=strt, stop=stop):
                        return e.matmul(ps_bc[32:33, :], ones_col[:], SCBr[r4][:, 512:1024],
                                        start=strt, stop=stop, tile_position=(0, 32))
                    m_avmm[c] = st("tensor", den_o, inc="pe")
                m_avmm_prev = {c - 32: m_avmm[c] for c in range(28, 32)}
                m_av = m_avmm[31]
                m_r1 = st("vector", lambda e, M: e.reciprocal(rv[64:65, :], ps_b5[64:65, :]),
                          waits=[("pe", m_av), ("pe", m_bc_done)], inc="dve")
                m_r1b = st("vector", lambda e, M: e.reciprocal(rv[32:33, :], ps_bc[32:33, :]),
                           waits=[("dve", m_r1)], inc="dve")
                m_r2 = st("vector", lambda e, M: e.tensor_copy(rvh[64:65, :], rv[64:65, :]),
                          waits=[("dve", m_r1b)], inc="dve")
                m_r2b = st("vector", lambda e, M: e.tensor_copy(rvh[32:33, :], rv[32:33, :]),
                           waits=[("dve", m_r2)], inc="dve")
                m_r3 = st("vector", lambda e, M: e.tensor_tensor(rvd[64:65, :], rv[64:65, :], rvh[64:65, :], OP.subtract),
                          waits=[("dve", m_r2b)], inc="dve")
                m_r3b = st("vector", lambda e, M: e.tensor_tensor(rvd[32:33, :], rv[32:33, :], rvh[32:33, :], OP.subtract),
                           waits=[("dve", m_r3)], inc="dve")
                m_r4a = st("vector", lambda e, M: e.tensor_copy(rvl[64:65, :], rvd[64:65, :]),
                           waits=[("dve", m_r3b)], inc="dve")
                m_r4_ = st("vector", lambda e, M: e.tensor_copy(rvl[32:33, :], rvd[32:33, :]),
                           waits=[("dve", m_r4a)], inc="dve")
                m_ae = st("vector", lambda e, M: e.tensor_copy(av_sb[0:64, :], ps_b5[0:64, :]),
                          waits=[("pe", m_av), ("dve", m_mult)], inc="dve")
                m_ao = st("vector", lambda e, M: e.tensor_copy(av_sb[64:128, :], ps_bc[64:128, :]),
                          waits=[("dve", m_ae)], inc="dve")
                st("tensor", lambda e, M: e.matmul(ps_tok[0:64, 0:512], ones_bc[64:65, 0:64], rvh[64:65, :],
                                                   start=True, stop=False, tile_position=(64, 0)),
                   waits=[("dve", m_r4_), ("dve", m_mult)])
                st("tensor", lambda e, M: e.matmul(ps_tok[0:64, 0:512], ones_bc[64:65, 0:64], rvl[64:65, :],
                                                   start=False, stop=True, tile_position=(64, 0)))
                st("tensor", lambda e, M: e.matmul(ps_tok[64:128, 512:1024], ones_bc[32:33, 0:64], rvh[32:33, :],
                                                   start=True, stop=False, tile_position=(32, 64)))
                m_bc_done = st("tensor", lambda e, M: e.matmul(ps_tok[64:128, 512:1024], ones_bc[32:33, 0:64], rvl[32:33, :],
                                                               start=False, stop=True, tile_position=(32, 64)), inc="pe")
                m_me = st("vector", lambda e, M, ch=ch, qt=qt: e.tensor_tensor(
                    AOT[ch][0:64, qt * 512:(qt + 1) * 512], av_sb[0:64, :], ps_tok[0:64, 0:512], OP.mult),
                    waits=[("pe", m_bc_done), ("dve", m_ao)], inc="dve")
                m_mult = st("vector", lambda e, M, ch=ch, qt=qt: e.tensor_tensor(
                    AOT[ch][64:128, qt * 512:(qt + 1) * 512], av_sb[64:128, :], ps_tok[64:128, 512:1024], OP.mult),
                    waits=[("dve", m_me)], inc="dve")
                m_norm_prev = m_mult
                m_av_done = m_av

    p2_pe_done = sc.n["pe"]
    p2_mult_done = m_mult

    if not do_p3:
        do_p4 = False
    m_wo = 0
    for c in (range(8) if do_p3 else range(0)):
        m_wo = st("gpsimd", lambda e, M, c=c: e.dma_start(
            out=Wo_sb[c].ap(), in_=wo.rearrange("(c p) n -> p c n", p=128)[:, c, :]),
            waits=[("pe", p2_pe_done)], inc="dg")
    m_oev = [0] * 8
    m_tr_prev = 0
    m_xadd = [0, 0]
    m_sq_prev = 0
    m_ln_prev = 0
    m_var0_prev = 0
    m_x1store = [0, 0]
    m_xres_ld = [0, 0]
    m_ps_tok_free = 0
    for qt in (range(4) if do_p3 else range(0)):
        for outc in range(8):
            ps_o = ps_b5 if outc % 2 == 0 else ps_bc
            w0 = [("dg", m_wo), ("dve", p2_mult_done), ("act", m_oev[outc - 2] if outc >= 2 else 0),
                  ("pe", m_tr_prev)]
            if qt == 0 and outc < 2:
                w0 += [("dve", p2_mult_done)]
            mm_last = None
            for ci in range(8):
                def mmo(e, M, ci=ci, outc=outc, qt=qt, ps_o=ps_o):
                    return e.matmul(ps_o[:], Wo_sb[ci][:, outc * 128:(outc + 1) * 128],
                                    AOT[ci][:, qt * 512:(qt + 1) * 512],
                                    start=(ci == 0), stop=(ci == 7))
                mm_last = st("tensor", mmo, waits=(w0 if ci == 0 else ()), inc=("pe" if ci == 7 else None))
            m_oev[outc] = st("scalar", lambda e, M, outc=outc, ps_o=ps_o: e.activation(
                o_sb[outc][:], ps_o[:], AF.Identity, bias=bo_sb[:, outc:outc + 1]),
                waits=[("pe", mm_last), ("pe", m_tr_prev)], inc="act")
        for qs in range(4):
            i2 = qs % 2
            tok0 = qt * 512 + qs * 128
            m_xres_ld[i2] = st("sync", lambda e, M, i2=i2, tok0=tok0: e.dma_start(
                out=xres[i2][:], in_=x_in[tok0:tok0 + 128, :]),
                waits=[("dve", m_xadd[i2]), ("pe", p2_pe_done)], inc="ds")
            m_tr = None
            for outc in range(8):
                def trp(e, M, outc=outc, qs=qs):
                    return e.transpose(ps_tok[:, outc * 128:(outc + 1) * 128],
                                      o_sb[outc][:, qs * 128:(qs + 1) * 128], ident[:])
                m_tr = st("tensor", trp,
                          waits=([("act", m_oev[7]), ("dve", m_ps_tok_free), ("dve", p2_mult_done)] if outc == 0 else ()),
                          inc=("pe" if outc == 7 else None))
            m_tr_prev = m_tr
            m_add = st("vector", lambda e, M, i2=i2: e.tensor_tensor(
                xpre[i2][:], ps_tok[:], xres[i2][:], OP.add),
                waits=[("pe", m_tr), ("ds", m_xres_ld[i2]), ("act", m_sq_prev), ("dve", m_ln_prev)], inc="dve")
            m_ps_tok_free = m_add
            m_xadd[i2] = m_add
            m_red = st("vector", lambda e, M, i2=i2: e.tensor_reduce(
                ln_sum[:], xpre[i2][:], AX.X, OP.add), waits=[("dve", m_add), ("dve", m_ln_prev)], inc="dve")
            m_sq = st("scalar", lambda e, M, i2=i2: e.activation(
                sq1[:], xpre[i2][:], AF.Square, accum_out=ln_ssq[:, 0:1]),
                waits=[("dve", m_add), ("dve", m_var0_prev)], inc="act")
            m_sq_prev = m_sq
            m_mean = st("vector", lambda e, M: e.tensor_scalar(
                ln_mean[:], ln_sum[:], 1.0 / D, None, OP.mult), waits=[("dve", m_red)], inc="dve")
            m_m2 = st("vector", lambda e, M: e.tensor_tensor(
                ln_m2[:], ln_mean[:], ln_mean[:], OP.mult), waits=[("dve", m_mean)], inc="dve")
            m_var0 = st("vector", lambda e, M: e.tensor_scalar(
                ln_var[:], ln_ssq[:], 1.0 / D, EPS, OP.mult, OP.add),
                waits=[("act", m_sq), ("dve", m_m2)], inc="dve")
            m_var0_prev = m_var0
            m_var = st("vector", lambda e, M: e.tensor_tensor(
                ln_var[:], ln_var[:], ln_m2[:], OP.subtract), waits=[("dve", m_var0)], inc="dve")
            m_sd = st("scalar", lambda e, M: e.activation(
                ln_sd[:], ln_var[:], AF.Sqrt), waits=[("dve", m_var)], inc="act")
            m_rstd = st("vector", lambda e, M: e.reciprocal(
                ln_rstd[:], ln_sd[:]), waits=[("act", m_sd)], inc="dve")
            m_app = st("vector", lambda e, M, i2=i2: e.tensor_scalar(
                x1t[i2][:], xpre[i2][:], ln_mean[:, 0:1], ln_rstd[:, 0:1], OP.subtract, OP.mult),
                waits=[("dve", m_rstd), ("dg", m_x1store[i2])], inc="dve")
            m_cast = st("vector", lambda e, M, i2=i2: e.tensor_copy(
                x1tb[i2][:], x1t[i2][:]), waits=[("dve", m_app)], inc="dve")
            m_ln_prev = m_app
            st("gpsimd", lambda e, M, i2=i2, tok0=tok0: e.dma_start(
                out=s_x1f[tok0:tok0 + 128, :], in_=x1t[i2][:]), waits=[("dve", m_app)], inc="dg")
            m_x1store[i2] = st("gpsimd", lambda e, M, i2=i2, tok0=tok0: e.dma_start(
                out=s_x1b[tok0:tok0 + 128, :], in_=x1tb[i2][:]), waits=[("dve", m_cast)], inc="dg")

    p3_pe = sc.n["pe"]
    p3_act = sc.n["act"]
    p3_dve = sc.n["dve"]
    p3_dg = sc.n["dg"]
    R8 = range(8) if do_p4 else range(0)
    R24 = range(24) if do_p4 else range(0)

    p3w = [("pe", p3_pe), ("act", p3_act), ("dve", p3_dve), ("dg", p3_dg)]
    m_x1f = 0
    for c in R8:
        m_x1f = st("sync", lambda e, M, c=c: e.dma_start_transpose(
            X1F[c][:], s_x1b[:, c * 128:(c + 1) * 128]), waits=p3w, inc="ds")
    m_wgt = 0
    for c in R8:
        m_wgt = st("sync", lambda e, M, c=c: e.dma_start_transpose(
            WGT[c][:], s_wg[:, c * 128:(c + 1) * 128]), waits=p3w + [("dg", m_wgc)], inc="ds")
    m_wut = 0
    for c in R8:
        m_wut = st("sync", lambda e, M, c=c: e.dma_start_transpose(
            WUT[c][:], s_wu[:, c * 128:(c + 1) * 128]), waits=p3w + [("dg", m_wuc)], inc="ds")
    m_wdt = 0
    for j in R24:
        m_wdt = st("sync", lambda e, M, j=j: e.dma_start_transpose(
            WDT[j][:], s_wd[:, j * 128:(j + 1) * 128]), waits=p3w + [("dg", m_wdc)], inc="ds")

    m_si = [0] * 24
    m_h = [0] * 24
    m_dnj = [0] * 24
    m_dnev = [0] * 8
    m_x2add = [0, 0]
    m_x2store = [0, 0]
    m_tr2_prev = 0
    m_x1re_read = 0
    m_dnev_last = 0
    for qt in (range(8) if do_p4 else range(0)):
        for j in range(24):
            pj = ps_b5 if j % 2 == 0 else ps_bc
            w0 = [("ds", m_wgt), ("ds", m_wut), ("ds", m_x1f)]
            if j >= 2:
                w0 += [("act", m_si[j - 2]), ("dve", m_h[j - 2])]
            elif qt > 0:
                w0 += [("act", m_si[22 + j]), ("dve", m_h[22 + j])]
            else:
                w0 += [("dve", p2_mult_done)]
            mm_last = None
            for ci in range(8):
                def mmg(e, M, ci=ci, j=j, qt=qt, pj=pj):
                    return e.matmul(pj[:, 0:256], WGT[ci][:, j * 128:(j + 1) * 128],
                                    X1F[ci][:, qt * 256:(qt + 1) * 256],
                                    start=(ci == 0), stop=(ci == 7))
                st("tensor", mmg, waits=(w0 if ci == 0 else ()))
            for ci in range(8):
                def mmu(e, M, ci=ci, j=j, qt=qt, pj=pj):
                    return e.matmul(pj[:, 256:512], WUT[ci][:, j * 128:(j + 1) * 128],
                                    X1F[ci][:, qt * 256:(qt + 1) * 256],
                                    start=(ci == 0), stop=(ci == 7))
                mm_last = st("tensor", mmu, inc=("pe" if ci == 7 else None))
            m_si[j] = st("scalar", lambda e, M, pj=pj: e.activation(
                s_sb[:], pj[:, 0:256], AF.Silu, scale=DEQ),
                waits=[("pe", mm_last)] + ([("dve", m_h[j - 1])] if j >= 1 else
                                           ([("dve", m_h[23])] if qt > 0 else [])), inc="act")
            m_h[j] = st("vector", lambda e, M, j=j, pj=pj: e.tensor_tensor(
                h_sb[j % 2][:], s_sb[:], pj[:, 256:512], OP.mult),
                waits=[("act", m_si[j])] + ([("pe", m_dnj[j - 2])] if j >= 2 else
                                            ([("pe", m_dnj[22 + j])] if qt > 0 else [])), inc="dve")
            w_dn = [("dve", m_h[j]), ("ds", m_wdt)]
            if j == 0:
                w_dn += [("dve", m_x2add[0]), ("dve", m_x2add[1]), ("act", m_dnev_last)]
                for bank in range(4):
                    def mmz(e, M, bank=bank):
                        return e.matmul(ps_big[:, bank * 512:(bank + 1) * 512],
                                        zeros_row[:], bvrow[0:1, 0:512], start=True, stop=False)
                    st("tensor", mmz, waits=(w_dn if bank == 0 else ()))
                w_dn = []
            mm_last = None
            for outc in range(8):
                def mmdn(e, M, outc=outc, j=j):
                    return e.matmul(ps_big[:, outc * 256:(outc + 1) * 256],
                                    WDT[j][:, outc * 128:(outc + 1) * 128], h_sb[j % 2][:],
                                    start=False, stop=(j == 23))
                mm_last = st("tensor", mmdn, waits=(w_dn if outc == 0 else ()), inc=("pe" if outc == 7 else None))
            m_dnj[j] = mm_last
        for outc in range(8):
            m_dnev[outc] = st("scalar", lambda e, M, outc=outc: e.activation(
                dn_sb[:, outc * 256:(outc + 1) * 256], ps_big[:, outc * 256:(outc + 1) * 256],
                AF.Identity, scale=DEQ2),
                waits=[("pe", m_dnj[23]), ("pe", m_tr2_prev)], inc="act")
        m_dnev_last = m_dnev[7]
        for qs in range(2):
            i2 = qs % 2
            tok0 = qt * 256 + qs * 128
            m_ld = st("sync", lambda e, M, tok0=tok0: e.dma_start(
                out=x1re[:], in_=s_x1f[tok0:tok0 + 128, :]),
                waits=[("dve", m_x1re_read), ("dg", p3_dg)], inc="ds")
            m_tr = None
            for outc in range(8):
                def trp2(e, M, outc=outc, qs=qs):
                    return e.transpose(ps_big[:, qs * 1024 + outc * 128: qs * 1024 + (outc + 1) * 128],
                                      dn_sb[:, outc * 256 + qs * 128: outc * 256 + (qs + 1) * 128], ident[:])
                m_tr = st("tensor", trp2,
                          waits=([("act", m_dnev[7]), ("dve", m_x2add[i2])] if outc == 0 else ()),
                          inc=("pe" if outc == 7 else None))
            m_tr2_prev = m_tr
            m_add = st("vector", lambda e, M, i2=i2, qs=qs: e.tensor_tensor(
                x2pre[i2][:], ps_big[:, qs * 1024:(qs + 1) * 1024], x1re[:], OP.add),
                waits=[("pe", m_tr), ("ds", m_ld), ("dg", m_x2store[i2]),
                       ("act", m_sq_prev), ("dve", m_ln_prev)], inc="dve")
            m_x1re_read = m_add
            m_x2add[i2] = m_add
            m_red = st("vector", lambda e, M, i2=i2: e.tensor_reduce(
                ln_sum[:], x2pre[i2][:], AX.X, OP.add), waits=[("dve", m_add), ("dve", m_ln_prev)], inc="dve")
            m_sq = st("scalar", lambda e, M, i2=i2: e.activation(
                sq2[:], x2pre[i2][:], AF.Square, accum_out=ln_ssq[:, 0:1]),
                waits=[("dve", m_add), ("dve", m_var0_prev)], inc="act")
            m_sq_prev = m_sq
            m_mean = st("vector", lambda e, M: e.tensor_scalar(
                ln_mean[:], ln_sum[:], 1.0 / D, None, OP.mult), waits=[("dve", m_red)], inc="dve")
            m_m2 = st("vector", lambda e, M: e.tensor_tensor(
                ln_m2[:], ln_mean[:], ln_mean[:], OP.mult), waits=[("dve", m_mean)], inc="dve")
            m_var0 = st("vector", lambda e, M: e.tensor_scalar(
                ln_var[:], ln_ssq[:], 1.0 / D, EPS, OP.mult, OP.add),
                waits=[("act", m_sq), ("dve", m_m2)], inc="dve")
            m_var0_prev = m_var0
            m_var = st("vector", lambda e, M: e.tensor_tensor(
                ln_var[:], ln_var[:], ln_m2[:], OP.subtract), waits=[("dve", m_var0)], inc="dve")
            m_sd = st("scalar", lambda e, M: e.activation(
                ln_sd[:], ln_var[:], AF.Sqrt), waits=[("dve", m_var)], inc="act")
            m_rstd = st("vector", lambda e, M: e.reciprocal(
                ln_rstd[:], ln_sd[:]), waits=[("act", m_sd)], inc="dve")
            m_app = st("vector", lambda e, M, i2=i2: e.tensor_scalar(
                x2pre[i2][:], x2pre[i2][:], ln_mean[:, 0:1], ln_rstd[:, 0:1], OP.subtract, OP.mult),
                waits=[("dve", m_rstd)], inc="dve")
            m_ln_prev = m_app
            m_x2store[i2] = st("gpsimd", lambda e, M, i2=i2, tok0=tok0: e.dma_start(
                out=out[tok0:tok0 + 128, :], in_=x2pre[i2][:]), waits=[("dve", m_app)], inc="dg")

    final_dg = sc.n["dg"]
    sc.q["gpsimd"].append(lambda e, SM: e.wait_ge(SM["dg"], final_dg))
    sc.meta["gpsimd"].append(((("dg", final_dg),), None, "final"))
    sc.check()

    from contextlib import ExitStack
    es = ExitStack()
    SM = {s: es.enter_context(nc.semaphore(s)) for s in ("ds", "dg", "pe", "act", "dve")}
    block = es.enter_context(nc.Block())

    @block.sync
    def _(e):
        for f in sc.q["sync"]:
            f(e, SM)

    @block.gpsimd
    def _(e):
        for f in sc.q["gpsimd"]:
            f(e, SM)

    @block.tensor
    def _(e):
        for f in sc.q["tensor"]:
            f(e, SM)

    @block.scalar
    def _(e):
        for f in sc.q["scalar"]:
            f(e, SM)

    @block.vector
    def _(e):
        for f in sc.q["vector"]:
            f(e, SM)

    es.close()
    return nc


_NC_CACHE = None


def kernel(**inputs):
    global _NC_CACHE
    X = np.asarray(inputs["L0_attn_X"], dtype=np.float32)
    args = {
        "wq": np.ascontiguousarray(inputs["L0_attn_Wq"], dtype=np.float32),
        "wk": np.ascontiguousarray(inputs["L0_attn_Wk"], dtype=np.float32),
        "wv": np.ascontiguousarray(inputs["L0_attn_Wv"], dtype=np.float32),
        "wo": np.ascontiguousarray(inputs["L0_attn_Wo"], dtype=np.float32),
        "bq": np.ascontiguousarray(inputs["L0_attn_bq"], dtype=np.float32),
        "bk": np.ascontiguousarray(inputs["L0_attn_bk"], dtype=np.float32),
        "bv": np.ascontiguousarray(inputs["L0_attn_bv"], dtype=np.float32),
        "bo": np.ascontiguousarray(inputs["L0_attn_bo"], dtype=np.float32),
        "wg": np.ascontiguousarray(inputs["L0_ffn_w_gate"], dtype=np.int32),
        "wu": np.ascontiguousarray(inputs["L0_ffn_w_up"], dtype=np.int32),
        "wd": np.ascontiguousarray(inputs["L0_ffn_w_down"], dtype=np.int32),
    }
    if _NC_CACHE is None:
        _NC_CACHE = build_nc()
    nc = _NC_CACHE
    in_maps = []
    for c in range(8):
        b, qh = c // 2, c % 2
        xb = X[b]
        if qh:
            xb = np.roll(xb, -T, axis=0)
        m = {"x_in": np.ascontiguousarray(xb)}
        m.update(args)
        in_maps.append(m)
    res = run_bass_kernel_spmd(nc, in_maps, core_ids=list(range(8)))
    outp = np.empty((B, S, D), dtype=np.float32)
    for c in range(8):
        b, qh = c // 2, c % 2
        outp[b, qh * T:(qh + 1) * T] = res.results[c]["out"]
    return outp


if __name__ == "__main__":
    rng = np.random.default_rng(0)
    ins = {
        "L0_attn_X": rng.standard_normal((B, S, D), dtype=np.float32),
        "L0_attn_Wq": rng.standard_normal((D, D), dtype=np.float32) / 32,
        "L0_attn_bq": rng.standard_normal(D, dtype=np.float32) * 0.1,
        "L0_attn_Wk": rng.standard_normal((D, D), dtype=np.float32) / 32,
        "L0_attn_bk": rng.standard_normal(D, dtype=np.float32) * 0.1,
        "L0_attn_Wv": rng.standard_normal((D, D), dtype=np.float32) / 32,
        "L0_attn_bv": rng.standard_normal(D, dtype=np.float32) * 0.1,
        "L0_attn_Wo": rng.standard_normal((D, D), dtype=np.float32) / 32,
        "L0_attn_bo": rng.standard_normal(D, dtype=np.float32) * 0.1,
        "L0_ffn_w_gate": rng.integers(-127, 128, (HID, D)).astype(np.int32),
        "L0_ffn_w_up": rng.integers(-127, 128, (HID, D)).astype(np.int32),
        "L0_ffn_w_down": rng.integers(-127, 128, (D, HID)).astype(np.int32),
    }
    y = kernel(**ins)
    print("out", y.shape, y.dtype, np.abs(y).max())


# version 09
# speedup vs baseline: 1.6594x; 1.6594x over previous
import numpy as np
import concourse.bass as bass
import concourse.mybir as mybir
from concourse.bass_utils import run_bass_kernel_spmd

F32 = mybir.dt.float32
BF16 = mybir.dt.bfloat16
I32 = mybir.dt.int32
AF = mybir.ActivationFunctionType
OP = mybir.AluOpType
AX = mybir.AxisListType

B, S, D, H, DK, HID = 4, 4096, 1024, 16, 64, 3072
T = 2048
NQT = 4
EPS = 1e-5
DEQ = 1.0 / 127.0
DEQ2 = DEQ * DEQ

ENGS = ("sync", "gpsimd", "tensor", "scalar", "vector")
DMA_SEMS = ("ds", "dg")


class Sched:

    def __init__(self, limit=None):
        self.limit = limit
        self.total = 0
        self.q = {e: [] for e in ENGS}
        self.n = {s: 0 for s in ("ds", "dg", "pe", "act", "dve")}
        self.meta = {e: [] for e in ENGS}

    def check(self):
        ptr = {e: 0 for e in ENGS}
        cnt = {s: 0 for s in self.n}
        progress = True
        while progress:
            progress = False
            for e in ENGS:
                while ptr[e] < len(self.meta[e]):
                    waits, inc, tag = self.meta[e][ptr[e]]
                    if any(cnt[s] < t for (s, t) in waits):
                        break
                    if inc:
                        cnt[inc] += 16 if inc in DMA_SEMS else 1
                    ptr[e] += 1
                    progress = True
        stuck = {e: ptr[e] for e in ENGS if ptr[e] < len(self.meta[e])}
        if stuck:
            detail = {}
            for e in stuck:
                waits, inc, tag = self.meta[e][ptr[e]]
                detail[e] = (ptr[e], tag, [(s, t, cnt[s]) for (s, t) in waits if cnt[s] < t])
            raise RuntimeError(f"DEADLOCK: stuck steps {detail}")

    def step(self, eng, emit, waits=(), inc=None, tag=""):
        if self.limit is not None and self.total >= self.limit:
            return self.n.get(inc)
        self.total += 1
        waits = tuple((s, t) for (s, t) in waits if t)
        for s, t in waits:
            assert t <= self.n[s], f"wait target {s}={t} not yet emitted ({self.n[s]})"
        tgt = None
        if inc is not None:
            self.n[inc] += 16 if inc in DMA_SEMS else 1
            tgt = self.n[inc]

        def run(e, SM, waits=waits, emit=emit, inc=inc):
            for s, t in waits:
                e.wait_ge(SM[s], t)
            r = emit(e, SM)
            if inc is not None:
                assert r is not None, "inc requires a real instruction"
                r.then_inc(SM[inc], 16 if inc in DMA_SEMS else 1)
        self.q[eng].append(run)
        self.meta[eng].append((waits, inc, tag))
        return tgt


def build_nc(n_quads=4, n_heads=4, do_p3=True, do_p4=True, limit=None):
    nc = bass.Bass("TRN2", target_bir_lowering=False)

    x_in = nc.dram_tensor("x_in", [S, D], F32, kind="ExternalInput").ap()
    wq = nc.dram_tensor("wq", [D, D], F32, kind="ExternalInput").ap()
    wk = nc.dram_tensor("wk", [D, D], F32, kind="ExternalInput").ap()
    wv = nc.dram_tensor("wv", [D, D], F32, kind="ExternalInput").ap()
    wo = nc.dram_tensor("wo", [D, D], F32, kind="ExternalInput").ap()
    bq = nc.dram_tensor("bq", [D], F32, kind="ExternalInput").ap()
    bk = nc.dram_tensor("bk", [D], F32, kind="ExternalInput").ap()
    bv = nc.dram_tensor("bv", [D], F32, kind="ExternalInput").ap()
    bo = nc.dram_tensor("bo", [D], F32, kind="ExternalInput").ap()
    wg = nc.dram_tensor("wg", [HID, D], I32, kind="ExternalInput").ap()
    wu = nc.dram_tensor("wu", [HID, D], I32, kind="ExternalInput").ap()
    wd = nc.dram_tensor("wd", [D, HID], I32, kind="ExternalInput").ap()
    out = nc.dram_tensor("out", [T, D], F32, kind="ExternalOutput").ap()

    s_xbf = nc.dram_tensor("s_xbf", [S, D], BF16, kind="Internal").ap()
    s_wg = nc.dram_tensor("s_wg", [HID, D], BF16, kind="Internal").ap()
    s_wu = nc.dram_tensor("s_wu", [HID, D], BF16, kind="Internal").ap()
    s_wd = nc.dram_tensor("s_wd", [D, HID], BF16, kind="Internal").ap()
    s_x1b = nc.dram_tensor("s_x1b", [T, D], BF16, kind="Internal").ap()
    s_x1f = nc.dram_tensor("s_x1f", [T, D], F32, kind="Internal").ap()

    ident_dram = nc.inline_tensor(np.eye(128, dtype=np.float32), name="ident128")

    SBASE = 16512
    A = lambda name, shape, dt, off: nc.alloc_sbuf_tensor_at(name, shape, dt, offset=SBASE + off)
    XT = [A(f"XT{c}", [128, S], BF16, 8192 * c) for c in range(8)]
    WQq = A("WQq", [128, 2048], BF16, 65536)
    WKq = A("WKq", [128, 2048], BF16, 69632)
    WVq = A("WVq", [128, 2048], BF16, 73728)
    KTq = [A(f"KTq{p}", [128, S], BF16, 77824 + 8192 * p) for p in range(2)]
    QTq = [A(f"QTq{p}", [128, T], BF16, 94208 + 4096 * p) for p in range(2)]
    Vq = [A(f"Vq{t}", [128, 256], BF16, 102400 + 512 * t) for t in range(32)]
    SCBr = [A(f"SCBr{r}", [128, 1024], BF16, 118784 + 2048 * r) for r in range(4)]
    Kf32 = A("Kf32", [128, S], F32, 126976)
    AOT = [A(f"AOT{c}", [128, T], BF16, 151552 + 4096 * c) for c in range(8)]
    av_sb = A("av_sb", [128, 512], F32, 184320)
    rv = A("rv", [65, 512], F32, 186368)
    rvh = A("rvh", [65, 512], BF16, 188416)
    rvl = A("rvl", [65, 512], BF16, 189440)
    rvd = A("rvd", [65, 512], F32, 190464)
    Wo_sb = [A(f"Wo{c}", [128, 1024], BF16, 2048 * c) for c in range(8)]
    o_sb = [A(f"osb{i}", [128, 512], F32, 16384 + 2048 * i) for i in range(8)]
    xres = [A(f"xres{i}", [128, 1024], F32, 32768 + 4096 * i) for i in range(2)]
    xpre = [A(f"xpre{i}", [128, 1024], F32, 40960 + 4096 * i) for i in range(2)]
    sq1 = A("sq1", [128, 1024], F32, 49152)
    x1t = [A(f"x1t{i}", [128, 1024], F32, 53248 + 4096 * i) for i in range(2)]
    x1tb = [A(f"x1tb{i}", [128, 1024], BF16, 61440 + 2048 * i) for i in range(2)]
    X1F = [A(f"X1F{c}", [128, T], BF16, 4096 * c) for c in range(8)]
    WGT = [A(f"WGT{c}", [128, HID], BF16, 32768 + 6144 * c) for c in range(8)]
    WUT = [A(f"WUT{c}", [128, HID], BF16, 81920 + 6144 * c) for c in range(8)]
    WDT = [A(f"WDT{j}", [128, 1024], BF16, 131072 + 2048 * j) for j in range(24)]
    s_sb = A("s_sb", [128, 256], BF16, 180224)
    h_sb = [A(f"hsb{i}", [128, 256], BF16, 180736 + 512 * i) for i in range(2)]
    dn_sb = A("dn_sb", [128, 2048], F32, 181760)
    x2pre = [A(f"x2pre{i}", [128, 1024], F32, 189952 + 4096 * i) for i in range(2)]
    sq2 = A("sq2", [128, 1024], F32, 198144)
    x1re = A("x1re", [128, 1024], F32, 202240)
    P = lambda name, shape, dt: nc.alloc_sbuf_tensor(name, shape, dt, side="right")
    ident = P("ident", [128, 128], F32)
    ones_col = P("ones_col", [128, 1], BF16)
    ones_bc = P("ones_bc", [65, 64], BF16)
    ones_row = P("ones_row", [1, 128], BF16)
    zeros_row = P("zeros_row", [1, 128], BF16)
    bvrow = P("bvrow", [1, 1024], BF16)
    bq_sb = P("bq_sb", [128, 8], F32)
    bk_sb = P("bk_sb", [128, 8], F32)
    bo_sb = P("bo_sb", [128, 8], F32)
    ln_sum = P("ln_sum", [128, 1], F32)
    ln_ssq = P("ln_ssq", [128, 1], F32)
    ln_mean = P("ln_mean", [128, 1], F32)
    ln_var = P("ln_var", [128, 1], F32)
    ln_m2 = P("ln_m2", [128, 1], F32)
    ln_sd = P("ln_sd", [128, 1], F32)
    ln_rstd = P("ln_rstd", [128, 1], F32)
    kmean = P("kmean", [128, 1], F32)

    ps_big = nc.alloc_psum_tensor("ps_big", [128, 2048], F32)
    ps_b5 = nc.alloc_psum_tensor("ps_b5", [128, 512], F32)
    ps_bc = nc.alloc_psum_tensor("ps_bc", [128, 512], F32)
    ps_tok = nc.alloc_psum_tensor("ps_tok", [128, 1024], F32)

    sc = Sched(limit)
    st = sc.step

    m_xc = []
    for rq in range(4):
        m_xc.append(st("gpsimd", lambda e, M, rq=rq: e.dma_start(
            out=s_xbf[rq * 1024:(rq + 1) * 1024, :], in_=x_in[rq * 1024:(rq + 1) * 1024, :]), inc="dg"))
    m_bv = st("gpsimd", lambda e, M: e.dma_start(out=bvrow[0:1, :], in_=bv[None, :]), inc="dg")
    st("sync", lambda e, M: e.dma_start(out=ident[:], in_=ident_dram.ap()), inc="ds")
    for c in range(8):
        st("sync", lambda e, M, c=c: e.dma_start(out=bq_sb[:, c], in_=bq[c * 128:(c + 1) * 128]), inc="ds")
        st("sync", lambda e, M, c=c: e.dma_start(out=bk_sb[:, c], in_=bk[c * 128:(c + 1) * 128]), inc="ds")
        st("sync", lambda e, M, c=c: e.dma_start(out=bo_sb[:, c], in_=bo[c * 128:(c + 1) * 128]), inc="ds")
    m_bias = sc.n["ds"]
    st("vector", lambda e, M: e.memset(ones_col[:], 1.0), inc="dve")
    st("vector", lambda e, M: e.memset(ones_bc[:], 1.0), inc="dve")
    st("vector", lambda e, M: e.memset(ones_row[:], 1.0), inc="dve")
    m_ones = st("vector", lambda e, M: e.memset(zeros_row[:], 0.0), inc="dve")
    m_xt = 0
    for half in range(2):
        for c in range(8):
            m_xt = st("sync", lambda e, M, c=c, half=half: e.dma_start_transpose(
                XT[c][:, half * 2048:(half + 1) * 2048],
                s_xbf[half * 2048:(half + 1) * 2048, c * 128:(c + 1) * 128]),
                waits=[("dg", m_xc[2 * half + 1])], inc="ds")

    quarter_free = [0, 0, 0, 0]
    m_exp = [0] * 8
    m_b5_free = []
    m_bc_read = 0
    m_bc_done = 0
    m_avsb = 0
    m_mult = 0
    m_av_done = 0
    m_kcent_all = 0
    m_vq = [0] * 32
    m_wslice_read = 0
    m_rvchain = 0
    m_avmm_prev = {}
    m_exp_prev = {}
    m_norm_prev = 0

    m_wgc = m_wuc = m_wdc = 0
    for g in range(n_quads):
        if g == 1:
            m_wgc = st("gpsimd", lambda e, M: e.dma_start(out=s_wg, in_=wg), inc="dg")
            m_wuc = st("gpsimd", lambda e, M: e.dma_start(out=s_wu, in_=wu), inc="dg")
            m_wdc = st("gpsimd", lambda e, M: e.dma_start(out=s_wd, in_=wd), inc="dg")
        cols = slice(g * 256, (g + 1) * 256)
        m_wqs = st("gpsimd", lambda e, M, v=wq.rearrange("(c p) n -> p c n", p=128)[:, :, cols]:
                   e.dma_start(out=WQq.ap().rearrange("p (c n) -> p c n", c=8), in_=v),
                   waits=[("pe", m_wslice_read)], inc="dg")
        m_wks = st("gpsimd", lambda e, M, v=wk.rearrange("(c p) n -> p c n", p=128)[:, :, cols]:
                   e.dma_start(out=WKq.ap().rearrange("p (c n) -> p c n", c=8), in_=v),
                   waits=[("pe", m_wslice_read)], inc="dg")
        m_wvs = st("gpsimd", lambda e, M, v=wv.rearrange("(c p) n -> p c n", p=128)[:, :, cols]:
                   e.dma_start(out=WVq.ap().rearrange("p (c n) -> p c n", c=8), in_=v),
                   waits=[("pe", m_wslice_read)], inc="dg")

        m_kcent = [0, 0]
        for pc in range(2):
            m_kevac = [0] * 8
            for tcol in range(8):
                qb = tcol % 4
                w0 = [("ds", m_xt), ("dg", m_wks), ("act", quarter_free[qb]),
                      ("pe", m_av_done), ("dve", m_mult)]
                mm_last = None
                for ci in range(8):
                    def mmk(e, M, ci=ci, pc=pc, tcol=tcol, qb=qb):
                        return e.matmul(ps_big[:, qb * 512:qb * 512 + 512],
                                        WKq[:, ci * 256 + pc * 128: ci * 256 + pc * 128 + 128],
                                        XT[ci][:, tcol * 512:(tcol + 1) * 512],
                                        start=(ci == 0), stop=(ci == 7))
                    mm_last = st("tensor", mmk, waits=(w0 if ci == 0 else ()), inc=("pe" if ci == 7 else None))
                def kev(e, M, pc=pc, tcol=tcol, qb=qb, g=g):
                    return e.activation(Kf32[:, tcol * 512:(tcol + 1) * 512], ps_big[:, qb * 512:qb * 512 + 512],
                                        AF.Identity, bias=bk_sb[:, 2 * g + pc:2 * g + pc + 1])
                m_kevac[tcol] = st("scalar", kev,
                                   waits=[("pe", mm_last), ("ds", m_bias),
                                          ("dve", m_kcent[0] if pc == 1 else m_kcent_all)],
                                   inc="act")
                quarter_free[qb] = m_kevac[tcol]
            m_red = st("vector", lambda e, M: e.tensor_reduce(kmean[:], Kf32[:], AX.X, OP.add),
                       waits=[("act", m_kevac[7]), ("dve", m_kcent_all)], inc="dve")
            m_scl = st("vector", lambda e, M: e.tensor_scalar(kmean[:], kmean[:], 1.0 / S, None, OP.mult),
                       waits=[("dve", m_red)], inc="dve")
            m_kcent[pc] = st("vector", lambda e, M, pc=pc: e.tensor_scalar(
                KTq[pc][:], Kf32[:], kmean[:, 0:1], None, OP.subtract),
                waits=[("dve", m_scl), ("pe", m_av_done)], inc="dve")
        m_kcent_all = m_kcent[1]

        m_qevac = [[0] * NQT, [0] * NQT]
        for pc in range(2):
            for qt in range(NQT):
                qb = qt % 4
                w0 = [("act", quarter_free[qb]), ("dg", m_wqs)]
                mm_last = None
                for ci in range(8):
                    def mmq(e, M, ci=ci, pc=pc, qt=qt, qb=qb):
                        return e.matmul(ps_big[:, qb * 512:qb * 512 + 512],
                                        WQq[:, ci * 256 + pc * 128: ci * 256 + pc * 128 + 128],
                                        XT[ci][:, qt * 512:(qt + 1) * 512],
                                        start=(ci == 0), stop=(ci == 7))
                    mm_last = st("tensor", mmq, waits=(w0 if ci == 0 else ()), inc=("pe" if ci == 7 else None))
                def qev(e, M, pc=pc, qt=qt, qb=qb, g=g):
                    return e.activation(QTq[pc][:, qt * 512:(qt + 1) * 512], ps_big[:, qb * 512:qb * 512 + 512],
                                        AF.Identity, bias=bq_sb[:, 2 * g + pc:2 * g + pc + 1])
                m_qevac[pc][qt] = st("scalar", qev, waits=[("pe", mm_last)], inc="act")
                quarter_free[qb] = m_qevac[pc][qt]

        for tt in range(32):
            pjv = ps_b5 if tt % 2 == 0 else ps_bc
            w0 = [("dg", m_wvs), ("dve", m_ones), ("dg", m_bv)]
            if tt >= 2:
                w0.append(("dve", m_vq[tt - 2]))
            else:
                w0 += m_b5_free
            mm_last = None
            for ci in range(8):
                def mmv(e, M, ci=ci, tt=tt, pjv=pjv):
                    return e.matmul(pjv[:, 0:256],
                                    XT[ci][:, tt * 128:(tt + 1) * 128],
                                    WVq[:, ci * 256:(ci + 1) * 256],
                                    start=(ci == 0), stop=False)
                st("tensor", mmv, waits=(w0 if ci == 0 else ()))
            m_vmm = st("tensor", lambda e, M, pjv=pjv, g=g: e.matmul(
                pjv[:, 0:256], ones_row[:], bvrow[0:1, g * 256:(g + 1) * 256],
                start=False, stop=True), inc="pe")
            m_vq[tt] = st("vector", lambda e, M, tt=tt, pjv=pjv: e.tensor_copy(
                Vq[tt][:], pjv[:, 0:256]), waits=[("pe", m_vmm)], inc="dve")
        m_wslice_read = sc.n["pe"]
        m_b5_free = [("dve", m_vq[30]), ("dve", m_vq[31])]

        for pc in range(2):
            ch = 2 * g + pc
            for qt in range(NQT):
                m_avmm = dict(m_avmm_prev)
                m_e = {c - 32: v for c, v in m_exp_prev.items()}

                def emit_scores(c):
                    i2 = c % 2
                    r4 = c % 4
                    w0 = [("dve", m_kcent[pc]), ("act", m_qevac[pc][qt]),
                          ("act", quarter_free[2 * i2]), ("act", quarter_free[2 * i2 + 1])]
                    def mmse(e, M, c=c, i2=i2, pc=pc, qt=qt):
                        return e.matmul(ps_big[:, i2 * 1024: i2 * 1024 + 512],
                                        KTq[pc][0:64, c * 128:(c + 1) * 128],
                                        QTq[pc][0:64, qt * 512:(qt + 1) * 512],
                                        start=True, stop=True, tile_position=(0, 0))
                    st("tensor", mmse, waits=w0)
                    def mmso(e, M, c=c, i2=i2, pc=pc, qt=qt):
                        return e.matmul(ps_big[:, i2 * 1024 + 512: i2 * 1024 + 1024],
                                        KTq[pc][64:128, c * 128:(c + 1) * 128],
                                        QTq[pc][64:128, qt * 512:(qt + 1) * 512],
                                        start=True, stop=True, tile_position=(64, 0))
                    m_smm = st("tensor", mmso, inc="pe")
                    m_e[c] = st("scalar", lambda e, M, c=c, i2=i2, r4=r4: e.activation(
                        SCBr[r4][:], ps_big[:, i2 * 1024:(i2 + 1) * 1024], AF.Exp, scale=0.125),
                        waits=[("pe", m_smm), ("pe", m_avmm.get(c - 4, 0))], inc="act")
                    quarter_free[2 * i2] = m_e[c]
                    quarter_free[2 * i2 + 1] = m_e[c]

                def emit_av(c):
                    r4 = c % 4
                    strt, stop = (c == 0), (c == 31)
                    wav = [("act", m_e[c])]
                    if strt:
                        wav += [("dve", m_norm_prev), ("dve", m_vq[31])]
                    def av_e(e, M, c=c, r4=r4, pc=pc, strt=strt, stop=stop):
                        return e.matmul(ps_b5[0:64, :], Vq[c][:, (2 * pc) * 64:(2 * pc) * 64 + 64],
                                        SCBr[r4][:, 0:512], start=strt, stop=stop, tile_position=(0, 0))
                    st("tensor", av_e, waits=wav)
                    def den_e(e, M, c=c, r4=r4, strt=strt, stop=stop):
                        return e.matmul(ps_b5[64:65, :], ones_col[:], SCBr[r4][:, 0:512],
                                        start=strt, stop=stop, tile_position=(0, 64))
                    st("tensor", den_e)
                    def av_o(e, M, c=c, r4=r4, pc=pc, strt=strt, stop=stop):
                        return e.matmul(ps_bc[64:128, :], Vq[c][:, (2 * pc + 1) * 64:(2 * pc + 1) * 64 + 64],
                                        SCBr[r4][:, 512:1024], start=strt, stop=stop, tile_position=(0, 64))
                    st("tensor", av_o)
                    def den_o(e, M, c=c, r4=r4, strt=strt, stop=stop):
                        return e.matmul(ps_bc[32:33, :], ones_col[:], SCBr[r4][:, 512:1024],
                                        start=strt, stop=stop, tile_position=(0, 32))
                    m_avmm[c] = st("tensor", den_o, inc="pe")

                for c in range(32):
                    emit_scores(c)
                    if c >= 1:
                        emit_av(c - 1)
                emit_av(31)
                m_avmm_prev = {c - 32: m_avmm[c] for c in range(28, 32)}
                m_exp_prev = {c: m_e[c] for c in range(30, 32)}
                m_av = m_avmm[31]
                m_r1 = st("vector", lambda e, M: e.reciprocal(rv[64:65, :], ps_b5[64:65, :]),
                          waits=[("pe", m_av), ("pe", m_bc_done)], inc="dve")
                m_r1b = st("vector", lambda e, M: e.reciprocal(rv[32:33, :], ps_bc[32:33, :]),
                           waits=[("dve", m_r1)], inc="dve")
                m_r2 = st("vector", lambda e, M: e.tensor_copy(rvh[64:65, :], rv[64:65, :]),
                          waits=[("dve", m_r1b)], inc="dve")
                m_r2b = st("vector", lambda e, M: e.tensor_copy(rvh[32:33, :], rv[32:33, :]),
                           waits=[("dve", m_r2)], inc="dve")
                m_r3 = st("vector", lambda e, M: e.tensor_tensor(rvd[64:65, :], rv[64:65, :], rvh[64:65, :], OP.subtract),
                          waits=[("dve", m_r2b)], inc="dve")
                m_r3b = st("vector", lambda e, M: e.tensor_tensor(rvd[32:33, :], rv[32:33, :], rvh[32:33, :], OP.subtract),
                           waits=[("dve", m_r3)], inc="dve")
                m_r4a = st("vector", lambda e, M: e.tensor_copy(rvl[64:65, :], rvd[64:65, :]),
                           waits=[("dve", m_r3b)], inc="dve")
                m_r4_ = st("vector", lambda e, M: e.tensor_copy(rvl[32:33, :], rvd[32:33, :]),
                           waits=[("dve", m_r4a)], inc="dve")
                m_ae = st("vector", lambda e, M: e.tensor_copy(av_sb[0:64, :], ps_b5[0:64, :]),
                          waits=[("pe", m_av), ("dve", m_mult)], inc="dve")
                m_ao = st("vector", lambda e, M: e.tensor_copy(av_sb[64:128, :], ps_bc[64:128, :]),
                          waits=[("dve", m_ae)], inc="dve")
                st("tensor", lambda e, M: e.matmul(ps_tok[0:64, 0:512], ones_bc[64:65, 0:64], rvh[64:65, :],
                                                   start=True, stop=False, tile_position=(64, 0)),
                   waits=[("dve", m_r4_), ("dve", m_mult)])
                st("tensor", lambda e, M: e.matmul(ps_tok[0:64, 0:512], ones_bc[64:65, 0:64], rvl[64:65, :],
                                                   start=False, stop=True, tile_position=(64, 0)))
                st("tensor", lambda e, M: e.matmul(ps_tok[64:128, 512:1024], ones_bc[32:33, 0:64], rvh[32:33, :],
                                                   start=True, stop=False, tile_position=(32, 64)))
                m_bc_done = st("tensor", lambda e, M: e.matmul(ps_tok[64:128, 512:1024], ones_bc[32:33, 0:64], rvl[32:33, :],
                                                               start=False, stop=True, tile_position=(32, 64)), inc="pe")
                m_me = st("vector", lambda e, M, ch=ch, qt=qt: e.tensor_tensor(
                    AOT[ch][0:64, qt * 512:(qt + 1) * 512], av_sb[0:64, :], ps_tok[0:64, 0:512], OP.mult),
                    waits=[("pe", m_bc_done), ("dve", m_ao)], inc="dve")
                m_mult = st("vector", lambda e, M, ch=ch, qt=qt: e.tensor_tensor(
                    AOT[ch][64:128, qt * 512:(qt + 1) * 512], av_sb[64:128, :], ps_tok[64:128, 512:1024], OP.mult),
                    waits=[("dve", m_me)], inc="dve")
                m_norm_prev = m_mult
                m_av_done = m_av

    p2_pe_done = sc.n["pe"]
    p2_mult_done = m_mult

    if not do_p3:
        do_p4 = False
    m_wo = 0
    for c in (range(8) if do_p3 else range(0)):
        m_wo = st("gpsimd", lambda e, M, c=c: e.dma_start(
            out=Wo_sb[c].ap(), in_=wo.rearrange("(c p) n -> p c n", p=128)[:, c, :]),
            waits=[("pe", p2_pe_done)], inc="dg")
    m_oev = [0] * 8
    m_tr_prev = 0
    m_xadd = [0, 0]
    m_sq_prev = 0
    m_ln_prev = 0
    m_var0_prev = 0
    m_x1store = [0, 0]
    m_xres_ld = [0, 0]
    m_ps_tok_free = 0
    for qt in (range(4) if do_p3 else range(0)):
        for outc in range(8):
            ps_o = ps_b5 if outc % 2 == 0 else ps_bc
            w0 = [("dg", m_wo), ("dve", p2_mult_done), ("act", m_oev[outc - 2] if outc >= 2 else 0),
                  ("pe", m_tr_prev)]
            if qt == 0 and outc < 2:
                w0 += [("dve", p2_mult_done)]
            mm_last = None
            for ci in range(8):
                def mmo(e, M, ci=ci, outc=outc, qt=qt, ps_o=ps_o):
                    return e.matmul(ps_o[:], Wo_sb[ci][:, outc * 128:(outc + 1) * 128],
                                    AOT[ci][:, qt * 512:(qt + 1) * 512],
                                    start=(ci == 0), stop=(ci == 7))
                mm_last = st("tensor", mmo, waits=(w0 if ci == 0 else ()), inc=("pe" if ci == 7 else None))
            m_oev[outc] = st("scalar", lambda e, M, outc=outc, ps_o=ps_o: e.activation(
                o_sb[outc][:], ps_o[:], AF.Identity, bias=bo_sb[:, outc:outc + 1]),
                waits=[("pe", mm_last), ("pe", m_tr_prev)], inc="act")
        for qs in range(4):
            i2 = qs % 2
            tok0 = qt * 512 + qs * 128
            m_xres_ld[i2] = st("sync", lambda e, M, i2=i2, tok0=tok0: e.dma_start(
                out=xres[i2][:], in_=x_in[tok0:tok0 + 128, :]),
                waits=[("dve", m_xadd[i2]), ("pe", p2_pe_done)], inc="ds")
            m_tr = None
            for outc in range(8):
                def trp(e, M, outc=outc, qs=qs):
                    return e.transpose(ps_tok[:, outc * 128:(outc + 1) * 128],
                                      o_sb[outc][:, qs * 128:(qs + 1) * 128], ident[:])
                m_tr = st("tensor", trp,
                          waits=([("act", m_oev[7]), ("dve", m_ps_tok_free), ("dve", p2_mult_done)] if outc == 0 else ()),
                          inc=("pe" if outc == 7 else None))
            m_tr_prev = m_tr
            m_add = st("vector", lambda e, M, i2=i2: e.tensor_tensor(
                xpre[i2][:], ps_tok[:], xres[i2][:], OP.add),
                waits=[("pe", m_tr), ("ds", m_xres_ld[i2]), ("act", m_sq_prev), ("dve", m_ln_prev)], inc="dve")
            m_ps_tok_free = m_add
            m_xadd[i2] = m_add
            m_red = st("vector", lambda e, M, i2=i2: e.tensor_reduce(
                ln_sum[:], xpre[i2][:], AX.X, OP.add), waits=[("dve", m_add), ("dve", m_ln_prev)], inc="dve")
            m_sq = st("scalar", lambda e, M, i2=i2: e.activation(
                sq1[:], xpre[i2][:], AF.Square, accum_out=ln_ssq[:, 0:1]),
                waits=[("dve", m_add), ("dve", m_var0_prev)], inc="act")
            m_sq_prev = m_sq
            m_mean = st("vector", lambda e, M: e.tensor_scalar(
                ln_mean[:], ln_sum[:], 1.0 / D, None, OP.mult), waits=[("dve", m_red)], inc="dve")
            m_m2 = st("vector", lambda e, M: e.tensor_tensor(
                ln_m2[:], ln_mean[:], ln_mean[:], OP.mult), waits=[("dve", m_mean)], inc="dve")
            m_var0 = st("vector", lambda e, M: e.tensor_scalar(
                ln_var[:], ln_ssq[:], 1.0 / D, EPS, OP.mult, OP.add),
                waits=[("act", m_sq), ("dve", m_m2)], inc="dve")
            m_var0_prev = m_var0
            m_var = st("vector", lambda e, M: e.tensor_tensor(
                ln_var[:], ln_var[:], ln_m2[:], OP.subtract), waits=[("dve", m_var0)], inc="dve")
            m_sd = st("scalar", lambda e, M: e.activation(
                ln_sd[:], ln_var[:], AF.Sqrt), waits=[("dve", m_var)], inc="act")
            m_rstd = st("vector", lambda e, M: e.reciprocal(
                ln_rstd[:], ln_sd[:]), waits=[("act", m_sd)], inc="dve")
            m_app = st("vector", lambda e, M, i2=i2: e.tensor_scalar(
                x1t[i2][:], xpre[i2][:], ln_mean[:, 0:1], ln_rstd[:, 0:1], OP.subtract, OP.mult),
                waits=[("dve", m_rstd), ("dg", m_x1store[i2])], inc="dve")
            m_cast = st("vector", lambda e, M, i2=i2: e.tensor_copy(
                x1tb[i2][:], x1t[i2][:]), waits=[("dve", m_app)], inc="dve")
            m_ln_prev = m_app
            st("gpsimd", lambda e, M, i2=i2, tok0=tok0: e.dma_start(
                out=s_x1f[tok0:tok0 + 128, :], in_=x1t[i2][:]), waits=[("dve", m_app)], inc="dg")
            m_x1store[i2] = st("gpsimd", lambda e, M, i2=i2, tok0=tok0: e.dma_start(
                out=s_x1b[tok0:tok0 + 128, :], in_=x1tb[i2][:]), waits=[("dve", m_cast)], inc="dg")

    p3_pe = sc.n["pe"]
    p3_act = sc.n["act"]
    p3_dve = sc.n["dve"]
    p3_dg = sc.n["dg"]
    R8 = range(8) if do_p4 else range(0)
    R24 = range(24) if do_p4 else range(0)

    p3w = [("pe", p3_pe), ("act", p3_act), ("dve", p3_dve), ("dg", p3_dg)]
    m_x1f = 0
    for c in R8:
        m_x1f = st("sync", lambda e, M, c=c: e.dma_start_transpose(
            X1F[c][:], s_x1b[:, c * 128:(c + 1) * 128]), waits=p3w, inc="ds")
    m_wgt = 0
    for c in R8:
        m_wgt = st("sync", lambda e, M, c=c: e.dma_start_transpose(
            WGT[c][:], s_wg[:, c * 128:(c + 1) * 128]), waits=p3w + [("dg", m_wgc)], inc="ds")
    m_wut = 0
    for c in R8:
        m_wut = st("sync", lambda e, M, c=c: e.dma_start_transpose(
            WUT[c][:], s_wu[:, c * 128:(c + 1) * 128]), waits=p3w + [("dg", m_wuc)], inc="ds")
    m_wdt = 0
    for j in R24:
        m_wdt = st("sync", lambda e, M, j=j: e.dma_start_transpose(
            WDT[j][:], s_wd[:, j * 128:(j + 1) * 128]), waits=p3w + [("dg", m_wdc)], inc="ds")

    m_si = [0] * 24
    m_h = [0] * 24
    m_dnj = [0] * 24
    m_dnev = [0] * 8
    m_x2add = [0, 0]
    m_x2store = [0, 0]
    m_tr2_prev = 0
    m_x1re_read = 0
    m_dnev_last = 0
    for qt in (range(8) if do_p4 else range(0)):
        for j in range(24):
            pj = ps_b5 if j % 2 == 0 else ps_bc
            w0 = [("ds", m_wgt), ("ds", m_wut), ("ds", m_x1f)]
            if j >= 2:
                w0 += [("act", m_si[j - 2]), ("dve", m_h[j - 2])]
            elif qt > 0:
                w0 += [("act", m_si[22 + j]), ("dve", m_h[22 + j])]
            else:
                w0 += [("dve", p2_mult_done)]
            mm_last = None
            for ci in range(8):
                def mmg(e, M, ci=ci, j=j, qt=qt, pj=pj):
                    return e.matmul(pj[:, 0:256], WGT[ci][:, j * 128:(j + 1) * 128],
                                    X1F[ci][:, qt * 256:(qt + 1) * 256],
                                    start=(ci == 0), stop=(ci == 7))
                st("tensor", mmg, waits=(w0 if ci == 0 else ()))
            for ci in range(8):
                def mmu(e, M, ci=ci, j=j, qt=qt, pj=pj):
                    return e.matmul(pj[:, 256:512], WUT[ci][:, j * 128:(j + 1) * 128],
                                    X1F[ci][:, qt * 256:(qt + 1) * 256],
                                    start=(ci == 0), stop=(ci == 7))
                mm_last = st("tensor", mmu, inc=("pe" if ci == 7 else None))
            m_si[j] = st("scalar", lambda e, M, pj=pj: e.activation(
                s_sb[:], pj[:, 0:256], AF.Silu, scale=DEQ),
                waits=[("pe", mm_last)] + ([("dve", m_h[j - 1])] if j >= 1 else
                                           ([("dve", m_h[23])] if qt > 0 else [])), inc="act")
            m_h[j] = st("vector", lambda e, M, j=j, pj=pj: e.tensor_tensor(
                h_sb[j % 2][:], s_sb[:], pj[:, 256:512], OP.mult),
                waits=[("act", m_si[j])] + ([("pe", m_dnj[j - 2])] if j >= 2 else
                                            ([("pe", m_dnj[22 + j])] if qt > 0 else [])), inc="dve")
            w_dn = [("dve", m_h[j]), ("ds", m_wdt)]
            if j == 0:
                w_dn += [("dve", m_x2add[0]), ("dve", m_x2add[1]), ("act", m_dnev_last)]
                for bank in range(4):
                    def mmz(e, M, bank=bank):
                        return e.matmul(ps_big[:, bank * 512:(bank + 1) * 512],
                                        zeros_row[:], bvrow[0:1, 0:512], start=True, stop=False)
                    st("tensor", mmz, waits=(w_dn if bank == 0 else ()))
                w_dn = []
            mm_last = None
            for outc in range(8):
                def mmdn(e, M, outc=outc, j=j):
                    return e.matmul(ps_big[:, outc * 256:(outc + 1) * 256],
                                    WDT[j][:, outc * 128:(outc + 1) * 128], h_sb[j % 2][:],
                                    start=False, stop=(j == 23))
                mm_last = st("tensor", mmdn, waits=(w_dn if outc == 0 else ()), inc=("pe" if outc == 7 else None))
            m_dnj[j] = mm_last
        for outc in range(8):
            m_dnev[outc] = st("scalar", lambda e, M, outc=outc: e.activation(
                dn_sb[:, outc * 256:(outc + 1) * 256], ps_big[:, outc * 256:(outc + 1) * 256],
                AF.Identity, scale=DEQ2),
                waits=[("pe", m_dnj[23]), ("pe", m_tr2_prev)], inc="act")
        m_dnev_last = m_dnev[7]
        for qs in range(2):
            i2 = qs % 2
            tok0 = qt * 256 + qs * 128
            m_ld = st("sync", lambda e, M, tok0=tok0: e.dma_start(
                out=x1re[:], in_=s_x1f[tok0:tok0 + 128, :]),
                waits=[("dve", m_x1re_read), ("dg", p3_dg)], inc="ds")
            m_tr = None
            for outc in range(8):
                def trp2(e, M, outc=outc, qs=qs):
                    return e.transpose(ps_big[:, qs * 1024 + outc * 128: qs * 1024 + (outc + 1) * 128],
                                      dn_sb[:, outc * 256 + qs * 128: outc * 256 + (qs + 1) * 128], ident[:])
                m_tr = st("tensor", trp2,
                          waits=([("act", m_dnev[7]), ("dve", m_x2add[i2])] if outc == 0 else ()),
                          inc=("pe" if outc == 7 else None))
            m_tr2_prev = m_tr
            m_add = st("vector", lambda e, M, i2=i2, qs=qs: e.tensor_tensor(
                x2pre[i2][:], ps_big[:, qs * 1024:(qs + 1) * 1024], x1re[:], OP.add),
                waits=[("pe", m_tr), ("ds", m_ld), ("dg", m_x2store[i2]),
                       ("act", m_sq_prev), ("dve", m_ln_prev)], inc="dve")
            m_x1re_read = m_add
            m_x2add[i2] = m_add
            m_red = st("vector", lambda e, M, i2=i2: e.tensor_reduce(
                ln_sum[:], x2pre[i2][:], AX.X, OP.add), waits=[("dve", m_add), ("dve", m_ln_prev)], inc="dve")
            m_sq = st("scalar", lambda e, M, i2=i2: e.activation(
                sq2[:], x2pre[i2][:], AF.Square, accum_out=ln_ssq[:, 0:1]),
                waits=[("dve", m_add), ("dve", m_var0_prev)], inc="act")
            m_sq_prev = m_sq
            m_mean = st("vector", lambda e, M: e.tensor_scalar(
                ln_mean[:], ln_sum[:], 1.0 / D, None, OP.mult), waits=[("dve", m_red)], inc="dve")
            m_m2 = st("vector", lambda e, M: e.tensor_tensor(
                ln_m2[:], ln_mean[:], ln_mean[:], OP.mult), waits=[("dve", m_mean)], inc="dve")
            m_var0 = st("vector", lambda e, M: e.tensor_scalar(
                ln_var[:], ln_ssq[:], 1.0 / D, EPS, OP.mult, OP.add),
                waits=[("act", m_sq), ("dve", m_m2)], inc="dve")
            m_var0_prev = m_var0
            m_var = st("vector", lambda e, M: e.tensor_tensor(
                ln_var[:], ln_var[:], ln_m2[:], OP.subtract), waits=[("dve", m_var0)], inc="dve")
            m_sd = st("scalar", lambda e, M: e.activation(
                ln_sd[:], ln_var[:], AF.Sqrt), waits=[("dve", m_var)], inc="act")
            m_rstd = st("vector", lambda e, M: e.reciprocal(
                ln_rstd[:], ln_sd[:]), waits=[("act", m_sd)], inc="dve")
            m_app = st("vector", lambda e, M, i2=i2: e.tensor_scalar(
                x2pre[i2][:], x2pre[i2][:], ln_mean[:, 0:1], ln_rstd[:, 0:1], OP.subtract, OP.mult),
                waits=[("dve", m_rstd)], inc="dve")
            m_ln_prev = m_app
            m_x2store[i2] = st("gpsimd", lambda e, M, i2=i2, tok0=tok0: e.dma_start(
                out=out[tok0:tok0 + 128, :], in_=x2pre[i2][:]), waits=[("dve", m_app)], inc="dg")

    final_dg = sc.n["dg"]
    sc.q["gpsimd"].append(lambda e, SM: e.wait_ge(SM["dg"], final_dg))
    sc.meta["gpsimd"].append(((("dg", final_dg),), None, "final"))
    sc.check()

    from contextlib import ExitStack
    es = ExitStack()
    SM = {s: es.enter_context(nc.semaphore(s)) for s in ("ds", "dg", "pe", "act", "dve")}
    block = es.enter_context(nc.Block())

    @block.sync
    def _(e):
        for f in sc.q["sync"]:
            f(e, SM)

    @block.gpsimd
    def _(e):
        for f in sc.q["gpsimd"]:
            f(e, SM)

    @block.tensor
    def _(e):
        for f in sc.q["tensor"]:
            f(e, SM)

    @block.scalar
    def _(e):
        for f in sc.q["scalar"]:
            f(e, SM)

    @block.vector
    def _(e):
        for f in sc.q["vector"]:
            f(e, SM)

    es.close()
    return nc


_NC_CACHE = None


def kernel(**inputs):
    global _NC_CACHE
    X = np.asarray(inputs["L0_attn_X"], dtype=np.float32)
    args = {
        "wq": np.ascontiguousarray(inputs["L0_attn_Wq"], dtype=np.float32),
        "wk": np.ascontiguousarray(inputs["L0_attn_Wk"], dtype=np.float32),
        "wv": np.ascontiguousarray(inputs["L0_attn_Wv"], dtype=np.float32),
        "wo": np.ascontiguousarray(inputs["L0_attn_Wo"], dtype=np.float32),
        "bq": np.ascontiguousarray(inputs["L0_attn_bq"], dtype=np.float32),
        "bk": np.ascontiguousarray(inputs["L0_attn_bk"], dtype=np.float32),
        "bv": np.ascontiguousarray(inputs["L0_attn_bv"], dtype=np.float32),
        "bo": np.ascontiguousarray(inputs["L0_attn_bo"], dtype=np.float32),
        "wg": np.ascontiguousarray(inputs["L0_ffn_w_gate"], dtype=np.int32),
        "wu": np.ascontiguousarray(inputs["L0_ffn_w_up"], dtype=np.int32),
        "wd": np.ascontiguousarray(inputs["L0_ffn_w_down"], dtype=np.int32),
    }
    if _NC_CACHE is None:
        _NC_CACHE = build_nc()
    nc = _NC_CACHE
    in_maps = []
    for c in range(8):
        b, qh = c // 2, c % 2
        xb = X[b]
        if qh:
            xb = np.roll(xb, -T, axis=0)
        m = {"x_in": np.ascontiguousarray(xb)}
        m.update(args)
        in_maps.append(m)
    res = run_bass_kernel_spmd(nc, in_maps, core_ids=list(range(8)))
    outp = np.empty((B, S, D), dtype=np.float32)
    for c in range(8):
        b, qh = c // 2, c % 2
        outp[b, qh * T:(qh + 1) * T] = res.results[c]["out"]
    return outp


if __name__ == "__main__":
    rng = np.random.default_rng(0)
    ins = {
        "L0_attn_X": rng.standard_normal((B, S, D), dtype=np.float32),
        "L0_attn_Wq": rng.standard_normal((D, D), dtype=np.float32) / 32,
        "L0_attn_bq": rng.standard_normal(D, dtype=np.float32) * 0.1,
        "L0_attn_Wk": rng.standard_normal((D, D), dtype=np.float32) / 32,
        "L0_attn_bk": rng.standard_normal(D, dtype=np.float32) * 0.1,
        "L0_attn_Wv": rng.standard_normal((D, D), dtype=np.float32) / 32,
        "L0_attn_bv": rng.standard_normal(D, dtype=np.float32) * 0.1,
        "L0_attn_Wo": rng.standard_normal((D, D), dtype=np.float32) / 32,
        "L0_attn_bo": rng.standard_normal(D, dtype=np.float32) * 0.1,
        "L0_ffn_w_gate": rng.integers(-127, 128, (HID, D)).astype(np.int32),
        "L0_ffn_w_up": rng.integers(-127, 128, (HID, D)).astype(np.int32),
        "L0_ffn_w_down": rng.integers(-127, 128, (D, HID)).astype(np.int32),
    }
    y = kernel(**ins)
    print("out", y.shape, y.dtype, np.abs(y).max())


# version 12
# speedup vs baseline: 1.6663x; 1.0042x over previous
import numpy as np
import concourse.bass as bass
import concourse.mybir as mybir
from concourse.bass_utils import run_bass_kernel_spmd

F32 = mybir.dt.float32
BF16 = mybir.dt.bfloat16
I32 = mybir.dt.int32
AF = mybir.ActivationFunctionType
OP = mybir.AluOpType
AX = mybir.AxisListType

B, S, D, H, DK, HID = 4, 4096, 1024, 16, 64, 3072
T = 2048
NQT = 4
EPS = 1e-5
DEQ = 1.0 / 127.0
DEQ2 = DEQ * DEQ

ENGS = ("sync", "gpsimd", "tensor", "scalar", "vector")
DMA_SEMS = ("ds", "dg")


class Sched:

    def __init__(self, limit=None):
        self.limit = limit
        self.total = 0
        self.q = {e: [] for e in ENGS}
        self.n = {s: 0 for s in ("ds", "dg", "pe", "act", "dve")}
        self.meta = {e: [] for e in ENGS}

    def check(self):
        ptr = {e: 0 for e in ENGS}
        cnt = {s: 0 for s in self.n}
        progress = True
        while progress:
            progress = False
            for e in ENGS:
                while ptr[e] < len(self.meta[e]):
                    waits, inc, tag = self.meta[e][ptr[e]]
                    if any(cnt[s] < t for (s, t) in waits):
                        break
                    if inc:
                        cnt[inc] += 16 if inc in DMA_SEMS else 1
                    ptr[e] += 1
                    progress = True
        stuck = {e: ptr[e] for e in ENGS if ptr[e] < len(self.meta[e])}
        if stuck:
            detail = {}
            for e in stuck:
                waits, inc, tag = self.meta[e][ptr[e]]
                detail[e] = (ptr[e], tag, [(s, t, cnt[s]) for (s, t) in waits if cnt[s] < t])
            raise RuntimeError(f"DEADLOCK: stuck steps {detail}")

    def step(self, eng, emit, waits=(), inc=None, tag=""):
        if self.limit is not None and self.total >= self.limit:
            return self.n.get(inc)
        self.total += 1
        waits = tuple((s, t) for (s, t) in waits if t)
        for s, t in waits:
            assert t <= self.n[s], f"wait target {s}={t} not yet emitted ({self.n[s]})"
        tgt = None
        if inc is not None:
            self.n[inc] += 16 if inc in DMA_SEMS else 1
            tgt = self.n[inc]

        def run(e, SM, waits=waits, emit=emit, inc=inc):
            for s, t in waits:
                e.wait_ge(SM[s], t)
            r = emit(e, SM)
            if inc is not None:
                assert r is not None, "inc requires a real instruction"
                r.then_inc(SM[inc], 16 if inc in DMA_SEMS else 1)
        self.q[eng].append(run)
        self.meta[eng].append((waits, inc, tag))
        return tgt


def build_nc(n_quads=4, n_heads=4, do_p3=True, do_p4=True, limit=None):
    nc = bass.Bass("TRN2", target_bir_lowering=False)

    x_in = nc.dram_tensor("x_in", [S, D], F32, kind="ExternalInput").ap()
    wq = nc.dram_tensor("wq", [D, D], F32, kind="ExternalInput").ap()
    wk = nc.dram_tensor("wk", [D, D], F32, kind="ExternalInput").ap()
    wv = nc.dram_tensor("wv", [D, D], F32, kind="ExternalInput").ap()
    wo = nc.dram_tensor("wo", [D, D], F32, kind="ExternalInput").ap()
    bq = nc.dram_tensor("bq", [D], F32, kind="ExternalInput").ap()
    bk = nc.dram_tensor("bk", [D], F32, kind="ExternalInput").ap()
    bv = nc.dram_tensor("bv", [D], F32, kind="ExternalInput").ap()
    bo = nc.dram_tensor("bo", [D], F32, kind="ExternalInput").ap()
    wg = nc.dram_tensor("wg", [HID, D], I32, kind="ExternalInput").ap()
    wu = nc.dram_tensor("wu", [HID, D], I32, kind="ExternalInput").ap()
    wd = nc.dram_tensor("wd", [D, HID], I32, kind="ExternalInput").ap()
    out = nc.dram_tensor("out", [T, D], F32, kind="ExternalOutput").ap()

    s_xbf = nc.dram_tensor("s_xbf", [8, S, 128], BF16, kind="Internal").ap()
    s_wg = nc.dram_tensor("s_wg", [8, HID, 128], BF16, kind="Internal").ap()
    s_wu = nc.dram_tensor("s_wu", [8, HID, 128], BF16, kind="Internal").ap()
    s_wd = nc.dram_tensor("s_wd", [24, D, 128], BF16, kind="Internal").ap()
    s_x1b = nc.dram_tensor("s_x1b", [8, T, 128], BF16, kind="Internal").ap()
    s_x1f = nc.dram_tensor("s_x1f", [T, D], F32, kind="Internal").ap()

    ident_dram = nc.inline_tensor(np.eye(128, dtype=np.float32), name="ident128")

    SBASE = 16512
    A = lambda name, shape, dt, off: nc.alloc_sbuf_tensor_at(name, shape, dt, offset=SBASE + off)
    XT = [A(f"XT{c}", [128, S], BF16, 8192 * c) for c in range(8)]
    WQq = A("WQq", [128, 2048], BF16, 65536)
    WKq = A("WKq", [128, 2048], BF16, 69632)
    WVq = A("WVq", [128, 2048], BF16, 73728)
    KTq = [A(f"KTq{p}", [128, S], BF16, 77824 + 8192 * p) for p in range(2)]
    QTq = [A(f"QTq{p}", [128, T], BF16, 94208 + 4096 * p) for p in range(2)]
    Vq = [A(f"Vq{t}", [128, 256], BF16, 102400 + 512 * t) for t in range(32)]
    SCBr = [A(f"SCBr{r}", [128, 1024], BF16, 118784 + 2048 * r) for r in range(4)]
    Kf32 = A("Kf32", [128, S], F32, 126976)
    AOT = [A(f"AOT{c}", [128, T], BF16, 151552 + 4096 * c) for c in range(8)]
    av_sb = A("av_sb", [128, 512], F32, 184320)
    rv = A("rv", [65, 512], F32, 186368)
    rvh = A("rvh", [65, 512], BF16, 188416)
    rvl = A("rvl", [65, 512], BF16, 189440)
    rvd = A("rvd", [65, 512], F32, 190464)
    Wo_sb = [A(f"Wo{c}", [128, 1024], BF16, 2048 * c) for c in range(8)]
    o_sb = [A(f"osb{i}", [128, 512], F32, 16384 + 2048 * i) for i in range(8)]
    xres = [A(f"xres{i}", [128, 1024], F32, 32768 + 4096 * i) for i in range(2)]
    xpre = [A(f"xpre{i}", [128, 1024], F32, 40960 + 4096 * i) for i in range(2)]
    sq1 = A("sq1", [128, 1024], F32, 49152)
    x1t = [A(f"x1t{i}", [128, 1024], F32, 53248 + 4096 * i) for i in range(2)]
    x1tb = [A(f"x1tb{i}", [128, 1024], BF16, 61440 + 2048 * i) for i in range(2)]
    X1F = [A(f"X1F{c}", [128, T], BF16, 4096 * c) for c in range(8)]
    WGT = [A(f"WGT{c}", [128, HID], BF16, 32768 + 6144 * c) for c in range(8)]
    WUT = [A(f"WUT{c}", [128, HID], BF16, 81920 + 6144 * c) for c in range(8)]
    WDT = [A(f"WDT{j}", [128, 1024], BF16, 131072 + 2048 * j) for j in range(24)]
    s_sb = A("s_sb", [128, 256], BF16, 180224)
    h_sb = [A(f"hsb{i}", [128, 256], BF16, 180736 + 512 * i) for i in range(2)]
    dn_sb = A("dn_sb", [128, 2048], F32, 181760)
    x2pre = [A(f"x2pre{i}", [128, 1024], F32, 189952 + 4096 * i) for i in range(2)]
    sq2 = A("sq2", [128, 1024], F32, 198144)
    x1re = A("x1re", [128, 1024], F32, 202240)
    P = lambda name, shape, dt: nc.alloc_sbuf_tensor(name, shape, dt, side="right")
    ident = P("ident", [128, 128], F32)
    ones_col = P("ones_col", [128, 1], BF16)
    ones_bc = P("ones_bc", [65, 64], BF16)
    ones_row = P("ones_row", [1, 128], BF16)
    zeros_row = P("zeros_row", [1, 128], BF16)
    bvrow = P("bvrow", [1, 1024], BF16)
    bq_sb = P("bq_sb", [128, 8], F32)
    bk_sb = P("bk_sb", [128, 8], F32)
    bo_sb = P("bo_sb", [128, 8], F32)
    ln_sum = P("ln_sum", [128, 1], F32)
    ln_ssq = P("ln_ssq", [128, 1], F32)
    ln_mean = P("ln_mean", [128, 1], F32)
    ln_var = P("ln_var", [128, 1], F32)
    ln_m2 = P("ln_m2", [128, 1], F32)
    ln_sd = P("ln_sd", [128, 1], F32)
    ln_rstd = P("ln_rstd", [128, 1], F32)
    kmean = P("kmean", [128, 1], F32)

    ps_big = nc.alloc_psum_tensor("ps_big", [128, 2048], F32)
    ps_b5 = nc.alloc_psum_tensor("ps_b5", [128, 512], F32)
    ps_bc = nc.alloc_psum_tensor("ps_bc", [128, 512], F32)
    ps_tok = nc.alloc_psum_tensor("ps_tok", [128, 1024], F32)

    sc = Sched(limit)
    st = sc.step

    m_xc = []
    for rq in range(4):
        m = 0
        for c8 in range(8):
            m = st("gpsimd", lambda e, M, rq=rq, c8=c8: e.dma_start(
                out=s_xbf[c8, rq * 1024:(rq + 1) * 1024, :],
                in_=x_in[rq * 1024:(rq + 1) * 1024, c8 * 128:(c8 + 1) * 128]), inc="dg")
        m_xc.append(m)
    m_bv = st("gpsimd", lambda e, M: e.dma_start(out=bvrow[0:1, :], in_=bv[None, :]), inc="dg")
    st("sync", lambda e, M: e.dma_start(out=ident[:], in_=ident_dram.ap()), inc="ds")
    for c in range(8):
        st("sync", lambda e, M, c=c: e.dma_start(out=bq_sb[:, c], in_=bq[c * 128:(c + 1) * 128]), inc="ds")
        st("sync", lambda e, M, c=c: e.dma_start(out=bk_sb[:, c], in_=bk[c * 128:(c + 1) * 128]), inc="ds")
        st("sync", lambda e, M, c=c: e.dma_start(out=bo_sb[:, c], in_=bo[c * 128:(c + 1) * 128]), inc="ds")
    m_bias = sc.n["ds"]
    st("vector", lambda e, M: e.memset(ones_col[:], 1.0), inc="dve")
    st("vector", lambda e, M: e.memset(ones_bc[:], 1.0), inc="dve")
    st("vector", lambda e, M: e.memset(ones_row[:], 1.0), inc="dve")
    m_ones = st("vector", lambda e, M: e.memset(zeros_row[:], 0.0), inc="dve")
    m_xt = 0
    for half in range(2):
        for c in range(8):
            m_xt = st("sync", lambda e, M, c=c, half=half: e.dma_start_transpose(
                XT[c][:, half * 2048:(half + 1) * 2048],
                s_xbf[c, half * 2048:(half + 1) * 2048, :]),
                waits=[("dg", m_xc[2 * half + 1])], inc="ds")

    quarter_free = [0, 0, 0, 0]
    m_exp = [0] * 8
    m_b5_free = []
    m_bc_read = 0
    m_bc_done = 0
    m_avsb = 0
    m_mult = 0
    m_av_done = 0
    m_kcent_all = 0
    m_vq = [0] * 32
    m_wslice_read = 0
    m_rvchain = 0
    m_avmm_prev = {}
    m_exp_prev = {}
    m_norm_prev = 0

    m_wgc = m_wuc = m_wdc = 0
    for g in range(n_quads):
        if g == 1:
            for c8 in range(8):
                m_wgc = st("gpsimd", lambda e, M, c8=c8: e.dma_start(
                    out=s_wg[c8], in_=wg[:, c8 * 128:(c8 + 1) * 128]), inc="dg")
            for c8 in range(8):
                m_wuc = st("gpsimd", lambda e, M, c8=c8: e.dma_start(
                    out=s_wu[c8], in_=wu[:, c8 * 128:(c8 + 1) * 128]), inc="dg")
            for j24 in range(24):
                m_wdc = st("gpsimd", lambda e, M, j24=j24: e.dma_start(
                    out=s_wd[j24], in_=wd[:, j24 * 128:(j24 + 1) * 128]), inc="dg")
        cols = slice(g * 256, (g + 1) * 256)
        m_wqs = st("gpsimd", lambda e, M, v=wq.rearrange("(c p) n -> p c n", p=128)[:, :, cols]:
                   e.dma_start(out=WQq.ap().rearrange("p (c n) -> p c n", c=8), in_=v),
                   waits=[("pe", m_wslice_read)], inc="dg")
        m_wks = st("gpsimd", lambda e, M, v=wk.rearrange("(c p) n -> p c n", p=128)[:, :, cols]:
                   e.dma_start(out=WKq.ap().rearrange("p (c n) -> p c n", c=8), in_=v),
                   waits=[("pe", m_wslice_read)], inc="dg")
        m_wvs = st("gpsimd", lambda e, M, v=wv.rearrange("(c p) n -> p c n", p=128)[:, :, cols]:
                   e.dma_start(out=WVq.ap().rearrange("p (c n) -> p c n", c=8), in_=v),
                   waits=[("pe", m_wslice_read)], inc="dg")

        m_kcent = [0, 0]
        for pc in range(2):
            m_kevac = [0] * 8
            for tcol in range(8):
                qb = tcol % 4
                w0 = [("ds", m_xt), ("dg", m_wks), ("act", quarter_free[qb]),
                      ("pe", m_av_done), ("dve", m_mult)]
                mm_last = None
                for ci in range(8):
                    def mmk(e, M, ci=ci, pc=pc, tcol=tcol, qb=qb):
                        return e.matmul(ps_big[:, qb * 512:qb * 512 + 512],
                                        WKq[:, ci * 256 + pc * 128: ci * 256 + pc * 128 + 128],
                                        XT[ci][:, tcol * 512:(tcol + 1) * 512],
                                        start=(ci == 0), stop=(ci == 7))
                    mm_last = st("tensor", mmk, waits=(w0 if ci == 0 else ()), inc=("pe" if ci == 7 else None))
                def kev(e, M, pc=pc, tcol=tcol, qb=qb, g=g):
                    return e.activation(Kf32[:, tcol * 512:(tcol + 1) * 512], ps_big[:, qb * 512:qb * 512 + 512],
                                        AF.Identity, bias=bk_sb[:, 2 * g + pc:2 * g + pc + 1])
                m_kevac[tcol] = st("scalar", kev,
                                   waits=[("pe", mm_last), ("ds", m_bias),
                                          ("dve", m_kcent[0] if pc == 1 else m_kcent_all)],
                                   inc="act")
                quarter_free[qb] = m_kevac[tcol]
            m_red = st("vector", lambda e, M: e.tensor_reduce(kmean[:], Kf32[:], AX.X, OP.add),
                       waits=[("act", m_kevac[7]), ("dve", m_kcent_all)], inc="dve")
            m_scl = st("vector", lambda e, M: e.tensor_scalar(kmean[:], kmean[:], 1.0 / S, None, OP.mult),
                       waits=[("dve", m_red)], inc="dve")
            m_kcent[pc] = st("vector", lambda e, M, pc=pc: e.tensor_scalar(
                KTq[pc][:], Kf32[:], kmean[:, 0:1], None, OP.subtract),
                waits=[("dve", m_scl), ("pe", m_av_done)], inc="dve")
        m_kcent_all = m_kcent[1]

        m_qevac = [[0] * NQT, [0] * NQT]
        for pc in range(2):
            for qt in range(NQT):
                qb = qt % 4
                w0 = [("act", quarter_free[qb]), ("dg", m_wqs)]
                mm_last = None
                for ci in range(8):
                    def mmq(e, M, ci=ci, pc=pc, qt=qt, qb=qb):
                        return e.matmul(ps_big[:, qb * 512:qb * 512 + 512],
                                        WQq[:, ci * 256 + pc * 128: ci * 256 + pc * 128 + 128],
                                        XT[ci][:, qt * 512:(qt + 1) * 512],
                                        start=(ci == 0), stop=(ci == 7))
                    mm_last = st("tensor", mmq, waits=(w0 if ci == 0 else ()), inc=("pe" if ci == 7 else None))
                def qev(e, M, pc=pc, qt=qt, qb=qb, g=g):
                    return e.activation(QTq[pc][:, qt * 512:(qt + 1) * 512], ps_big[:, qb * 512:qb * 512 + 512],
                                        AF.Identity, bias=bq_sb[:, 2 * g + pc:2 * g + pc + 1])
                m_qevac[pc][qt] = st("scalar", qev, waits=[("pe", mm_last)], inc="act")
                quarter_free[qb] = m_qevac[pc][qt]

        for tt in range(32):
            pjv = ps_b5 if tt % 2 == 0 else ps_bc
            w0 = [("dg", m_wvs), ("dve", m_ones), ("dg", m_bv)]
            if tt >= 2:
                w0.append(("dve", m_vq[tt - 2]))
            else:
                w0 += m_b5_free
            mm_last = None
            for ci in range(8):
                def mmv(e, M, ci=ci, tt=tt, pjv=pjv):
                    return e.matmul(pjv[:, 0:256],
                                    XT[ci][:, tt * 128:(tt + 1) * 128],
                                    WVq[:, ci * 256:(ci + 1) * 256],
                                    start=(ci == 0), stop=False)
                st("tensor", mmv, waits=(w0 if ci == 0 else ()))
            m_vmm = st("tensor", lambda e, M, pjv=pjv, g=g: e.matmul(
                pjv[:, 0:256], ones_row[:], bvrow[0:1, g * 256:(g + 1) * 256],
                start=False, stop=True), inc="pe")
            m_vq[tt] = st("vector", lambda e, M, tt=tt, pjv=pjv: e.tensor_copy(
                Vq[tt][:], pjv[:, 0:256]), waits=[("pe", m_vmm)], inc="dve")
        m_wslice_read = sc.n["pe"]
        m_b5_free = [("dve", m_vq[30]), ("dve", m_vq[31])]

        for pc in range(2):
            ch = 2 * g + pc
            for qt in range(NQT):
                m_avmm = dict(m_avmm_prev)
                m_e = {c - 32: v for c, v in m_exp_prev.items()}

                def emit_scores(c):
                    i2 = c % 2
                    r4 = c % 4
                    w0 = [("dve", m_kcent[pc]), ("act", m_qevac[pc][qt]),
                          ("act", quarter_free[2 * i2]), ("act", quarter_free[2 * i2 + 1])]
                    def mmse(e, M, c=c, i2=i2, pc=pc, qt=qt):
                        return e.matmul(ps_big[:, i2 * 1024: i2 * 1024 + 512],
                                        KTq[pc][0:64, c * 128:(c + 1) * 128],
                                        QTq[pc][0:64, qt * 512:(qt + 1) * 512],
                                        start=True, stop=True, tile_position=(0, 0))
                    st("tensor", mmse, waits=w0)
                    def mmso(e, M, c=c, i2=i2, pc=pc, qt=qt):
                        return e.matmul(ps_big[:, i2 * 1024 + 512: i2 * 1024 + 1024],
                                        KTq[pc][64:128, c * 128:(c + 1) * 128],
                                        QTq[pc][64:128, qt * 512:(qt + 1) * 512],
                                        start=True, stop=True, tile_position=(64, 0))
                    m_smm = st("tensor", mmso, inc="pe")
                    m_e[c] = st("scalar", lambda e, M, c=c, i2=i2, r4=r4: e.activation(
                        SCBr[r4][:], ps_big[:, i2 * 1024:(i2 + 1) * 1024], AF.Exp, scale=0.125),
                        waits=[("pe", m_smm), ("pe", m_avmm.get(c - 4, 0))], inc="act")
                    quarter_free[2 * i2] = m_e[c]
                    quarter_free[2 * i2 + 1] = m_e[c]

                def emit_av(c):
                    r4 = c % 4
                    strt, stop = (c == 0), (c == 31)
                    wav = [("act", m_e[c])]
                    if strt:
                        wav += [("dve", m_norm_prev), ("dve", m_vq[31])]
                    def av_e(e, M, c=c, r4=r4, pc=pc, strt=strt, stop=stop):
                        return e.matmul(ps_b5[0:64, :], Vq[c][:, (2 * pc) * 64:(2 * pc) * 64 + 64],
                                        SCBr[r4][:, 0:512], start=strt, stop=stop, tile_position=(0, 0))
                    st("tensor", av_e, waits=wav)
                    def den_e(e, M, c=c, r4=r4, strt=strt, stop=stop):
                        return e.matmul(ps_b5[64:65, :], ones_col[:], SCBr[r4][:, 0:512],
                                        start=strt, stop=stop, tile_position=(0, 64))
                    st("tensor", den_e)
                    def av_o(e, M, c=c, r4=r4, pc=pc, strt=strt, stop=stop):
                        return e.matmul(ps_bc[64:128, :], Vq[c][:, (2 * pc + 1) * 64:(2 * pc + 1) * 64 + 64],
                                        SCBr[r4][:, 512:1024], start=strt, stop=stop, tile_position=(0, 64))
                    st("tensor", av_o)
                    def den_o(e, M, c=c, r4=r4, strt=strt, stop=stop):
                        return e.matmul(ps_bc[32:33, :], ones_col[:], SCBr[r4][:, 512:1024],
                                        start=strt, stop=stop, tile_position=(0, 32))
                    m_avmm[c] = st("tensor", den_o, inc="pe")

                for c in range(32):
                    emit_scores(c)
                    if c >= 1:
                        emit_av(c - 1)
                emit_av(31)
                m_avmm_prev = {c - 32: m_avmm[c] for c in range(28, 32)}
                m_exp_prev = {c: m_e[c] for c in range(30, 32)}
                m_av = m_avmm[31]
                m_r1 = st("vector", lambda e, M: e.reciprocal(rv[64:65, :], ps_b5[64:65, :]),
                          waits=[("pe", m_av), ("pe", m_bc_done)], inc="dve")
                m_r1b = st("vector", lambda e, M: e.reciprocal(rv[32:33, :], ps_bc[32:33, :]),
                           waits=[("dve", m_r1)], inc="dve")
                m_ae = st("vector", lambda e, M: e.tensor_copy(av_sb[0:64, :], ps_b5[0:64, :]),
                          waits=[("dve", m_r1b), ("dve", m_mult)], inc="dve")
                m_ao = st("vector", lambda e, M: e.tensor_copy(av_sb[64:128, :], ps_bc[64:128, :]),
                          waits=[("dve", m_ae)], inc="dve")
                m_r2 = st("vector", lambda e, M: e.tensor_copy(rvh[64:65, :], rv[64:65, :]),
                          waits=[("dve", m_ao), ("pe", m_bc_done)], inc="dve")
                m_r2b = st("vector", lambda e, M: e.tensor_copy(rvh[32:33, :], rv[32:33, :]),
                           waits=[("dve", m_r2)], inc="dve")
                m_r4a = st("vector", lambda e, M: e.tensor_tensor(rvl[64:65, :], rv[64:65, :], rvh[64:65, :], OP.subtract),
                           waits=[("dve", m_r2b)], inc="dve")
                m_r4_ = st("vector", lambda e, M: e.tensor_tensor(rvl[32:33, :], rv[32:33, :], rvh[32:33, :], OP.subtract),
                           waits=[("dve", m_r4a)], inc="dve")
                st("tensor", lambda e, M: e.matmul(ps_tok[0:64, 0:512], ones_bc[64:65, 0:64], rvh[64:65, :],
                                                   start=True, stop=False, tile_position=(64, 0)),
                   waits=[("dve", m_r4_), ("dve", m_mult)])
                st("tensor", lambda e, M: e.matmul(ps_tok[0:64, 0:512], ones_bc[64:65, 0:64], rvl[64:65, :],
                                                   start=False, stop=True, tile_position=(64, 0)))
                st("tensor", lambda e, M: e.matmul(ps_tok[64:128, 512:1024], ones_bc[32:33, 0:64], rvh[32:33, :],
                                                   start=True, stop=False, tile_position=(32, 64)))
                m_bc_done = st("tensor", lambda e, M: e.matmul(ps_tok[64:128, 512:1024], ones_bc[32:33, 0:64], rvl[32:33, :],
                                                               start=False, stop=True, tile_position=(32, 64)), inc="pe")
                m_me = st("vector", lambda e, M, ch=ch, qt=qt: e.tensor_tensor(
                    AOT[ch][0:64, qt * 512:(qt + 1) * 512], av_sb[0:64, :], ps_tok[0:64, 0:512], OP.mult),
                    waits=[("pe", m_bc_done), ("dve", m_ao)], inc="dve")
                m_mult = st("vector", lambda e, M, ch=ch, qt=qt: e.tensor_tensor(
                    AOT[ch][64:128, qt * 512:(qt + 1) * 512], av_sb[64:128, :], ps_tok[64:128, 512:1024], OP.mult),
                    waits=[("dve", m_me)], inc="dve")
                m_norm_prev = m_ao
                m_av_done = m_av

    p2_pe_done = sc.n["pe"]
    p2_mult_done = m_mult

    if not do_p3:
        do_p4 = False
    m_wo = 0
    for c in (range(8) if do_p3 else range(0)):
        m_wo = st("gpsimd", lambda e, M, c=c: e.dma_start(
            out=Wo_sb[c].ap(), in_=wo.rearrange("(c p) n -> p c n", p=128)[:, c, :]),
            waits=[("pe", p2_pe_done)], inc="dg")
    m_oev = [0] * 8
    m_tr_prev = 0
    m_xadd = [0, 0]
    m_sq_prev = 0
    m_ln_prev = 0
    m_var0_prev = 0
    m_x1store = [0, 0]
    m_xres_ld = [0, 0]
    m_ps_tok_free = 0
    for qt in (range(4) if do_p3 else range(0)):
        for outc in range(8):
            ps_o = ps_b5 if outc % 2 == 0 else ps_bc
            w0 = [("dg", m_wo), ("dve", p2_mult_done), ("act", m_oev[outc - 2] if outc >= 2 else 0),
                  ("pe", m_tr_prev)]
            if qt == 0 and outc < 2:
                w0 += [("dve", p2_mult_done)]
            mm_last = None
            for ci in range(8):
                def mmo(e, M, ci=ci, outc=outc, qt=qt, ps_o=ps_o):
                    return e.matmul(ps_o[:], Wo_sb[ci][:, outc * 128:(outc + 1) * 128],
                                    AOT[ci][:, qt * 512:(qt + 1) * 512],
                                    start=(ci == 0), stop=(ci == 7))
                mm_last = st("tensor", mmo, waits=(w0 if ci == 0 else ()), inc=("pe" if ci == 7 else None))
            m_oev[outc] = st("scalar", lambda e, M, outc=outc, ps_o=ps_o: e.activation(
                o_sb[outc][:], ps_o[:], AF.Identity, bias=bo_sb[:, outc:outc + 1]),
                waits=[("pe", mm_last), ("pe", m_tr_prev)], inc="act")
        for qs in range(4):
            i2 = qs % 2
            tok0 = qt * 512 + qs * 128
            m_xres_ld[i2] = st("sync", lambda e, M, i2=i2, tok0=tok0: e.dma_start(
                out=xres[i2][:], in_=x_in[tok0:tok0 + 128, :]),
                waits=[("dve", m_xadd[i2]), ("pe", p2_pe_done)], inc="ds")
            m_tr = None
            for outc in range(8):
                def trp(e, M, outc=outc, qs=qs):
                    return e.transpose(ps_tok[:, outc * 128:(outc + 1) * 128],
                                      o_sb[outc][:, qs * 128:(qs + 1) * 128], ident[:])
                m_tr = st("tensor", trp,
                          waits=([("act", m_oev[7]), ("dve", m_ps_tok_free), ("dve", p2_mult_done)] if outc == 0 else ()),
                          inc=("pe" if outc == 7 else None))
            m_tr_prev = m_tr
            m_add = st("vector", lambda e, M, i2=i2: e.tensor_tensor(
                xpre[i2][:], ps_tok[:], xres[i2][:], OP.add),
                waits=[("pe", m_tr), ("ds", m_xres_ld[i2]), ("act", m_sq_prev), ("dve", m_ln_prev)], inc="dve")
            m_ps_tok_free = m_add
            m_xadd[i2] = m_add
            m_red = st("vector", lambda e, M, i2=i2: e.tensor_reduce(
                ln_sum[:], xpre[i2][:], AX.X, OP.add), waits=[("dve", m_add), ("dve", m_ln_prev)], inc="dve")
            m_sq = st("scalar", lambda e, M, i2=i2: e.activation(
                sq1[:], xpre[i2][:], AF.Square, accum_out=ln_ssq[:, 0:1]),
                waits=[("dve", m_add), ("dve", m_var0_prev)], inc="act")
            m_sq_prev = m_sq
            m_mean = st("vector", lambda e, M: e.tensor_scalar(
                ln_mean[:], ln_sum[:], 1.0 / D, None, OP.mult), waits=[("dve", m_red)], inc="dve")
            m_m2 = st("vector", lambda e, M: e.tensor_tensor(
                ln_m2[:], ln_mean[:], ln_mean[:], OP.mult), waits=[("dve", m_mean)], inc="dve")
            m_var0 = st("vector", lambda e, M: e.tensor_scalar(
                ln_var[:], ln_ssq[:], 1.0 / D, EPS, OP.mult, OP.add),
                waits=[("act", m_sq), ("dve", m_m2)], inc="dve")
            m_var0_prev = m_var0
            m_var = st("vector", lambda e, M: e.tensor_tensor(
                ln_var[:], ln_var[:], ln_m2[:], OP.subtract), waits=[("dve", m_var0)], inc="dve")
            m_sd = st("scalar", lambda e, M: e.activation(
                ln_sd[:], ln_var[:], AF.Sqrt), waits=[("dve", m_var)], inc="act")
            m_rstd = st("vector", lambda e, M: e.reciprocal(
                ln_rstd[:], ln_sd[:]), waits=[("act", m_sd)], inc="dve")
            m_app = st("vector", lambda e, M, i2=i2: e.tensor_scalar(
                x1t[i2][:], xpre[i2][:], ln_mean[:, 0:1], ln_rstd[:, 0:1], OP.subtract, OP.mult),
                waits=[("dve", m_rstd), ("dg", m_x1store[i2])], inc="dve")
            m_cast = st("vector", lambda e, M, i2=i2: e.tensor_copy(
                x1tb[i2][:], x1t[i2][:]), waits=[("dve", m_app)], inc="dve")
            m_ln_prev = m_app
            st("gpsimd", lambda e, M, i2=i2, tok0=tok0: e.dma_start(
                out=s_x1f[tok0:tok0 + 128, :], in_=x1t[i2][:]), waits=[("dve", m_app)], inc="dg")
            for c8 in range(8):
                m_x1store[i2] = st("gpsimd", lambda e, M, i2=i2, tok0=tok0, c8=c8: e.dma_start(
                    out=s_x1b[c8, tok0:tok0 + 128, :],
                    in_=x1tb[i2][:, c8 * 128:(c8 + 1) * 128]),
                    waits=([("dve", m_cast)] if c8 == 0 else ()), inc="dg")

    p3_pe = sc.n["pe"]
    p3_act = sc.n["act"]
    p3_dve = sc.n["dve"]
    p3_dg = sc.n["dg"]
    R8 = range(8) if do_p4 else range(0)
    R24 = range(24) if do_p4 else range(0)

    p3w = [("pe", p3_pe), ("act", p3_act), ("dve", p3_dve), ("dg", p3_dg)]
    m_x1f = 0
    for c in R8:
        m_x1f = st("sync", lambda e, M, c=c: e.dma_start_transpose(
            X1F[c][:], s_x1b[c]), waits=p3w, inc="ds")
    m_wgt = 0
    for c in R8:
        m_wgt = st("sync", lambda e, M, c=c: e.dma_start_transpose(
            WGT[c][:], s_wg[c]), waits=p3w + [("dg", m_wgc)], inc="ds")
    m_wut = 0
    for c in R8:
        m_wut = st("sync", lambda e, M, c=c: e.dma_start_transpose(
            WUT[c][:], s_wu[c]), waits=p3w + [("dg", m_wuc)], inc="ds")
    m_wdt = 0
    for j in R24:
        m_wdt = st("sync", lambda e, M, j=j: e.dma_start_transpose(
            WDT[j][:], s_wd[j]), waits=p3w + [("dg", m_wdc)], inc="ds")

    m_si = [0] * 24
    m_h = [0] * 24
    m_dnj = [0] * 24
    m_dnev = [0] * 8
    m_x2add = [0, 0]
    m_x2store = [0, 0]
    m_tr2_prev = 0
    m_x1re_read = 0
    m_dnev_last = 0
    for qt in (range(8) if do_p4 else range(0)):
        for j in range(24):
            pj = (ps_b5, ps_bc, ps_tok)[j % 3]
            w0 = [("ds", m_wgt), ("ds", m_wut), ("ds", m_x1f)]
            if j >= 3:
                w0 += [("act", m_si[j - 3]), ("dve", m_h[j - 3])]
            elif qt > 0:
                w0 += [("act", m_si[21 + j]), ("dve", m_h[21 + j])]
            else:
                w0 += [("dve", p2_mult_done)]
            mm_last = None
            for ci in range(8):
                def mmg(e, M, ci=ci, j=j, qt=qt, pj=pj):
                    return e.matmul(pj[:, 0:256], WGT[ci][:, j * 128:(j + 1) * 128],
                                    X1F[ci][:, qt * 256:(qt + 1) * 256],
                                    start=(ci == 0), stop=(ci == 7))
                st("tensor", mmg, waits=(w0 if ci == 0 else ()))
            for ci in range(8):
                def mmu(e, M, ci=ci, j=j, qt=qt, pj=pj):
                    return e.matmul(pj[:, 256:512], WUT[ci][:, j * 128:(j + 1) * 128],
                                    X1F[ci][:, qt * 256:(qt + 1) * 256],
                                    start=(ci == 0), stop=(ci == 7))
                mm_last = st("tensor", mmu, inc=("pe" if ci == 7 else None))
            m_si[j] = st("scalar", lambda e, M, pj=pj: e.activation(
                s_sb[:], pj[:, 0:256], AF.Silu, scale=DEQ),
                waits=[("pe", mm_last)] + ([("dve", m_h[j - 1])] if j >= 1 else
                                           ([("dve", m_h[23])] if qt > 0 else [])), inc="act")
            m_h[j] = st("vector", lambda e, M, j=j, pj=pj: e.tensor_tensor(
                h_sb[j % 2][:], s_sb[:], pj[:, 256:512], OP.mult),
                waits=[("act", m_si[j])] + ([("pe", m_dnj[j - 2])] if j >= 2 else
                                            ([("pe", m_dnj[22 + j])] if qt > 0 else [])), inc="dve")
            w_dn = [("dve", m_h[j]), ("ds", m_wdt)]
            if j == 0:
                w_dn += [("dve", m_x2add[0]), ("dve", m_x2add[1]), ("act", m_dnev_last)]
                for bank in range(4):
                    def mmz(e, M, bank=bank):
                        return e.matmul(ps_big[:, bank * 512:(bank + 1) * 512],
                                        zeros_row[:], bvrow[0:1, 0:512], start=True, stop=False)
                    st("tensor", mmz, waits=(w_dn if bank == 0 else ()))
                w_dn = []
            mm_last = None
            for outc in range(8):
                def mmdn(e, M, outc=outc, j=j):
                    return e.matmul(ps_big[:, outc * 256:(outc + 1) * 256],
                                    WDT[j][:, outc * 128:(outc + 1) * 128], h_sb[j % 2][:],
                                    start=False, stop=(j == 23))
                mm_last = st("tensor", mmdn, waits=(w_dn if outc == 0 else ()), inc=("pe" if outc == 7 else None))
            m_dnj[j] = mm_last
        for outc in range(8):
            m_dnev[outc] = st("scalar", lambda e, M, outc=outc: e.activation(
                dn_sb[:, outc * 256:(outc + 1) * 256], ps_big[:, outc * 256:(outc + 1) * 256],
                AF.Identity, scale=DEQ2),
                waits=[("pe", m_dnj[23]), ("pe", m_tr2_prev)], inc="act")
        m_dnev_last = m_dnev[7]
        for qs in range(2):
            i2 = qs % 2
            tok0 = qt * 256 + qs * 128
            m_ld = st("sync", lambda e, M, tok0=tok0: e.dma_start(
                out=x1re[:], in_=s_x1f[tok0:tok0 + 128, :]),
                waits=[("dve", m_x1re_read), ("dg", p3_dg)], inc="ds")
            m_tr = None
            for outc in range(8):
                def trp2(e, M, outc=outc, qs=qs):
                    return e.transpose(ps_big[:, qs * 1024 + outc * 128: qs * 1024 + (outc + 1) * 128],
                                      dn_sb[:, outc * 256 + qs * 128: outc * 256 + (qs + 1) * 128], ident[:])
                m_tr = st("tensor", trp2,
                          waits=([("act", m_dnev[7]), ("dve", m_x2add[i2])] if outc == 0 else ()),
                          inc=("pe" if outc == 7 else None))
            m_tr2_prev = m_tr
            m_add = st("vector", lambda e, M, i2=i2, qs=qs: e.tensor_tensor(
                x2pre[i2][:], ps_big[:, qs * 1024:(qs + 1) * 1024], x1re[:], OP.add),
                waits=[("pe", m_tr), ("ds", m_ld), ("dg", m_x2store[i2]),
                       ("act", m_sq_prev), ("dve", m_ln_prev)], inc="dve")
            m_x1re_read = m_add
            m_x2add[i2] = m_add
            m_red = st("vector", lambda e, M, i2=i2: e.tensor_reduce(
                ln_sum[:], x2pre[i2][:], AX.X, OP.add), waits=[("dve", m_add), ("dve", m_ln_prev)], inc="dve")
            m_sq = st("scalar", lambda e, M, i2=i2: e.activation(
                sq2[:], x2pre[i2][:], AF.Square, accum_out=ln_ssq[:, 0:1]),
                waits=[("dve", m_add), ("dve", m_var0_prev)], inc="act")
            m_sq_prev = m_sq
            m_mean = st("vector", lambda e, M: e.tensor_scalar(
                ln_mean[:], ln_sum[:], 1.0 / D, None, OP.mult), waits=[("dve", m_red)], inc="dve")
            m_m2 = st("vector", lambda e, M: e.tensor_tensor(
                ln_m2[:], ln_mean[:], ln_mean[:], OP.mult), waits=[("dve", m_mean)], inc="dve")
            m_var0 = st("vector", lambda e, M: e.tensor_scalar(
                ln_var[:], ln_ssq[:], 1.0 / D, EPS, OP.mult, OP.add),
                waits=[("act", m_sq), ("dve", m_m2)], inc="dve")
            m_var0_prev = m_var0
            m_var = st("vector", lambda e, M: e.tensor_tensor(
                ln_var[:], ln_var[:], ln_m2[:], OP.subtract), waits=[("dve", m_var0)], inc="dve")
            m_sd = st("scalar", lambda e, M: e.activation(
                ln_sd[:], ln_var[:], AF.Sqrt), waits=[("dve", m_var)], inc="act")
            m_rstd = st("vector", lambda e, M: e.reciprocal(
                ln_rstd[:], ln_sd[:]), waits=[("act", m_sd)], inc="dve")
            m_app = st("vector", lambda e, M, i2=i2: e.tensor_scalar(
                x2pre[i2][:], x2pre[i2][:], ln_mean[:, 0:1], ln_rstd[:, 0:1], OP.subtract, OP.mult),
                waits=[("dve", m_rstd)], inc="dve")
            m_ln_prev = m_app
            m_x2store[i2] = st("gpsimd", lambda e, M, i2=i2, tok0=tok0: e.dma_start(
                out=out[tok0:tok0 + 128, :], in_=x2pre[i2][:]), waits=[("dve", m_app)], inc="dg")

    final_dg = sc.n["dg"]
    sc.q["gpsimd"].append(lambda e, SM: e.wait_ge(SM["dg"], final_dg))
    sc.meta["gpsimd"].append(((("dg", final_dg),), None, "final"))
    sc.check()

    from contextlib import ExitStack
    es = ExitStack()
    SM = {s: es.enter_context(nc.semaphore(s)) for s in ("ds", "dg", "pe", "act", "dve")}
    block = es.enter_context(nc.Block())

    @block.sync
    def _(e):
        for f in sc.q["sync"]:
            f(e, SM)

    @block.gpsimd
    def _(e):
        for f in sc.q["gpsimd"]:
            f(e, SM)

    @block.tensor
    def _(e):
        for f in sc.q["tensor"]:
            f(e, SM)

    @block.scalar
    def _(e):
        for f in sc.q["scalar"]:
            f(e, SM)

    @block.vector
    def _(e):
        for f in sc.q["vector"]:
            f(e, SM)

    es.close()
    return nc


_NC_CACHE = None


def kernel(**inputs):
    global _NC_CACHE
    X = np.asarray(inputs["L0_attn_X"], dtype=np.float32)
    args = {
        "wq": np.ascontiguousarray(inputs["L0_attn_Wq"], dtype=np.float32),
        "wk": np.ascontiguousarray(inputs["L0_attn_Wk"], dtype=np.float32),
        "wv": np.ascontiguousarray(inputs["L0_attn_Wv"], dtype=np.float32),
        "wo": np.ascontiguousarray(inputs["L0_attn_Wo"], dtype=np.float32),
        "bq": np.ascontiguousarray(inputs["L0_attn_bq"], dtype=np.float32),
        "bk": np.ascontiguousarray(inputs["L0_attn_bk"], dtype=np.float32),
        "bv": np.ascontiguousarray(inputs["L0_attn_bv"], dtype=np.float32),
        "bo": np.ascontiguousarray(inputs["L0_attn_bo"], dtype=np.float32),
        "wg": np.ascontiguousarray(inputs["L0_ffn_w_gate"], dtype=np.int32),
        "wu": np.ascontiguousarray(inputs["L0_ffn_w_up"], dtype=np.int32),
        "wd": np.ascontiguousarray(inputs["L0_ffn_w_down"], dtype=np.int32),
    }
    if _NC_CACHE is None:
        _NC_CACHE = build_nc()
    nc = _NC_CACHE
    in_maps = []
    for c in range(8):
        b, qh = c // 2, c % 2
        xb = X[b]
        if qh:
            xb = np.roll(xb, -T, axis=0)
        m = {"x_in": np.ascontiguousarray(xb)}
        m.update(args)
        in_maps.append(m)
    res = run_bass_kernel_spmd(nc, in_maps, core_ids=list(range(8)))
    outp = np.empty((B, S, D), dtype=np.float32)
    for c in range(8):
        b, qh = c // 2, c % 2
        outp[b, qh * T:(qh + 1) * T] = res.results[c]["out"]
    return outp


if __name__ == "__main__":
    rng = np.random.default_rng(0)
    ins = {
        "L0_attn_X": rng.standard_normal((B, S, D), dtype=np.float32),
        "L0_attn_Wq": rng.standard_normal((D, D), dtype=np.float32) / 32,
        "L0_attn_bq": rng.standard_normal(D, dtype=np.float32) * 0.1,
        "L0_attn_Wk": rng.standard_normal((D, D), dtype=np.float32) / 32,
        "L0_attn_bk": rng.standard_normal(D, dtype=np.float32) * 0.1,
        "L0_attn_Wv": rng.standard_normal((D, D), dtype=np.float32) / 32,
        "L0_attn_bv": rng.standard_normal(D, dtype=np.float32) * 0.1,
        "L0_attn_Wo": rng.standard_normal((D, D), dtype=np.float32) / 32,
        "L0_attn_bo": rng.standard_normal(D, dtype=np.float32) * 0.1,
        "L0_ffn_w_gate": rng.integers(-127, 128, (HID, D)).astype(np.int32),
        "L0_ffn_w_up": rng.integers(-127, 128, (HID, D)).astype(np.int32),
        "L0_ffn_w_down": rng.integers(-127, 128, (D, HID)).astype(np.int32),
    }
    y = kernel(**ins)
    print("out", y.shape, y.dtype, np.abs(y).max())


# version 13
# speedup vs baseline: 1.8016x; 1.0812x over previous
import numpy as np
import concourse.bass as bass
import concourse.mybir as mybir
from concourse.bass_utils import run_bass_kernel_spmd

F32 = mybir.dt.float32
BF16 = mybir.dt.bfloat16
I32 = mybir.dt.int32
AF = mybir.ActivationFunctionType
OP = mybir.AluOpType
AX = mybir.AxisListType

B, S, D, H, DK, HID = 4, 4096, 1024, 16, 64, 3072
T = 2048
NQT = 4
EPS = 1e-5
DEQ = 1.0 / 127.0
DEQ2 = DEQ * DEQ

ENGS = ("sync", "gpsimd", "tensor", "scalar", "vector")
DMA_SEMS = ("ds", "dg")


class Sched:

    def __init__(self, limit=None):
        self.limit = limit
        self.total = 0
        self.q = {e: [] for e in ENGS}
        self.n = {s: 0 for s in ("ds", "dg", "pe", "act", "dve")}
        self.meta = {e: [] for e in ENGS}

    def check(self):
        ptr = {e: 0 for e in ENGS}
        cnt = {s: 0 for s in self.n}
        progress = True
        while progress:
            progress = False
            for e in ENGS:
                while ptr[e] < len(self.meta[e]):
                    waits, inc, tag = self.meta[e][ptr[e]]
                    if any(cnt[s] < t for (s, t) in waits):
                        break
                    if inc:
                        cnt[inc] += 16 if inc in DMA_SEMS else 1
                    ptr[e] += 1
                    progress = True
        stuck = {e: ptr[e] for e in ENGS if ptr[e] < len(self.meta[e])}
        if stuck:
            detail = {}
            for e in stuck:
                waits, inc, tag = self.meta[e][ptr[e]]
                detail[e] = (ptr[e], tag, [(s, t, cnt[s]) for (s, t) in waits if cnt[s] < t])
            raise RuntimeError(f"DEADLOCK: stuck steps {detail}")

    def step(self, eng, emit, waits=(), inc=None, tag=""):
        if self.limit is not None and self.total >= self.limit:
            return self.n.get(inc)
        self.total += 1
        waits = tuple((s, t) for (s, t) in waits if t)
        for s, t in waits:
            assert t <= self.n[s], f"wait target {s}={t} not yet emitted ({self.n[s]})"
        tgt = None
        if inc is not None:
            self.n[inc] += 16 if inc in DMA_SEMS else 1
            tgt = self.n[inc]

        def run(e, SM, waits=waits, emit=emit, inc=inc):
            for s, t in waits:
                e.wait_ge(SM[s], t)
            r = emit(e, SM)
            if inc is not None:
                assert r is not None, "inc requires a real instruction"
                r.then_inc(SM[inc], 16 if inc in DMA_SEMS else 1)
        self.q[eng].append(run)
        self.meta[eng].append((waits, inc, tag))
        return tgt


def build_nc(n_quads=4, n_heads=4, do_p3=True, do_p4=True, limit=None):
    nc = bass.Bass("TRN2", target_bir_lowering=False)

    x_in = nc.dram_tensor("x_in", [S, D], F32, kind="ExternalInput").ap()
    wq = nc.dram_tensor("wq", [D, D], F32, kind="ExternalInput").ap()
    wk = nc.dram_tensor("wk", [D, D], F32, kind="ExternalInput").ap()
    wv = nc.dram_tensor("wv", [D, D], F32, kind="ExternalInput").ap()
    wo = nc.dram_tensor("wo", [D, D], F32, kind="ExternalInput").ap()
    bq = nc.dram_tensor("bq", [D], F32, kind="ExternalInput").ap()
    bk = nc.dram_tensor("bk", [D], F32, kind="ExternalInput").ap()
    bv = nc.dram_tensor("bv", [D], F32, kind="ExternalInput").ap()
    bo = nc.dram_tensor("bo", [D], F32, kind="ExternalInput").ap()
    wg = nc.dram_tensor("wg", [HID, D], I32, kind="ExternalInput").ap()
    wu = nc.dram_tensor("wu", [HID, D], I32, kind="ExternalInput").ap()
    wd = nc.dram_tensor("wd", [D, HID], I32, kind="ExternalInput").ap()
    out = nc.dram_tensor("out", [T, D], F32, kind="ExternalOutput").ap()

    s_xbf = nc.dram_tensor("s_xbf", [8, S, 128], BF16, kind="Internal").ap()
    s_wg = nc.dram_tensor("s_wg", [8, HID, 128], BF16, kind="Internal").ap()
    s_wu = nc.dram_tensor("s_wu", [8, HID, 128], BF16, kind="Internal").ap()
    s_wd = nc.dram_tensor("s_wd", [24, D, 128], BF16, kind="Internal").ap()
    s_x1b = nc.dram_tensor("s_x1b", [8, T, 128], BF16, kind="Internal").ap()
    s_x1f = nc.dram_tensor("s_x1f", [T, D], F32, kind="Internal").ap()

    ident_dram = nc.inline_tensor(np.eye(128, dtype=np.float32), name="ident128")

    SBASE = 16512
    A = lambda name, shape, dt, off: nc.alloc_sbuf_tensor_at(name, shape, dt, offset=SBASE + off)
    XT = [A(f"XT{c}", [128, S], BF16, 8192 * c) for c in range(8)]
    WQq = A("WQq", [128, 2048], BF16, 65536)
    WKq = A("WKq", [128, 2048], BF16, 69632)
    WVq = A("WVq", [128, 2048], BF16, 73728)
    KTq = [A(f"KTq{p}", [128, S], BF16, 77824 + 8192 * p) for p in range(2)]
    QTq = [A(f"QTq{p}", [128, T], BF16, 94208 + 4096 * p) for p in range(2)]
    Vq = [A(f"Vq{t}", [128, 256], BF16, 102400 + 512 * t) for t in range(32)]
    SCBr = [A(f"SCBr{r}", [128, 1024], BF16, 118784 + 2048 * r) for r in range(4)]
    Kf32 = A("Kf32", [128, S], F32, 126976)
    AOT = [A(f"AOT{c}", [128, T], BF16, 151552 + 4096 * c) for c in range(8)]
    av_sb = A("av_sb", [128, 512], F32, 184320)
    rv = A("rv", [65, 512], F32, 186368)
    rvh = A("rvh", [65, 512], BF16, 188416)
    rvl = A("rvl", [65, 512], BF16, 189440)
    rvd = A("rvd", [65, 512], F32, 190464)
    Wo_sb = [A(f"Wo{c}", [128, 1024], BF16, 2048 * c) for c in range(8)]
    o_sb = [A(f"osb{i}", [128, 512], F32, 16384 + 2048 * i) for i in range(8)]
    xres = [A(f"xres{i}", [128, 1024], F32, 32768 + 4096 * i) for i in range(2)]
    xpre = [A(f"xpre{i}", [128, 1024], F32, 40960 + 4096 * i) for i in range(2)]
    sq1 = A("sq1", [128, 1024], F32, 49152)
    x1t = [A(f"x1t{i}", [128, 1024], F32, 53248 + 4096 * i) for i in range(2)]
    x1tb = [A(f"x1tb{i}", [128, 1024], BF16, 61440 + 2048 * i) for i in range(2)]
    X1F = [A(f"X1F{c}", [128, T], BF16, 4096 * c) for c in range(8)]
    WGT = [A(f"WGT{c}", [128, HID], BF16, 32768 + 6144 * c) for c in range(8)]
    WUT = [A(f"WUT{c}", [128, HID], BF16, 81920 + 6144 * c) for c in range(8)]
    WDT = [A(f"WDT{j}", [128, 1024], BF16, 131072 + 2048 * j) for j in range(24)]
    s_sb = A("s_sb", [128, 256], BF16, 180224)
    h_sb = [A(f"hsb{i}", [128, 256], BF16, 180736 + 512 * i) for i in range(2)]
    dn_sb = A("dn_sb", [128, 2048], F32, 181760)
    x2pre = [A(f"x2pre{i}", [128, 1024], F32, 189952 + 4096 * i) for i in range(2)]
    sq2 = A("sq2", [128, 1024], F32, 198144)
    x1re = A("x1re", [128, 1024], F32, 202240)
    P = lambda name, shape, dt: nc.alloc_sbuf_tensor(name, shape, dt, side="right")
    ident = P("ident", [128, 128], F32)
    ones_col = P("ones_col", [128, 1], BF16)
    ones_bc = P("ones_bc", [65, 64], BF16)
    ones_row = P("ones_row", [1, 128], BF16)
    zeros_row = P("zeros_row", [1, 128], BF16)
    bvrow = P("bvrow", [1, 1024], BF16)
    bq_sb = P("bq_sb", [128, 8], F32)
    bk_sb = P("bk_sb", [128, 8], F32)
    bo_sb = P("bo_sb", [128, 8], F32)
    ln_sum = P("ln_sum", [128, 1], F32)
    ln_ssq = P("ln_ssq", [128, 1], F32)
    ln_mean = P("ln_mean", [128, 1], F32)
    ln_var = P("ln_var", [128, 1], F32)
    ln_m2 = P("ln_m2", [128, 1], F32)
    ln_sd = P("ln_sd", [128, 1], F32)
    ln_rstd = P("ln_rstd", [128, 1], F32)
    kmean = P("kmean", [128, 1], F32)

    ps_big = nc.alloc_psum_tensor("ps_big", [128, 2048], F32)
    ps_b5 = nc.alloc_psum_tensor("ps_b5", [128, 512], F32)
    ps_bc = nc.alloc_psum_tensor("ps_bc", [128, 512], F32)
    ps_tok = nc.alloc_psum_tensor("ps_tok", [128, 1024], F32)

    sc = Sched(limit)
    st = sc.step

    m_xc = []
    for rq in range(4):
        m = 0
        for c8 in range(8):
            m = st("gpsimd", lambda e, M, rq=rq, c8=c8: e.dma_start(
                out=s_xbf[c8, rq * 1024:(rq + 1) * 1024, :],
                in_=x_in[rq * 1024:(rq + 1) * 1024, c8 * 128:(c8 + 1) * 128]), inc="dg")
        m_xc.append(m)
    m_bv = st("gpsimd", lambda e, M: e.dma_start(out=bvrow[0:1, :], in_=bv[None, :]), inc="dg")
    st("sync", lambda e, M: e.dma_start(out=ident[:], in_=ident_dram.ap()), inc="ds")
    for c in range(8):
        st("sync", lambda e, M, c=c: e.dma_start(out=bq_sb[:, c], in_=bq[c * 128:(c + 1) * 128]), inc="ds")
        st("sync", lambda e, M, c=c: e.dma_start(out=bk_sb[:, c], in_=bk[c * 128:(c + 1) * 128]), inc="ds")
        st("sync", lambda e, M, c=c: e.dma_start(out=bo_sb[:, c], in_=bo[c * 128:(c + 1) * 128]), inc="ds")
    m_bias = sc.n["ds"]
    st("vector", lambda e, M: e.memset(ones_col[:], 1.0), inc="dve")
    st("vector", lambda e, M: e.memset(ones_bc[:], 1.0), inc="dve")
    st("vector", lambda e, M: e.memset(ones_row[:], 1.0), inc="dve")
    m_ones = st("vector", lambda e, M: e.memset(zeros_row[:], 0.0), inc="dve")
    m_xt = 0
    for half in range(2):
        for c in range(8):
            m_xt = st("sync", lambda e, M, c=c, half=half: e.dma_start_transpose(
                XT[c][:, half * 2048:(half + 1) * 2048],
                s_xbf[c, half * 2048:(half + 1) * 2048, :]),
                waits=[("dg", m_xc[2 * half + 1])], inc="ds")

    quarter_free = [0, 0, 0, 0]
    m_exp = [0] * 8
    m_b5_free = []
    m_bc_read = 0
    m_bc_done = 0
    m_avsb = 0
    m_mult = 0
    m_av_done = 0
    m_kcent_all = 0
    m_vq = [0] * 32
    m_wslice_read = 0
    m_rvchain = 0
    m_avmm_prev = {}
    m_exp_prev = {}
    m_norm_prev = 0

    m_wgc = m_wuc = m_wdc = 0
    for g in range(n_quads):
        if g == 1:
            for c8 in range(8):
                m_wgc = st("gpsimd", lambda e, M, c8=c8: e.dma_start(
                    out=s_wg[c8], in_=wg[:, c8 * 128:(c8 + 1) * 128]), inc="dg")
            for c8 in range(8):
                m_wuc = st("gpsimd", lambda e, M, c8=c8: e.dma_start(
                    out=s_wu[c8], in_=wu[:, c8 * 128:(c8 + 1) * 128]), inc="dg")
            for j24 in range(24):
                m_wdc = st("gpsimd", lambda e, M, j24=j24: e.dma_start(
                    out=s_wd[j24], in_=wd[:, j24 * 128:(j24 + 1) * 128]), inc="dg")
        cols = slice(g * 256, (g + 1) * 256)
        m_wqs = st("gpsimd", lambda e, M, v=wq.rearrange("(c p) n -> p c n", p=128)[:, :, cols]:
                   e.dma_start(out=WQq.ap().rearrange("p (c n) -> p c n", c=8), in_=v),
                   waits=[("pe", m_wslice_read)], inc="dg")
        m_wks = st("gpsimd", lambda e, M, v=wk.rearrange("(c p) n -> p c n", p=128)[:, :, cols]:
                   e.dma_start(out=WKq.ap().rearrange("p (c n) -> p c n", c=8), in_=v),
                   waits=[("pe", m_wslice_read)], inc="dg")
        m_wvs = st("gpsimd", lambda e, M, v=wv.rearrange("(c p) n -> p c n", p=128)[:, :, cols]:
                   e.dma_start(out=WVq.ap().rearrange("p (c n) -> p c n", c=8), in_=v),
                   waits=[("pe", m_wslice_read)], inc="dg")

        m_kcent = [0, 0]
        for pc in range(2):
            m_kevac = [0] * 8
            for tcol in range(8):
                qb = tcol % 4
                w0 = [("ds", m_xt), ("dg", m_wks), ("act", quarter_free[qb]),
                      ("pe", m_av_done), ("dve", m_mult)]
                mm_last = None
                for ci in range(8):
                    def mmk(e, M, ci=ci, pc=pc, tcol=tcol, qb=qb):
                        return e.matmul(ps_big[:, qb * 512:qb * 512 + 512],
                                        WKq[:, ci * 256 + pc * 128: ci * 256 + pc * 128 + 128],
                                        XT[ci][:, tcol * 512:(tcol + 1) * 512],
                                        start=(ci == 0), stop=(ci == 7))
                    mm_last = st("tensor", mmk, waits=(w0 if ci == 0 else ()), inc=("pe" if ci == 7 else None))
                def kev(e, M, pc=pc, tcol=tcol, qb=qb, g=g):
                    return e.activation(Kf32[:, tcol * 512:(tcol + 1) * 512], ps_big[:, qb * 512:qb * 512 + 512],
                                        AF.Identity, bias=bk_sb[:, 2 * g + pc:2 * g + pc + 1])
                m_kevac[tcol] = st("scalar", kev,
                                   waits=[("pe", mm_last), ("ds", m_bias),
                                          ("dve", m_kcent[0] if pc == 1 else m_kcent_all)],
                                   inc="act")
                quarter_free[qb] = m_kevac[tcol]
            m_red = st("vector", lambda e, M: e.tensor_reduce(kmean[:], Kf32[:], AX.X, OP.add),
                       waits=[("act", m_kevac[7]), ("dve", m_kcent_all)], inc="dve")
            m_scl = st("vector", lambda e, M: e.tensor_scalar(kmean[:], kmean[:], 1.0 / S, None, OP.mult),
                       waits=[("dve", m_red)], inc="dve")
            m_kcent[pc] = st("vector", lambda e, M, pc=pc: e.tensor_scalar(
                KTq[pc][:], Kf32[:], kmean[:, 0:1], None, OP.subtract),
                waits=[("dve", m_scl), ("pe", m_av_done)], inc="dve")
        m_kcent_all = m_kcent[1]

        m_qevac = [[0] * NQT, [0] * NQT]
        for pc in range(2):
            for qt in range(NQT):
                qb = qt % 4
                w0 = [("act", quarter_free[qb]), ("dg", m_wqs)]
                mm_last = None
                for ci in range(8):
                    def mmq(e, M, ci=ci, pc=pc, qt=qt, qb=qb):
                        return e.matmul(ps_big[:, qb * 512:qb * 512 + 512],
                                        WQq[:, ci * 256 + pc * 128: ci * 256 + pc * 128 + 128],
                                        XT[ci][:, qt * 512:(qt + 1) * 512],
                                        start=(ci == 0), stop=(ci == 7))
                    mm_last = st("tensor", mmq, waits=(w0 if ci == 0 else ()), inc=("pe" if ci == 7 else None))
                def qev(e, M, pc=pc, qt=qt, qb=qb, g=g):
                    return e.activation(QTq[pc][:, qt * 512:(qt + 1) * 512], ps_big[:, qb * 512:qb * 512 + 512],
                                        AF.Identity, bias=bq_sb[:, 2 * g + pc:2 * g + pc + 1])
                m_qevac[pc][qt] = st("scalar", qev, waits=[("pe", mm_last)], inc="act")
                quarter_free[qb] = m_qevac[pc][qt]

        for tt in range(32):
            pjv = ps_b5 if tt % 2 == 0 else ps_bc
            w0 = [("dg", m_wvs), ("dve", m_ones), ("dg", m_bv)]
            if tt >= 2:
                w0.append(("dve", m_vq[tt - 2]))
            else:
                w0 += m_b5_free
            mm_last = None
            for ci in range(8):
                def mmv(e, M, ci=ci, tt=tt, pjv=pjv):
                    return e.matmul(pjv[:, 0:256],
                                    XT[ci][:, tt * 128:(tt + 1) * 128],
                                    WVq[:, ci * 256:(ci + 1) * 256],
                                    start=(ci == 0), stop=False)
                st("tensor", mmv, waits=(w0 if ci == 0 else ()))
            m_vmm = st("tensor", lambda e, M, pjv=pjv, g=g: e.matmul(
                pjv[:, 0:256], ones_row[:], bvrow[0:1, g * 256:(g + 1) * 256],
                start=False, stop=True), inc="pe")
            m_vq[tt] = st("vector", lambda e, M, tt=tt, pjv=pjv: e.tensor_copy(
                Vq[tt][:], pjv[:, 0:256]), waits=[("pe", m_vmm)], inc="dve")
        m_wslice_read = sc.n["pe"]
        m_b5_free = [("dve", m_vq[30]), ("dve", m_vq[31])]

        for pc in range(2):
            ch = 2 * g + pc
            for qt in range(NQT):
                m_avmm = dict(m_avmm_prev)
                m_e = {c - 32: v for c, v in m_exp_prev.items()}

                def emit_scores(c):
                    i2 = c % 2
                    r4 = c % 4
                    w0 = [("dve", m_kcent[pc]), ("act", m_qevac[pc][qt]),
                          ("act", quarter_free[2 * i2]), ("act", quarter_free[2 * i2 + 1])]
                    def mmse(e, M, c=c, i2=i2, pc=pc, qt=qt):
                        return e.matmul(ps_big[:, i2 * 1024: i2 * 1024 + 512],
                                        KTq[pc][0:64, c * 128:(c + 1) * 128],
                                        QTq[pc][0:64, qt * 512:(qt + 1) * 512],
                                        start=True, stop=True, tile_position=(0, 0))
                    st("tensor", mmse, waits=w0)
                    def mmso(e, M, c=c, i2=i2, pc=pc, qt=qt):
                        return e.matmul(ps_big[:, i2 * 1024 + 512: i2 * 1024 + 1024],
                                        KTq[pc][64:128, c * 128:(c + 1) * 128],
                                        QTq[pc][64:128, qt * 512:(qt + 1) * 512],
                                        start=True, stop=True, tile_position=(64, 0))
                    m_smm = st("tensor", mmso, inc="pe")
                    m_e[c] = st("scalar", lambda e, M, c=c, i2=i2, r4=r4: e.activation(
                        SCBr[r4][:], ps_big[:, i2 * 1024:(i2 + 1) * 1024], AF.Exp, scale=0.125),
                        waits=[("pe", m_smm), ("pe", m_avmm.get(c - 4, 0))], inc="act")
                    quarter_free[2 * i2] = m_e[c]
                    quarter_free[2 * i2 + 1] = m_e[c]

                def emit_av(c):
                    r4 = c % 4
                    strt, stop = (c == 0), (c == 31)
                    wav = [("act", m_e[c])]
                    if strt:
                        wav += [("dve", m_norm_prev), ("dve", m_vq[31])]
                    def av_e(e, M, c=c, r4=r4, pc=pc, strt=strt, stop=stop):
                        return e.matmul(ps_b5[0:64, :], Vq[c][:, (2 * pc) * 64:(2 * pc) * 64 + 64],
                                        SCBr[r4][:, 0:512], start=strt, stop=stop, tile_position=(0, 0))
                    st("tensor", av_e, waits=wav)
                    def den_e(e, M, c=c, r4=r4, strt=strt, stop=stop):
                        return e.matmul(ps_b5[64:65, :], ones_col[:], SCBr[r4][:, 0:512],
                                        start=strt, stop=stop, tile_position=(0, 64))
                    st("tensor", den_e)
                    def av_o(e, M, c=c, r4=r4, pc=pc, strt=strt, stop=stop):
                        return e.matmul(ps_bc[64:128, :], Vq[c][:, (2 * pc + 1) * 64:(2 * pc + 1) * 64 + 64],
                                        SCBr[r4][:, 512:1024], start=strt, stop=stop, tile_position=(0, 64))
                    st("tensor", av_o)
                    def den_o(e, M, c=c, r4=r4, strt=strt, stop=stop):
                        return e.matmul(ps_bc[32:33, :], ones_col[:], SCBr[r4][:, 512:1024],
                                        start=strt, stop=stop, tile_position=(0, 32))
                    m_avmm[c] = st("tensor", den_o, inc="pe")

                for c in range(32):
                    emit_scores(c)
                    if c >= 2:
                        emit_av(c - 2)
                emit_av(30)
                emit_av(31)
                m_avmm_prev = {c - 32: m_avmm[c] for c in range(28, 32)}
                m_exp_prev = {c: m_e[c] for c in range(30, 32)}
                m_av = m_avmm[31]
                m_r1 = st("vector", lambda e, M: e.reciprocal(rv[64:65, :], ps_b5[64:65, :]),
                          waits=[("pe", m_av), ("pe", m_bc_done)], inc="dve")
                m_r1b = st("vector", lambda e, M: e.reciprocal(rv[32:33, :], ps_bc[32:33, :]),
                           waits=[("dve", m_r1)], inc="dve")
                m_ae = st("vector", lambda e, M: e.tensor_copy(av_sb[0:64, :], ps_b5[0:64, :]),
                          waits=[("dve", m_r1b), ("dve", m_mult)], inc="dve")
                m_ao = st("vector", lambda e, M: e.tensor_copy(av_sb[64:128, :], ps_bc[64:128, :]),
                          waits=[("dve", m_ae)], inc="dve")
                m_r2 = st("vector", lambda e, M: e.tensor_copy(rvh[64:65, :], rv[64:65, :]),
                          waits=[("dve", m_ao), ("pe", m_bc_done)], inc="dve")
                m_r2b = st("vector", lambda e, M: e.tensor_copy(rvh[32:33, :], rv[32:33, :]),
                           waits=[("dve", m_r2)], inc="dve")
                m_r4a = st("vector", lambda e, M: e.tensor_tensor(rvl[64:65, :], rv[64:65, :], rvh[64:65, :], OP.subtract),
                           waits=[("dve", m_r2b)], inc="dve")
                m_r4_ = st("vector", lambda e, M: e.tensor_tensor(rvl[32:33, :], rv[32:33, :], rvh[32:33, :], OP.subtract),
                           waits=[("dve", m_r4a)], inc="dve")
                st("tensor", lambda e, M: e.matmul(ps_tok[0:64, 0:512], ones_bc[64:65, 0:64], rvh[64:65, :],
                                                   start=True, stop=False, tile_position=(64, 0)),
                   waits=[("dve", m_r4_), ("dve", m_mult)])
                st("tensor", lambda e, M: e.matmul(ps_tok[0:64, 0:512], ones_bc[64:65, 0:64], rvl[64:65, :],
                                                   start=False, stop=True, tile_position=(64, 0)))
                st("tensor", lambda e, M: e.matmul(ps_tok[64:128, 512:1024], ones_bc[32:33, 0:64], rvh[32:33, :],
                                                   start=True, stop=False, tile_position=(32, 64)))
                m_bc_done = st("tensor", lambda e, M: e.matmul(ps_tok[64:128, 512:1024], ones_bc[32:33, 0:64], rvl[32:33, :],
                                                               start=False, stop=True, tile_position=(32, 64)), inc="pe")
                m_me = st("vector", lambda e, M, ch=ch, qt=qt: e.tensor_tensor(
                    AOT[ch][0:64, qt * 512:(qt + 1) * 512], av_sb[0:64, :], ps_tok[0:64, 0:512], OP.mult),
                    waits=[("pe", m_bc_done), ("dve", m_ao)], inc="dve")
                m_mult = st("vector", lambda e, M, ch=ch, qt=qt: e.tensor_tensor(
                    AOT[ch][64:128, qt * 512:(qt + 1) * 512], av_sb[64:128, :], ps_tok[64:128, 512:1024], OP.mult),
                    waits=[("dve", m_me)], inc="dve")
                m_norm_prev = m_ao
                m_av_done = m_av

    p2_pe_done = sc.n["pe"]
    p2_mult_done = m_mult

    if not do_p3:
        do_p4 = False
    m_wo = 0
    for c in (range(8) if do_p3 else range(0)):
        m_wo = st("gpsimd", lambda e, M, c=c: e.dma_start(
            out=Wo_sb[c].ap(), in_=wo.rearrange("(c p) n -> p c n", p=128)[:, c, :]),
            waits=[("pe", p2_pe_done)], inc="dg")
    m_oev = [0] * 8
    m_tr_prev = 0
    m_xadd = [0, 0]
    m_sq_prev = 0
    m_ln_prev = 0
    m_var0_prev = 0
    m_x1store = [0, 0]
    m_xres_ld = [0, 0]
    m_ps_tok_free = 0
    for qt in (range(4) if do_p3 else range(0)):
        for outc in range(8):
            ps_o = ps_b5 if outc % 2 == 0 else ps_bc
            w0 = [("dg", m_wo), ("dve", p2_mult_done), ("act", m_oev[outc - 2] if outc >= 2 else 0),
                  ("pe", m_tr_prev)]
            if qt == 0 and outc < 2:
                w0 += [("dve", p2_mult_done)]
            mm_last = None
            for ci in range(8):
                def mmo(e, M, ci=ci, outc=outc, qt=qt, ps_o=ps_o):
                    return e.matmul(ps_o[:], Wo_sb[ci][:, outc * 128:(outc + 1) * 128],
                                    AOT[ci][:, qt * 512:(qt + 1) * 512],
                                    start=(ci == 0), stop=(ci == 7))
                mm_last = st("tensor", mmo, waits=(w0 if ci == 0 else ()), inc=("pe" if ci == 7 else None))
            m_oev[outc] = st("scalar", lambda e, M, outc=outc, ps_o=ps_o: e.activation(
                o_sb[outc][:], ps_o[:], AF.Identity, bias=bo_sb[:, outc:outc + 1]),
                waits=[("pe", mm_last), ("pe", m_tr_prev)], inc="act")
        for qs in range(4):
            i2 = qs % 2
            tok0 = qt * 512 + qs * 128
            m_xres_ld[i2] = st("sync", lambda e, M, i2=i2, tok0=tok0: e.dma_start(
                out=xres[i2][:], in_=x_in[tok0:tok0 + 128, :]),
                waits=[("dve", m_xadd[i2]), ("pe", p2_pe_done)], inc="ds")
            m_tr = None
            for outc in range(8):
                def trp(e, M, outc=outc, qs=qs):
                    return e.transpose(ps_tok[:, outc * 128:(outc + 1) * 128],
                                      o_sb[outc][:, qs * 128:(qs + 1) * 128], ident[:])
                m_tr = st("tensor", trp,
                          waits=([("act", m_oev[7]), ("dve", m_ps_tok_free), ("dve", p2_mult_done)] if outc == 0 else ()),
                          inc=("pe" if outc == 7 else None))
            m_tr_prev = m_tr
            m_add = st("vector", lambda e, M, i2=i2: e.tensor_tensor(
                xpre[i2][:], ps_tok[:], xres[i2][:], OP.add),
                waits=[("pe", m_tr), ("ds", m_xres_ld[i2]), ("act", m_sq_prev), ("dve", m_ln_prev)], inc="dve")
            m_ps_tok_free = m_add
            m_xadd[i2] = m_add
            m_red = st("vector", lambda e, M, i2=i2: e.tensor_reduce(
                ln_sum[:], xpre[i2][:], AX.X, OP.add), waits=[("dve", m_add), ("dve", m_ln_prev)], inc="dve")
            m_sq = st("scalar", lambda e, M, i2=i2: e.activation(
                sq1[:], xpre[i2][:], AF.Square, accum_out=ln_ssq[:, 0:1]),
                waits=[("dve", m_add), ("dve", m_var0_prev)], inc="act")
            m_sq_prev = m_sq
            m_mean = st("vector", lambda e, M: e.tensor_scalar(
                ln_mean[:], ln_sum[:], 1.0 / D, None, OP.mult), waits=[("dve", m_red)], inc="dve")
            m_m2 = st("vector", lambda e, M: e.tensor_tensor(
                ln_m2[:], ln_mean[:], ln_mean[:], OP.mult), waits=[("dve", m_mean)], inc="dve")
            m_var0 = st("vector", lambda e, M: e.tensor_scalar(
                ln_var[:], ln_ssq[:], 1.0 / D, EPS, OP.mult, OP.add),
                waits=[("act", m_sq), ("dve", m_m2)], inc="dve")
            m_var0_prev = m_var0
            m_var = st("vector", lambda e, M: e.tensor_tensor(
                ln_var[:], ln_var[:], ln_m2[:], OP.subtract), waits=[("dve", m_var0)], inc="dve")
            m_sd = st("scalar", lambda e, M: e.activation(
                ln_sd[:], ln_var[:], AF.Sqrt), waits=[("dve", m_var)], inc="act")
            m_rstd = st("vector", lambda e, M: e.reciprocal(
                ln_rstd[:], ln_sd[:]), waits=[("act", m_sd)], inc="dve")
            m_app = st("vector", lambda e, M, i2=i2: e.tensor_scalar(
                x1t[i2][:], xpre[i2][:], ln_mean[:, 0:1], ln_rstd[:, 0:1], OP.subtract, OP.mult),
                waits=[("dve", m_rstd), ("dg", m_x1store[i2])], inc="dve")
            m_cast = st("vector", lambda e, M, i2=i2: e.tensor_copy(
                x1tb[i2][:], x1t[i2][:]), waits=[("dve", m_app)], inc="dve")
            m_ln_prev = m_app
            st("gpsimd", lambda e, M, i2=i2, tok0=tok0: e.dma_start(
                out=s_x1f[tok0:tok0 + 128, :], in_=x1t[i2][:]), waits=[("dve", m_app)], inc="dg")
            for c8 in range(8):
                m_x1store[i2] = st("gpsimd", lambda e, M, i2=i2, tok0=tok0, c8=c8: e.dma_start(
                    out=s_x1b[c8, tok0:tok0 + 128, :],
                    in_=x1tb[i2][:, c8 * 128:(c8 + 1) * 128]),
                    waits=([("dve", m_cast)] if c8 == 0 else ()), inc="dg")

    p3_pe = sc.n["pe"]
    p3_act = sc.n["act"]
    p3_dve = sc.n["dve"]
    p3_dg = sc.n["dg"]
    R8 = range(8) if do_p4 else range(0)
    R24 = range(24) if do_p4 else range(0)

    p3w = [("pe", p3_pe), ("act", p3_act), ("dve", p3_dve), ("dg", p3_dg)]
    m_x1f = 0
    for c in R8:
        m_x1f = st("sync", lambda e, M, c=c: e.dma_start_transpose(
            X1F[c][:], s_x1b[c]), waits=p3w, inc="ds")
    m_wgt = 0
    for c in R8:
        m_wgt = st("sync", lambda e, M, c=c: e.dma_start_transpose(
            WGT[c][:], s_wg[c]), waits=p3w + [("dg", m_wgc)], inc="ds")
    m_wut = 0
    for c in R8:
        m_wut = st("sync", lambda e, M, c=c: e.dma_start_transpose(
            WUT[c][:], s_wu[c]), waits=p3w + [("dg", m_wuc)], inc="ds")
    m_wdt = 0
    for j in R24:
        m_wdt = st("sync", lambda e, M, j=j: e.dma_start_transpose(
            WDT[j][:], s_wd[j]), waits=p3w + [("dg", m_wdc)], inc="ds")

    m_si = [0] * 24
    m_h = [0] * 24
    m_dnj = [0] * 24
    m_dnev = [0] * 8
    m_x2add = [0, 0]
    m_x2store = [0, 0]
    m_tr2_prev = 0
    m_x1re_read = 0
    m_dnev_last = 0
    for qt in (range(8) if do_p4 else range(0)):
        def emit_gu(j):
            pj = (ps_b5, ps_bc, ps_tok)[j % 3]
            w0 = [("ds", m_wgt), ("ds", m_wut), ("ds", m_x1f)]
            if j >= 3:
                w0 += [("act", m_si[j - 3]), ("dve", m_h[j - 3])]
            elif qt > 0:
                w0 += [("act", m_si[21 + j]), ("dve", m_h[21 + j])]
            else:
                w0 += [("dve", p2_mult_done)]
            mm_last = None
            for ci in range(8):
                def mmg(e, M, ci=ci, j=j, qt=qt, pj=pj):
                    return e.matmul(pj[:, 0:256], WGT[ci][:, j * 128:(j + 1) * 128],
                                    X1F[ci][:, qt * 256:(qt + 1) * 256],
                                    start=(ci == 0), stop=(ci == 7))
                st("tensor", mmg, waits=(w0 if ci == 0 else ()))
            for ci in range(8):
                def mmu(e, M, ci=ci, j=j, qt=qt, pj=pj):
                    return e.matmul(pj[:, 256:512], WUT[ci][:, j * 128:(j + 1) * 128],
                                    X1F[ci][:, qt * 256:(qt + 1) * 256],
                                    start=(ci == 0), stop=(ci == 7))
                mm_last = st("tensor", mmu, inc=("pe" if ci == 7 else None))
            m_si[j] = st("scalar", lambda e, M, pj=pj: e.activation(
                s_sb[:], pj[:, 0:256], AF.Silu, scale=DEQ),
                waits=[("pe", mm_last)] + ([("dve", m_h[j - 1])] if j >= 1 else
                                           ([("dve", m_h[23])] if qt > 0 else [])), inc="act")
            m_h[j] = st("vector", lambda e, M, j=j, pj=pj: e.tensor_tensor(
                h_sb[j % 2][:], s_sb[:], pj[:, 256:512], OP.mult),
                waits=[("act", m_si[j])] + ([("pe", m_dnj[j - 2])] if j >= 2 else
                                            ([("pe", m_dnj[22 + j])] if qt > 0 else [])), inc="dve")

        def emit_dn(j):
            w_dn = [("dve", m_h[j]), ("ds", m_wdt)]
            if j == 0:
                w_dn += [("dve", m_x2add[0]), ("dve", m_x2add[1]), ("act", m_dnev_last)]
                for bank in range(4):
                    def mmz(e, M, bank=bank):
                        return e.matmul(ps_big[:, bank * 512:(bank + 1) * 512],
                                        zeros_row[:], bvrow[0:1, 0:512], start=True, stop=False)
                    st("tensor", mmz, waits=(w_dn if bank == 0 else ()))
                w_dn = []
            mm_last = None
            for outc in range(8):
                def mmdn(e, M, outc=outc, j=j):
                    return e.matmul(ps_big[:, outc * 256:(outc + 1) * 256],
                                    WDT[j][:, outc * 128:(outc + 1) * 128], h_sb[j % 2][:],
                                    start=False, stop=(j == 23))
                mm_last = st("tensor", mmdn, waits=(w_dn if outc == 0 else ()), inc=("pe" if outc == 7 else None))
            m_dnj[j] = mm_last

        for j in range(24):
            emit_gu(j)
            if j >= 1:
                emit_dn(j - 1)
        emit_dn(23)
        for outc in range(8):
            m_dnev[outc] = st("scalar", lambda e, M, outc=outc: e.activation(
                dn_sb[:, outc * 256:(outc + 1) * 256], ps_big[:, outc * 256:(outc + 1) * 256],
                AF.Identity, scale=DEQ2),
                waits=[("pe", m_dnj[23]), ("pe", m_tr2_prev)], inc="act")
        m_dnev_last = m_dnev[7]
        for qs in range(2):
            i2 = qs % 2
            tok0 = qt * 256 + qs * 128
            m_ld = st("sync", lambda e, M, tok0=tok0: e.dma_start(
                out=x1re[:], in_=s_x1f[tok0:tok0 + 128, :]),
                waits=[("dve", m_x1re_read), ("dg", p3_dg)], inc="ds")
            m_tr = None
            for outc in range(8):
                def trp2(e, M, outc=outc, qs=qs):
                    return e.transpose(ps_big[:, qs * 1024 + outc * 128: qs * 1024 + (outc + 1) * 128],
                                      dn_sb[:, outc * 256 + qs * 128: outc * 256 + (qs + 1) * 128], ident[:])
                m_tr = st("tensor", trp2,
                          waits=([("act", m_dnev[7]), ("dve", m_x2add[i2])] if outc == 0 else ()),
                          inc=("pe" if outc == 7 else None))
            m_tr2_prev = m_tr
            m_add = st("vector", lambda e, M, i2=i2, qs=qs: e.tensor_tensor(
                x2pre[i2][:], ps_big[:, qs * 1024:(qs + 1) * 1024], x1re[:], OP.add),
                waits=[("pe", m_tr), ("ds", m_ld), ("dg", m_x2store[i2]),
                       ("act", m_sq_prev), ("dve", m_ln_prev)], inc="dve")
            m_x1re_read = m_add
            m_x2add[i2] = m_add
            m_red = st("vector", lambda e, M, i2=i2: e.tensor_reduce(
                ln_sum[:], x2pre[i2][:], AX.X, OP.add), waits=[("dve", m_add), ("dve", m_ln_prev)], inc="dve")
            m_sq = st("scalar", lambda e, M, i2=i2: e.activation(
                sq2[:], x2pre[i2][:], AF.Square, accum_out=ln_ssq[:, 0:1]),
                waits=[("dve", m_add), ("dve", m_var0_prev)], inc="act")
            m_sq_prev = m_sq
            m_mean = st("vector", lambda e, M: e.tensor_scalar(
                ln_mean[:], ln_sum[:], 1.0 / D, None, OP.mult), waits=[("dve", m_red)], inc="dve")
            m_m2 = st("vector", lambda e, M: e.tensor_tensor(
                ln_m2[:], ln_mean[:], ln_mean[:], OP.mult), waits=[("dve", m_mean)], inc="dve")
            m_var0 = st("vector", lambda e, M: e.tensor_scalar(
                ln_var[:], ln_ssq[:], 1.0 / D, EPS, OP.mult, OP.add),
                waits=[("act", m_sq), ("dve", m_m2)], inc="dve")
            m_var0_prev = m_var0
            m_var = st("vector", lambda e, M: e.tensor_tensor(
                ln_var[:], ln_var[:], ln_m2[:], OP.subtract), waits=[("dve", m_var0)], inc="dve")
            m_sd = st("scalar", lambda e, M: e.activation(
                ln_sd[:], ln_var[:], AF.Sqrt), waits=[("dve", m_var)], inc="act")
            m_rstd = st("vector", lambda e, M: e.reciprocal(
                ln_rstd[:], ln_sd[:]), waits=[("act", m_sd)], inc="dve")
            m_app = st("vector", lambda e, M, i2=i2: e.tensor_scalar(
                x2pre[i2][:], x2pre[i2][:], ln_mean[:, 0:1], ln_rstd[:, 0:1], OP.subtract, OP.mult),
                waits=[("dve", m_rstd)], inc="dve")
            m_ln_prev = m_app
            m_x2store[i2] = st("gpsimd", lambda e, M, i2=i2, tok0=tok0: e.dma_start(
                out=out[tok0:tok0 + 128, :], in_=x2pre[i2][:]), waits=[("dve", m_app)], inc="dg")

    final_dg = sc.n["dg"]
    sc.q["gpsimd"].append(lambda e, SM: e.wait_ge(SM["dg"], final_dg))
    sc.meta["gpsimd"].append(((("dg", final_dg),), None, "final"))
    sc.check()

    from contextlib import ExitStack
    es = ExitStack()
    SM = {s: es.enter_context(nc.semaphore(s)) for s in ("ds", "dg", "pe", "act", "dve")}
    block = es.enter_context(nc.Block())

    @block.sync
    def _(e):
        for f in sc.q["sync"]:
            f(e, SM)

    @block.gpsimd
    def _(e):
        for f in sc.q["gpsimd"]:
            f(e, SM)

    @block.tensor
    def _(e):
        for f in sc.q["tensor"]:
            f(e, SM)

    @block.scalar
    def _(e):
        for f in sc.q["scalar"]:
            f(e, SM)

    @block.vector
    def _(e):
        for f in sc.q["vector"]:
            f(e, SM)

    es.close()
    return nc


_NC_CACHE = None


def kernel(**inputs):
    global _NC_CACHE
    X = np.asarray(inputs["L0_attn_X"], dtype=np.float32)
    args = {
        "wq": np.ascontiguousarray(inputs["L0_attn_Wq"], dtype=np.float32),
        "wk": np.ascontiguousarray(inputs["L0_attn_Wk"], dtype=np.float32),
        "wv": np.ascontiguousarray(inputs["L0_attn_Wv"], dtype=np.float32),
        "wo": np.ascontiguousarray(inputs["L0_attn_Wo"], dtype=np.float32),
        "bq": np.ascontiguousarray(inputs["L0_attn_bq"], dtype=np.float32),
        "bk": np.ascontiguousarray(inputs["L0_attn_bk"], dtype=np.float32),
        "bv": np.ascontiguousarray(inputs["L0_attn_bv"], dtype=np.float32),
        "bo": np.ascontiguousarray(inputs["L0_attn_bo"], dtype=np.float32),
        "wg": np.ascontiguousarray(inputs["L0_ffn_w_gate"], dtype=np.int32),
        "wu": np.ascontiguousarray(inputs["L0_ffn_w_up"], dtype=np.int32),
        "wd": np.ascontiguousarray(inputs["L0_ffn_w_down"], dtype=np.int32),
    }
    if _NC_CACHE is None:
        _NC_CACHE = build_nc()
    nc = _NC_CACHE
    in_maps = []
    for c in range(8):
        b, qh = c // 2, c % 2
        xb = X[b]
        if qh:
            xb = np.roll(xb, -T, axis=0)
        m = {"x_in": np.ascontiguousarray(xb)}
        m.update(args)
        in_maps.append(m)
    res = run_bass_kernel_spmd(nc, in_maps, core_ids=list(range(8)))
    outp = np.empty((B, S, D), dtype=np.float32)
    for c in range(8):
        b, qh = c // 2, c % 2
        outp[b, qh * T:(qh + 1) * T] = res.results[c]["out"]
    return outp


if __name__ == "__main__":
    rng = np.random.default_rng(0)
    ins = {
        "L0_attn_X": rng.standard_normal((B, S, D), dtype=np.float32),
        "L0_attn_Wq": rng.standard_normal((D, D), dtype=np.float32) / 32,
        "L0_attn_bq": rng.standard_normal(D, dtype=np.float32) * 0.1,
        "L0_attn_Wk": rng.standard_normal((D, D), dtype=np.float32) / 32,
        "L0_attn_bk": rng.standard_normal(D, dtype=np.float32) * 0.1,
        "L0_attn_Wv": rng.standard_normal((D, D), dtype=np.float32) / 32,
        "L0_attn_bv": rng.standard_normal(D, dtype=np.float32) * 0.1,
        "L0_attn_Wo": rng.standard_normal((D, D), dtype=np.float32) / 32,
        "L0_attn_bo": rng.standard_normal(D, dtype=np.float32) * 0.1,
        "L0_ffn_w_gate": rng.integers(-127, 128, (HID, D)).astype(np.int32),
        "L0_ffn_w_up": rng.integers(-127, 128, (HID, D)).astype(np.int32),
        "L0_ffn_w_down": rng.integers(-127, 128, (D, HID)).astype(np.int32),
    }
    y = kernel(**ins)
    print("out", y.shape, y.dtype, np.abs(y).max())


# version 14
# speedup vs baseline: 1.8324x; 1.0171x over previous
import numpy as np
import concourse.bass as bass
import concourse.mybir as mybir
from concourse.bass_utils import run_bass_kernel_spmd

F32 = mybir.dt.float32
BF16 = mybir.dt.bfloat16
I32 = mybir.dt.int32
AF = mybir.ActivationFunctionType
OP = mybir.AluOpType
AX = mybir.AxisListType

B, S, D, H, DK, HID = 4, 4096, 1024, 16, 64, 3072
T = 2048
NQT = 4
EPS = 1e-5
DEQ = 1.0 / 127.0
DEQ2 = DEQ * DEQ

ENGS = ("sync", "gpsimd", "tensor", "scalar", "vector")
DMA_SEMS = ("ds", "dg")


class Sched:

    def __init__(self, limit=None):
        self.limit = limit
        self.total = 0
        self.q = {e: [] for e in ENGS}
        self.n = {s: 0 for s in ("ds", "dg", "pe", "act", "dve")}
        self.meta = {e: [] for e in ENGS}

    def check(self):
        ptr = {e: 0 for e in ENGS}
        cnt = {s: 0 for s in self.n}
        progress = True
        while progress:
            progress = False
            for e in ENGS:
                while ptr[e] < len(self.meta[e]):
                    waits, inc, tag = self.meta[e][ptr[e]]
                    if any(cnt[s] < t for (s, t) in waits):
                        break
                    if inc:
                        cnt[inc] += 16 if inc in DMA_SEMS else 1
                    ptr[e] += 1
                    progress = True
        stuck = {e: ptr[e] for e in ENGS if ptr[e] < len(self.meta[e])}
        if stuck:
            detail = {}
            for e in stuck:
                waits, inc, tag = self.meta[e][ptr[e]]
                detail[e] = (ptr[e], tag, [(s, t, cnt[s]) for (s, t) in waits if cnt[s] < t])
            raise RuntimeError(f"DEADLOCK: stuck steps {detail}")

    def step(self, eng, emit, waits=(), inc=None, tag=""):
        if self.limit is not None and self.total >= self.limit:
            return self.n.get(inc)
        self.total += 1
        waits = tuple((s, t) for (s, t) in waits if t)
        for s, t in waits:
            assert t <= self.n[s], f"wait target {s}={t} not yet emitted ({self.n[s]})"
        tgt = None
        if inc is not None:
            self.n[inc] += 16 if inc in DMA_SEMS else 1
            tgt = self.n[inc]

        def run(e, SM, waits=waits, emit=emit, inc=inc):
            for s, t in waits:
                e.wait_ge(SM[s], t)
            r = emit(e, SM)
            if inc is not None:
                assert r is not None, "inc requires a real instruction"
                r.then_inc(SM[inc], 16 if inc in DMA_SEMS else 1)
        self.q[eng].append(run)
        self.meta[eng].append((waits, inc, tag))
        return tgt


def build_nc(n_quads=4, n_heads=4, do_p3=True, do_p4=True, limit=None):
    nc = bass.Bass("TRN2", target_bir_lowering=False)

    x_in = nc.dram_tensor("x_in", [S, D], F32, kind="ExternalInput").ap()
    wq = nc.dram_tensor("wq", [D, D], F32, kind="ExternalInput").ap()
    wk = nc.dram_tensor("wk", [D, D], F32, kind="ExternalInput").ap()
    wv = nc.dram_tensor("wv", [D, D], F32, kind="ExternalInput").ap()
    wo = nc.dram_tensor("wo", [D, D], F32, kind="ExternalInput").ap()
    bq = nc.dram_tensor("bq", [D], F32, kind="ExternalInput").ap()
    bk = nc.dram_tensor("bk", [D], F32, kind="ExternalInput").ap()
    bv = nc.dram_tensor("bv", [D], F32, kind="ExternalInput").ap()
    bo = nc.dram_tensor("bo", [D], F32, kind="ExternalInput").ap()
    wg = nc.dram_tensor("wg", [HID, D], I32, kind="ExternalInput").ap()
    wu = nc.dram_tensor("wu", [HID, D], I32, kind="ExternalInput").ap()
    wd = nc.dram_tensor("wd", [D, HID], I32, kind="ExternalInput").ap()
    out = nc.dram_tensor("out", [T, D], F32, kind="ExternalOutput").ap()

    s_xbf = nc.dram_tensor("s_xbf", [8, S, 128], BF16, kind="Internal").ap()
    s_wg = nc.dram_tensor("s_wg", [8, HID, 128], BF16, kind="Internal").ap()
    s_wu = nc.dram_tensor("s_wu", [8, HID, 128], BF16, kind="Internal").ap()
    s_wd = nc.dram_tensor("s_wd", [24, D, 128], BF16, kind="Internal").ap()
    s_x1b = nc.dram_tensor("s_x1b", [8, T, 128], BF16, kind="Internal").ap()
    s_x1f = nc.dram_tensor("s_x1f", [T, D], F32, kind="Internal").ap()

    ident_dram = nc.inline_tensor(np.eye(128, dtype=np.float32), name="ident128")

    SBASE = 16512
    A = lambda name, shape, dt, off: nc.alloc_sbuf_tensor_at(name, shape, dt, offset=SBASE + off)
    XT = [A(f"XT{c}", [128, S], BF16, 8192 * c) for c in range(8)]
    WQq = A("WQq", [128, 2048], BF16, 65536)
    WKq = A("WKq", [128, 2048], BF16, 69632)
    WVq = A("WVq", [128, 2048], BF16, 73728)
    KTq = [A(f"KTq{p}", [128, S], BF16, 77824 + 8192 * p) for p in range(2)]
    QTq = [A(f"QTq{p}", [128, T], BF16, 94208 + 4096 * p) for p in range(2)]
    Vq = [A(f"Vq{t}", [128, 256], BF16, 102400 + 512 * t) for t in range(32)]
    SCBr = [A(f"SCBr{r}", [128, 1024], BF16, 118784 + 2048 * r) for r in range(4)]
    Kf32 = A("Kf32", [128, S], F32, 126976)
    AOT = [A(f"AOT{c}", [128, T], BF16, 151552 + 4096 * c) for c in range(8)]
    av_sb = A("av_sb", [128, 512], F32, 184320)
    rv = A("rv", [65, 512], F32, 186368)
    rvh = A("rvh", [65, 512], BF16, 188416)
    rvl = A("rvl", [65, 512], BF16, 189440)
    rvd = A("rvd", [65, 512], F32, 190464)
    Wo_sb = [A(f"Wo{c}", [128, 1024], BF16, 2048 * c) for c in range(8)]
    o_sb = [A(f"osb{i}", [128, 512], F32, 16384 + 2048 * i) for i in range(8)]
    xres = [A(f"xres{i}", [128, 1024], F32, 32768 + 4096 * i) for i in range(2)]
    xpre = [A(f"xpre{i}", [128, 1024], F32, 40960 + 4096 * i) for i in range(2)]
    sq1 = A("sq1", [128, 1024], F32, 49152)
    x1t = [A(f"x1t{i}", [128, 1024], F32, 53248 + 4096 * i) for i in range(2)]
    x1tb = [A(f"x1tb{i}", [128, 1024], BF16, 61440 + 2048 * i) for i in range(2)]
    X1F = [A(f"X1F{c}", [128, T], BF16, 49152 + 4096 * c) for c in range(8)]
    WGT = [A(f"WGTx{c}", [128, HID], BF16, 81920 + 6144 * c) for c in range(8)]
    WUT = [A(f"WUTx{c}", [128, HID], BF16, 131072 + 6144 * c) for c in range(8)]
    WDT = [A(f"WDTx{j}", [128, 1024], BF16, 0 + 2048 * j) for j in range(24)]
    s_sb = A("s_sb", [128, 256], BF16, 180224)
    h_sb = [A(f"hsb{i}", [128, 256], BF16, 180736 + 512 * i) for i in range(2)]
    dn_sb = A("dn_sb", [128, 2048], F32, 181760)
    x2pre = [A(f"x2pre{i}", [128, 1024], F32, 189952 + 4096 * i) for i in range(2)]
    sq2 = A("sq2", [128, 1024], F32, 198144)
    x1re = A("x1re", [128, 1024], F32, 202240)
    P = lambda name, shape, dt: nc.alloc_sbuf_tensor(name, shape, dt, side="right")
    ident = P("ident", [128, 128], F32)
    ones_col = P("ones_col", [128, 1], BF16)
    ones_bc = P("ones_bc", [65, 64], BF16)
    ones_row = P("ones_row", [1, 128], BF16)
    zeros_row = P("zeros_row", [1, 128], BF16)
    bvrow = P("bvrow", [1, 1024], BF16)
    bq_sb = P("bq_sb", [128, 8], F32)
    bk_sb = P("bk_sb", [128, 8], F32)
    bo_sb = P("bo_sb", [128, 8], F32)
    ln_sum = P("ln_sum", [128, 1], F32)
    ln_ssq = P("ln_ssq", [128, 1], F32)
    ln_mean = P("ln_mean", [128, 1], F32)
    ln_var = P("ln_var", [128, 1], F32)
    ln_m2 = P("ln_m2", [128, 1], F32)
    ln_sd = P("ln_sd", [128, 1], F32)
    ln_rstd = P("ln_rstd", [128, 1], F32)
    kmean = P("kmean", [128, 1], F32)

    ps_big = nc.alloc_psum_tensor("ps_big", [128, 2048], F32)
    ps_b5 = nc.alloc_psum_tensor("ps_b5", [128, 512], F32)
    ps_bc = nc.alloc_psum_tensor("ps_bc", [128, 512], F32)
    ps_tok = nc.alloc_psum_tensor("ps_tok", [128, 1024], F32)

    sc = Sched(limit)
    st = sc.step

    m_xc = []
    for rq in range(4):
        m = 0
        for c8 in range(8):
            m = st("gpsimd", lambda e, M, rq=rq, c8=c8: e.dma_start(
                out=s_xbf[c8, rq * 1024:(rq + 1) * 1024, :],
                in_=x_in[rq * 1024:(rq + 1) * 1024, c8 * 128:(c8 + 1) * 128]), inc="dg")
        m_xc.append(m)
    m_bv = st("gpsimd", lambda e, M: e.dma_start(out=bvrow[0:1, :], in_=bv[None, :]), inc="dg")
    st("sync", lambda e, M: e.dma_start(out=ident[:], in_=ident_dram.ap()), inc="ds")
    for c in range(8):
        st("sync", lambda e, M, c=c: e.dma_start(out=bq_sb[:, c], in_=bq[c * 128:(c + 1) * 128]), inc="ds")
        st("sync", lambda e, M, c=c: e.dma_start(out=bk_sb[:, c], in_=bk[c * 128:(c + 1) * 128]), inc="ds")
        st("sync", lambda e, M, c=c: e.dma_start(out=bo_sb[:, c], in_=bo[c * 128:(c + 1) * 128]), inc="ds")
    m_bias = sc.n["ds"]
    st("vector", lambda e, M: e.memset(ones_col[:], 1.0), inc="dve")
    st("vector", lambda e, M: e.memset(ones_bc[:], 1.0), inc="dve")
    st("vector", lambda e, M: e.memset(ones_row[:], 1.0), inc="dve")
    m_ones = st("vector", lambda e, M: e.memset(zeros_row[:], 0.0), inc="dve")
    m_xt = 0
    m_xtp = [[0, 0] for _ in range(8)]
    for half in range(2):
        for c in range(8):
            m_xt = st("sync", lambda e, M, c=c, half=half: e.dma_start_transpose(
                XT[c][:, half * 2048:(half + 1) * 2048],
                s_xbf[c, half * 2048:(half + 1) * 2048, :]),
                waits=[("dg", m_xc[2 * half + 1])], inc="ds")
            m_xtp[c][half] = m_xt

    quarter_free = [0, 0, 0, 0]
    m_exp = [0] * 8
    m_b5_free = []
    m_bc_read = 0
    m_bc_done = 0
    m_avsb = 0
    m_mult = 0
    m_av_done = 0
    m_kcent_all = 0
    m_vq = [0] * 32
    m_wslice_read = 0
    m_rvchain = 0
    m_avmm_prev = {}
    m_exp_prev = {}
    m_norm_prev = 0

    m_wgc = m_wuc = m_wdc = 0
    for g in range(n_quads):
        if g == 1:
            for c8 in range(8):
                m_wgc = st("gpsimd", lambda e, M, c8=c8: e.dma_start(
                    out=s_wg[c8], in_=wg[:, c8 * 128:(c8 + 1) * 128]), inc="dg")
            for c8 in range(8):
                m_wuc = st("gpsimd", lambda e, M, c8=c8: e.dma_start(
                    out=s_wu[c8], in_=wu[:, c8 * 128:(c8 + 1) * 128]), inc="dg")
            for j24 in range(24):
                m_wdc = st("gpsimd", lambda e, M, j24=j24: e.dma_start(
                    out=s_wd[j24], in_=wd[:, j24 * 128:(j24 + 1) * 128]), inc="dg")
        cols = slice(g * 256, (g + 1) * 256)
        m_wqs = st("gpsimd", lambda e, M, v=wq.rearrange("(c p) n -> p c n", p=128)[:, :, cols]:
                   e.dma_start(out=WQq.ap().rearrange("p (c n) -> p c n", c=8), in_=v),
                   waits=[("pe", m_wslice_read)], inc="dg")
        m_wks = st("gpsimd", lambda e, M, v=wk.rearrange("(c p) n -> p c n", p=128)[:, :, cols]:
                   e.dma_start(out=WKq.ap().rearrange("p (c n) -> p c n", c=8), in_=v),
                   waits=[("pe", m_wslice_read)], inc="dg")
        m_wvs = st("gpsimd", lambda e, M, v=wv.rearrange("(c p) n -> p c n", p=128)[:, :, cols]:
                   e.dma_start(out=WVq.ap().rearrange("p (c n) -> p c n", c=8), in_=v),
                   waits=[("pe", m_wslice_read)], inc="dg")

        m_kcent = [0, 0]
        for pc in range(2):
            m_kevac = [0] * 8
            for tcol in range(8):
                qb = tcol % 4
                w0 = [("ds", m_xt), ("dg", m_wks), ("act", quarter_free[qb]),
                      ("pe", m_av_done), ("dve", m_mult)]
                mm_last = None
                for ci in range(8):
                    def mmk(e, M, ci=ci, pc=pc, tcol=tcol, qb=qb):
                        return e.matmul(ps_big[:, qb * 512:qb * 512 + 512],
                                        WKq[:, ci * 256 + pc * 128: ci * 256 + pc * 128 + 128],
                                        XT[ci][:, tcol * 512:(tcol + 1) * 512],
                                        start=(ci == 0), stop=(ci == 7))
                    mm_last = st("tensor", mmk, waits=(w0 if ci == 0 else ()), inc=("pe" if ci == 7 else None))
                def kev(e, M, pc=pc, tcol=tcol, qb=qb, g=g):
                    return e.activation(Kf32[:, tcol * 512:(tcol + 1) * 512], ps_big[:, qb * 512:qb * 512 + 512],
                                        AF.Identity, bias=bk_sb[:, 2 * g + pc:2 * g + pc + 1])
                m_kevac[tcol] = st("scalar", kev,
                                   waits=[("pe", mm_last), ("ds", m_bias),
                                          ("dve", m_kcent[0] if pc == 1 else m_kcent_all)],
                                   inc="act")
                quarter_free[qb] = m_kevac[tcol]
            m_red = st("vector", lambda e, M: e.tensor_reduce(kmean[:], Kf32[:], AX.X, OP.add),
                       waits=[("act", m_kevac[7]), ("dve", m_kcent_all)], inc="dve")
            m_scl = st("vector", lambda e, M: e.tensor_scalar(kmean[:], kmean[:], 1.0 / S, None, OP.mult),
                       waits=[("dve", m_red)], inc="dve")
            m_kcent[pc] = st("vector", lambda e, M, pc=pc: e.tensor_scalar(
                KTq[pc][:], Kf32[:], kmean[:, 0:1], None, OP.subtract),
                waits=[("dve", m_scl), ("pe", m_av_done)], inc="dve")
        m_kcent_all = m_kcent[1]

        m_qevac = [[0] * NQT, [0] * NQT]
        for pc in range(2):
            for qt in range(NQT):
                qb = qt % 4
                w0 = [("act", quarter_free[qb]), ("dg", m_wqs)]
                mm_last = None
                for ci in range(8):
                    def mmq(e, M, ci=ci, pc=pc, qt=qt, qb=qb):
                        return e.matmul(ps_big[:, qb * 512:qb * 512 + 512],
                                        WQq[:, ci * 256 + pc * 128: ci * 256 + pc * 128 + 128],
                                        XT[ci][:, qt * 512:(qt + 1) * 512],
                                        start=(ci == 0), stop=(ci == 7))
                    mm_last = st("tensor", mmq, waits=(w0 if ci == 0 else ()), inc=("pe" if ci == 7 else None))
                def qev(e, M, pc=pc, qt=qt, qb=qb, g=g):
                    return e.activation(QTq[pc][:, qt * 512:(qt + 1) * 512], ps_big[:, qb * 512:qb * 512 + 512],
                                        AF.Identity, bias=bq_sb[:, 2 * g + pc:2 * g + pc + 1])
                m_qevac[pc][qt] = st("scalar", qev, waits=[("pe", mm_last)], inc="act")
                quarter_free[qb] = m_qevac[pc][qt]

        for tt in range(32):
            pjv = ps_b5 if tt % 2 == 0 else ps_bc
            w0 = [("dg", m_wvs), ("dve", m_ones), ("dg", m_bv)]
            if tt >= 2:
                w0.append(("dve", m_vq[tt - 2]))
            else:
                w0 += m_b5_free
            mm_last = None
            for ci in range(8):
                def mmv(e, M, ci=ci, tt=tt, pjv=pjv):
                    return e.matmul(pjv[:, 0:256],
                                    XT[ci][:, tt * 128:(tt + 1) * 128],
                                    WVq[:, ci * 256:(ci + 1) * 256],
                                    start=(ci == 0), stop=False)
                st("tensor", mmv, waits=(w0 if ci == 0 else ()))
            m_vmm = st("tensor", lambda e, M, pjv=pjv, g=g: e.matmul(
                pjv[:, 0:256], ones_row[:], bvrow[0:1, g * 256:(g + 1) * 256],
                start=False, stop=True), inc="pe")
            m_vq[tt] = st("vector", lambda e, M, tt=tt, pjv=pjv: e.tensor_copy(
                Vq[tt][:], pjv[:, 0:256]), waits=[("pe", m_vmm)], inc="dve")
        m_wslice_read = sc.n["pe"]
        m_b5_free = [("dve", m_vq[30]), ("dve", m_vq[31])]

        for pc in range(2):
            ch = 2 * g + pc
            for qt in range(NQT):
                m_avmm = dict(m_avmm_prev)
                m_e = {c - 32: v for c, v in m_exp_prev.items()}

                def emit_scores(c):
                    i2 = c % 2
                    r4 = c % 4
                    w0 = [("dve", m_kcent[pc]), ("act", m_qevac[pc][qt]),
                          ("act", quarter_free[2 * i2]), ("act", quarter_free[2 * i2 + 1])]
                    def mmse(e, M, c=c, i2=i2, pc=pc, qt=qt):
                        return e.matmul(ps_big[:, i2 * 1024: i2 * 1024 + 512],
                                        KTq[pc][0:64, c * 128:(c + 1) * 128],
                                        QTq[pc][0:64, qt * 512:(qt + 1) * 512],
                                        start=True, stop=True, tile_position=(0, 0))
                    st("tensor", mmse, waits=w0)
                    def mmso(e, M, c=c, i2=i2, pc=pc, qt=qt):
                        return e.matmul(ps_big[:, i2 * 1024 + 512: i2 * 1024 + 1024],
                                        KTq[pc][64:128, c * 128:(c + 1) * 128],
                                        QTq[pc][64:128, qt * 512:(qt + 1) * 512],
                                        start=True, stop=True, tile_position=(64, 0))
                    m_smm = st("tensor", mmso, inc="pe")
                    m_e[c] = st("scalar", lambda e, M, c=c, i2=i2, r4=r4: e.activation(
                        SCBr[r4][:], ps_big[:, i2 * 1024:(i2 + 1) * 1024], AF.Exp, scale=0.125),
                        waits=[("pe", m_smm), ("pe", m_avmm.get(c - 4, 0))], inc="act")
                    quarter_free[2 * i2] = m_e[c]
                    quarter_free[2 * i2 + 1] = m_e[c]

                def emit_av(c):
                    r4 = c % 4
                    strt, stop = (c == 0), (c == 31)
                    wav = [("act", m_e[c])]
                    if strt:
                        wav += [("dve", m_norm_prev), ("dve", m_vq[31])]
                    def av_e(e, M, c=c, r4=r4, pc=pc, strt=strt, stop=stop):
                        return e.matmul(ps_b5[0:64, :], Vq[c][:, (2 * pc) * 64:(2 * pc) * 64 + 64],
                                        SCBr[r4][:, 0:512], start=strt, stop=stop, tile_position=(0, 0))
                    st("tensor", av_e, waits=wav)
                    def den_e(e, M, c=c, r4=r4, strt=strt, stop=stop):
                        return e.matmul(ps_b5[64:65, :], ones_col[:], SCBr[r4][:, 0:512],
                                        start=strt, stop=stop, tile_position=(0, 64))
                    st("tensor", den_e)
                    def av_o(e, M, c=c, r4=r4, pc=pc, strt=strt, stop=stop):
                        return e.matmul(ps_bc[64:128, :], Vq[c][:, (2 * pc + 1) * 64:(2 * pc + 1) * 64 + 64],
                                        SCBr[r4][:, 512:1024], start=strt, stop=stop, tile_position=(0, 64))
                    st("tensor", av_o)
                    def den_o(e, M, c=c, r4=r4, strt=strt, stop=stop):
                        return e.matmul(ps_bc[32:33, :], ones_col[:], SCBr[r4][:, 512:1024],
                                        start=strt, stop=stop, tile_position=(0, 32))
                    m_avmm[c] = st("tensor", den_o, inc="pe")

                for c in range(32):
                    emit_scores(c)
                    if c >= 2:
                        emit_av(c - 2)
                emit_av(30)
                emit_av(31)
                m_avmm_prev = {c - 32: m_avmm[c] for c in range(28, 32)}
                m_exp_prev = {c: m_e[c] for c in range(30, 32)}
                m_av = m_avmm[31]
                m_r1 = st("vector", lambda e, M: e.reciprocal(rv[64:65, :], ps_b5[64:65, :]),
                          waits=[("pe", m_av), ("pe", m_bc_done)], inc="dve")
                m_r1b = st("vector", lambda e, M: e.reciprocal(rv[32:33, :], ps_bc[32:33, :]),
                           waits=[("dve", m_r1)], inc="dve")
                m_ae = st("vector", lambda e, M: e.tensor_copy(av_sb[0:64, :], ps_b5[0:64, :]),
                          waits=[("dve", m_r1b), ("dve", m_mult)], inc="dve")
                m_ao = st("vector", lambda e, M: e.tensor_copy(av_sb[64:128, :], ps_bc[64:128, :]),
                          waits=[("dve", m_ae)], inc="dve")
                m_r2 = st("vector", lambda e, M: e.tensor_copy(rvh[64:65, :], rv[64:65, :]),
                          waits=[("dve", m_ao), ("pe", m_bc_done)], inc="dve")
                m_r2b = st("vector", lambda e, M: e.tensor_copy(rvh[32:33, :], rv[32:33, :]),
                           waits=[("dve", m_r2)], inc="dve")
                m_r4a = st("vector", lambda e, M: e.tensor_tensor(rvl[64:65, :], rv[64:65, :], rvh[64:65, :], OP.subtract),
                           waits=[("dve", m_r2b)], inc="dve")
                m_r4_ = st("vector", lambda e, M: e.tensor_tensor(rvl[32:33, :], rv[32:33, :], rvh[32:33, :], OP.subtract),
                           waits=[("dve", m_r4a)], inc="dve")
                st("tensor", lambda e, M: e.matmul(ps_tok[0:64, 0:512], ones_bc[64:65, 0:64], rvh[64:65, :],
                                                   start=True, stop=False, tile_position=(64, 0)),
                   waits=[("dve", m_r4_), ("dve", m_mult)])
                st("tensor", lambda e, M: e.matmul(ps_tok[0:64, 0:512], ones_bc[64:65, 0:64], rvl[64:65, :],
                                                   start=False, stop=True, tile_position=(64, 0)))
                st("tensor", lambda e, M: e.matmul(ps_tok[64:128, 512:1024], ones_bc[32:33, 0:64], rvh[32:33, :],
                                                   start=True, stop=False, tile_position=(32, 64)))
                m_bc_done = st("tensor", lambda e, M: e.matmul(ps_tok[64:128, 512:1024], ones_bc[32:33, 0:64], rvl[32:33, :],
                                                               start=False, stop=True, tile_position=(32, 64)), inc="pe")
                m_me = st("vector", lambda e, M, ch=ch, qt=qt: e.tensor_tensor(
                    AOT[ch][0:64, qt * 512:(qt + 1) * 512], av_sb[0:64, :], ps_tok[0:64, 0:512], OP.mult),
                    waits=[("pe", m_bc_done), ("dve", m_ao)], inc="dve")
                m_mult = st("vector", lambda e, M, ch=ch, qt=qt: e.tensor_tensor(
                    AOT[ch][64:128, qt * 512:(qt + 1) * 512], av_sb[64:128, :], ps_tok[64:128, 512:1024], OP.mult),
                    waits=[("dve", m_me)], inc="dve")
                m_norm_prev = m_ao
                m_av_done = m_av

    p2_pe_done = sc.n["pe"]
    p2_mult_done = m_mult
    p2_act_done = sc.n["act"]
    p2w_pre = [("pe", p2_pe_done), ("act", p2_act_done), ("dve", p2_mult_done)]
    m_wgt = 0
    for c in range(8):
        m_wgt = st("sync", lambda e, M, c=c: e.dma_start_transpose(
            WGT[c][:], s_wg[c]), waits=p2w_pre + [("dg", m_wgc)], inc="ds")
    m_wut = 0
    for c in range(3):
        m_wut = st("sync", lambda e, M, c=c: e.dma_start_transpose(
            WUT[c][:], s_wu[c]), waits=p2w_pre + [("dg", m_wuc)], inc="ds")

    if not do_p3:
        do_p4 = False
    m_wo = 0
    for c in (range(8) if do_p3 else range(0)):
        m_wo = st("gpsimd", lambda e, M, c=c: e.dma_start(
            out=Wo_sb[c].ap(), in_=wo.rearrange("(c p) n -> p c n", p=128)[:, c, :]),
            waits=[("pe", p2_pe_done)], inc="dg")
    m_oev = [0] * 8
    m_tr_prev = 0
    m_xadd = [0, 0]
    m_sq_prev = 0
    m_ln_prev = 0
    m_var0_prev = 0
    m_x1store = [0, 0]
    m_xres_ld = [0, 0]
    m_ps_tok_free = 0
    for qt in (range(4) if do_p3 else range(0)):
        for outc in range(8):
            ps_o = ps_b5 if outc % 2 == 0 else ps_bc
            w0 = [("dg", m_wo), ("dve", p2_mult_done), ("act", m_oev[outc - 2] if outc >= 2 else 0),
                  ("pe", m_tr_prev)]
            if qt == 0 and outc < 2:
                w0 += [("dve", p2_mult_done)]
            mm_last = None
            for ci in range(8):
                def mmo(e, M, ci=ci, outc=outc, qt=qt, ps_o=ps_o):
                    return e.matmul(ps_o[:], Wo_sb[ci][:, outc * 128:(outc + 1) * 128],
                                    AOT[ci][:, qt * 512:(qt + 1) * 512],
                                    start=(ci == 0), stop=(ci == 7))
                mm_last = st("tensor", mmo, waits=(w0 if ci == 0 else ()), inc=("pe" if ci == 7 else None))
            m_oev[outc] = st("scalar", lambda e, M, outc=outc, ps_o=ps_o: e.activation(
                o_sb[outc][:], ps_o[:], AF.Identity, bias=bo_sb[:, outc:outc + 1]),
                waits=[("pe", mm_last), ("pe", m_tr_prev)], inc="act")
        if qt == 3:
            for c in range(3, 8):
                m_wut = st("sync", lambda e, M, c=c: e.dma_start_transpose(
                    WUT[c][:], s_wu[c]), waits=[("pe", sc.n["pe"]), ("dg", m_wuc)], inc="ds")
        for qs in range(4):
            i2 = qs % 2
            tok0 = qt * 512 + qs * 128
            m_xres_ld[i2] = st("sync", lambda e, M, i2=i2, tok0=tok0: e.dma_start(
                out=xres[i2][:], in_=x_in[tok0:tok0 + 128, :]),
                waits=[("dve", m_xadd[i2]), ("pe", p2_pe_done)], inc="ds")
            m_tr = None
            for outc in range(8):
                def trp(e, M, outc=outc, qs=qs):
                    return e.transpose(ps_tok[:, outc * 128:(outc + 1) * 128],
                                      o_sb[outc][:, qs * 128:(qs + 1) * 128], ident[:])
                m_tr = st("tensor", trp,
                          waits=([("act", m_oev[7]), ("dve", m_ps_tok_free), ("dve", p2_mult_done)] if outc == 0 else ()),
                          inc=("pe" if outc == 7 else None))
            m_tr_prev = m_tr
            m_add = st("vector", lambda e, M, i2=i2: e.tensor_tensor(
                xpre[i2][:], ps_tok[:], xres[i2][:], OP.add),
                waits=[("pe", m_tr), ("ds", m_xres_ld[i2]), ("act", m_sq_prev), ("dve", m_ln_prev)], inc="dve")
            m_ps_tok_free = m_add
            m_xadd[i2] = m_add
            m_red = st("vector", lambda e, M, i2=i2: e.tensor_reduce(
                ln_sum[:], xpre[i2][:], AX.X, OP.add), waits=[("dve", m_add), ("dve", m_ln_prev)], inc="dve")
            m_sq = st("scalar", lambda e, M, i2=i2: e.activation(
                sq1[:], xpre[i2][:], AF.Square, accum_out=ln_ssq[:, 0:1]),
                waits=[("dve", m_add), ("dve", m_var0_prev)], inc="act")
            m_sq_prev = m_sq
            m_mean = st("vector", lambda e, M: e.tensor_scalar(
                ln_mean[:], ln_sum[:], 1.0 / D, None, OP.mult), waits=[("dve", m_red)], inc="dve")
            m_m2 = st("vector", lambda e, M: e.tensor_tensor(
                ln_m2[:], ln_mean[:], ln_mean[:], OP.mult), waits=[("dve", m_mean)], inc="dve")
            m_var0 = st("vector", lambda e, M: e.tensor_scalar(
                ln_var[:], ln_ssq[:], 1.0 / D, EPS, OP.mult, OP.add),
                waits=[("act", m_sq), ("dve", m_m2)], inc="dve")
            m_var0_prev = m_var0
            m_var = st("vector", lambda e, M: e.tensor_tensor(
                ln_var[:], ln_var[:], ln_m2[:], OP.subtract), waits=[("dve", m_var0)], inc="dve")
            m_sd = st("scalar", lambda e, M: e.activation(
                ln_sd[:], ln_var[:], AF.Sqrt), waits=[("dve", m_var)], inc="act")
            m_rstd = st("vector", lambda e, M: e.reciprocal(
                ln_rstd[:], ln_sd[:]), waits=[("act", m_sd)], inc="dve")
            m_app = st("vector", lambda e, M, i2=i2: e.tensor_scalar(
                x1t[i2][:], xpre[i2][:], ln_mean[:, 0:1], ln_rstd[:, 0:1], OP.subtract, OP.mult),
                waits=[("dve", m_rstd), ("dg", m_x1store[i2])], inc="dve")
            m_cast = st("vector", lambda e, M, i2=i2: e.tensor_copy(
                x1tb[i2][:], x1t[i2][:]), waits=[("dve", m_app)], inc="dve")
            m_ln_prev = m_app
            st("gpsimd", lambda e, M, i2=i2, tok0=tok0: e.dma_start(
                out=s_x1f[tok0:tok0 + 128, :], in_=x1t[i2][:]), waits=[("dve", m_app)], inc="dg")
            for c8 in range(8):
                m_x1store[i2] = st("gpsimd", lambda e, M, i2=i2, tok0=tok0, c8=c8: e.dma_start(
                    out=s_x1b[c8, tok0:tok0 + 128, :],
                    in_=x1tb[i2][:, c8 * 128:(c8 + 1) * 128]),
                    waits=([("dve", m_cast)] if c8 == 0 else ()), inc="dg")

    p3_pe = sc.n["pe"]
    p3_act = sc.n["act"]
    p3_dve = sc.n["dve"]
    p3_dg = sc.n["dg"]
    R8 = range(8) if do_p4 else range(0)
    R24 = range(24) if do_p4 else range(0)

    p3w = [("pe", p3_pe), ("act", p3_act), ("dve", p3_dve), ("dg", p3_dg)]
    m_x1f = 0
    for c in R8:
        m_x1f = st("sync", lambda e, M, c=c: e.dma_start_transpose(
            X1F[c][:], s_x1b[c]), waits=[("dg", p3_dg), ("pe", p3_pe)], inc="ds")
    m_wdt = 0
    m_wdtj = []
    for j in R24:
        m_wdt = st("sync", lambda e, M, j=j: e.dma_start_transpose(
            WDT[j][:], s_wd[j]), waits=p3w + [("dg", m_wdc)], inc="ds")
        m_wdtj.append(m_wdt)

    m_si = [0] * 24
    m_h = [0] * 24
    m_dnj = [0] * 24
    m_dnev = [0] * 8
    m_x2add = [0, 0]
    m_x2store = [0, 0]
    m_tr2_prev = 0
    m_x1re_read = 0
    m_dnev_last = 0
    for qt in (range(8) if do_p4 else range(0)):
        def emit_gu(j):
            pj = (ps_b5, ps_bc, ps_tok)[j % 3]
            w0 = [("ds", m_wgt), ("ds", m_wut), ("ds", m_x1f)]
            if j >= 3:
                w0 += [("act", m_si[j - 3]), ("dve", m_h[j - 3])]
            elif qt > 0:
                w0 += [("act", m_si[21 + j]), ("dve", m_h[21 + j])]
            else:
                w0 += [("dve", p2_mult_done)]
            mm_last = None
            for ci in range(8):
                def mmg(e, M, ci=ci, j=j, qt=qt, pj=pj):
                    return e.matmul(pj[:, 0:256], WGT[ci][:, j * 128:(j + 1) * 128],
                                    X1F[ci][:, qt * 256:(qt + 1) * 256],
                                    start=(ci == 0), stop=(ci == 7))
                st("tensor", mmg, waits=(w0 if ci == 0 else ()))
            for ci in range(8):
                def mmu(e, M, ci=ci, j=j, qt=qt, pj=pj):
                    return e.matmul(pj[:, 256:512], WUT[ci][:, j * 128:(j + 1) * 128],
                                    X1F[ci][:, qt * 256:(qt + 1) * 256],
                                    start=(ci == 0), stop=(ci == 7))
                mm_last = st("tensor", mmu, inc=("pe" if ci == 7 else None))
            m_si[j] = st("scalar", lambda e, M, pj=pj: e.activation(
                s_sb[:], pj[:, 0:256], AF.Silu, scale=DEQ),
                waits=[("pe", mm_last)] + ([("dve", m_h[j - 1])] if j >= 1 else
                                           ([("dve", m_h[23])] if qt > 0 else [])), inc="act")
            m_h[j] = st("vector", lambda e, M, j=j, pj=pj: e.tensor_tensor(
                h_sb[j % 2][:], s_sb[:], pj[:, 256:512], OP.mult),
                waits=[("act", m_si[j])] + ([("pe", m_dnj[j - 2])] if j >= 2 else
                                            ([("pe", m_dnj[22 + j])] if qt > 0 else [])), inc="dve")

        def emit_dn(j):
            w_dn = [("dve", m_h[j]), ("ds", m_wdtj[j])]
            if j == 0:
                w_dn += [("dve", m_x2add[0]), ("dve", m_x2add[1]), ("act", m_dnev_last)]
                for bank in range(4):
                    def mmz(e, M, bank=bank):
                        return e.matmul(ps_big[:, bank * 512:(bank + 1) * 512],
                                        zeros_row[:], bvrow[0:1, 0:512], start=True, stop=False)
                    st("tensor", mmz, waits=(w_dn if bank == 0 else ()))
                w_dn = []
            mm_last = None
            for outc in range(8):
                def mmdn(e, M, outc=outc, j=j):
                    return e.matmul(ps_big[:, outc * 256:(outc + 1) * 256],
                                    WDT[j][:, outc * 128:(outc + 1) * 128], h_sb[j % 2][:],
                                    start=False, stop=(j == 23))
                mm_last = st("tensor", mmdn, waits=(w_dn if outc == 0 else ()), inc=("pe" if outc == 7 else None))
            m_dnj[j] = mm_last

        for j in range(24):
            emit_gu(j)
            if j >= 1:
                emit_dn(j - 1)
        emit_dn(23)
        for outc in range(8):
            m_dnev[outc] = st("scalar", lambda e, M, outc=outc: e.activation(
                dn_sb[:, outc * 256:(outc + 1) * 256], ps_big[:, outc * 256:(outc + 1) * 256],
                AF.Identity, scale=DEQ2),
                waits=[("pe", m_dnj[23]), ("pe", m_tr2_prev)], inc="act")
        m_dnev_last = m_dnev[7]
        for qs in range(2):
            i2 = qs % 2
            tok0 = qt * 256 + qs * 128
            m_ld = st("sync", lambda e, M, tok0=tok0: e.dma_start(
                out=x1re[:], in_=s_x1f[tok0:tok0 + 128, :]),
                waits=[("dve", m_x1re_read), ("dg", p3_dg)], inc="ds")
            m_tr = None
            for outc in range(8):
                def trp2(e, M, outc=outc, qs=qs):
                    return e.transpose(ps_big[:, qs * 1024 + outc * 128: qs * 1024 + (outc + 1) * 128],
                                      dn_sb[:, outc * 256 + qs * 128: outc * 256 + (qs + 1) * 128], ident[:])
                m_tr = st("tensor", trp2,
                          waits=([("act", m_dnev[7]), ("dve", m_x2add[i2])] if outc == 0 else ()),
                          inc=("pe" if outc == 7 else None))
            m_tr2_prev = m_tr
            m_add = st("vector", lambda e, M, i2=i2, qs=qs: e.tensor_tensor(
                x2pre[i2][:], ps_big[:, qs * 1024:(qs + 1) * 1024], x1re[:], OP.add),
                waits=[("pe", m_tr), ("ds", m_ld), ("dg", m_x2store[i2]),
                       ("act", m_sq_prev), ("dve", m_ln_prev)], inc="dve")
            m_x1re_read = m_add
            m_x2add[i2] = m_add
            m_red = st("vector", lambda e, M, i2=i2: e.tensor_reduce(
                ln_sum[:], x2pre[i2][:], AX.X, OP.add), waits=[("dve", m_add), ("dve", m_ln_prev)], inc="dve")
            m_sq = st("scalar", lambda e, M, i2=i2: e.activation(
                sq2[:], x2pre[i2][:], AF.Square, accum_out=ln_ssq[:, 0:1]),
                waits=[("dve", m_add), ("dve", m_var0_prev)], inc="act")
            m_sq_prev = m_sq
            m_mean = st("vector", lambda e, M: e.tensor_scalar(
                ln_mean[:], ln_sum[:], 1.0 / D, None, OP.mult), waits=[("dve", m_red)], inc="dve")
            m_m2 = st("vector", lambda e, M: e.tensor_tensor(
                ln_m2[:], ln_mean[:], ln_mean[:], OP.mult), waits=[("dve", m_mean)], inc="dve")
            m_var0 = st("vector", lambda e, M: e.tensor_scalar(
                ln_var[:], ln_ssq[:], 1.0 / D, EPS, OP.mult, OP.add),
                waits=[("act", m_sq), ("dve", m_m2)], inc="dve")
            m_var0_prev = m_var0
            m_var = st("vector", lambda e, M: e.tensor_tensor(
                ln_var[:], ln_var[:], ln_m2[:], OP.subtract), waits=[("dve", m_var0)], inc="dve")
            m_sd = st("scalar", lambda e, M: e.activation(
                ln_sd[:], ln_var[:], AF.Sqrt), waits=[("dve", m_var)], inc="act")
            m_rstd = st("vector", lambda e, M: e.reciprocal(
                ln_rstd[:], ln_sd[:]), waits=[("act", m_sd)], inc="dve")
            m_app = st("vector", lambda e, M, i2=i2: e.tensor_scalar(
                x2pre[i2][:], x2pre[i2][:], ln_mean[:, 0:1], ln_rstd[:, 0:1], OP.subtract, OP.mult),
                waits=[("dve", m_rstd)], inc="dve")
            m_ln_prev = m_app
            m_x2store[i2] = st("gpsimd", lambda e, M, i2=i2, tok0=tok0: e.dma_start(
                out=out[tok0:tok0 + 128, :], in_=x2pre[i2][:]), waits=[("dve", m_app)], inc="dg")

    final_dg = sc.n["dg"]
    sc.q["gpsimd"].append(lambda e, SM: e.wait_ge(SM["dg"], final_dg))
    sc.meta["gpsimd"].append(((("dg", final_dg),), None, "final"))
    sc.check()

    from contextlib import ExitStack
    es = ExitStack()
    SM = {s: es.enter_context(nc.semaphore(s)) for s in ("ds", "dg", "pe", "act", "dve")}
    block = es.enter_context(nc.Block())

    @block.sync
    def _(e):
        for f in sc.q["sync"]:
            f(e, SM)

    @block.gpsimd
    def _(e):
        for f in sc.q["gpsimd"]:
            f(e, SM)

    @block.tensor
    def _(e):
        for f in sc.q["tensor"]:
            f(e, SM)

    @block.scalar
    def _(e):
        for f in sc.q["scalar"]:
            f(e, SM)

    @block.vector
    def _(e):
        for f in sc.q["vector"]:
            f(e, SM)

    es.close()
    return nc


_NC_CACHE = None


def kernel(**inputs):
    global _NC_CACHE
    X = np.asarray(inputs["L0_attn_X"], dtype=np.float32)
    args = {
        "wq": np.ascontiguousarray(inputs["L0_attn_Wq"], dtype=np.float32),
        "wk": np.ascontiguousarray(inputs["L0_attn_Wk"], dtype=np.float32),
        "wv": np.ascontiguousarray(inputs["L0_attn_Wv"], dtype=np.float32),
        "wo": np.ascontiguousarray(inputs["L0_attn_Wo"], dtype=np.float32),
        "bq": np.ascontiguousarray(inputs["L0_attn_bq"], dtype=np.float32),
        "bk": np.ascontiguousarray(inputs["L0_attn_bk"], dtype=np.float32),
        "bv": np.ascontiguousarray(inputs["L0_attn_bv"], dtype=np.float32),
        "bo": np.ascontiguousarray(inputs["L0_attn_bo"], dtype=np.float32),
        "wg": np.ascontiguousarray(inputs["L0_ffn_w_gate"], dtype=np.int32),
        "wu": np.ascontiguousarray(inputs["L0_ffn_w_up"], dtype=np.int32),
        "wd": np.ascontiguousarray(inputs["L0_ffn_w_down"], dtype=np.int32),
    }
    if _NC_CACHE is None:
        _NC_CACHE = build_nc()
    nc = _NC_CACHE
    in_maps = []
    for c in range(8):
        b, qh = c // 2, c % 2
        xb = X[b]
        if qh:
            xb = np.roll(xb, -T, axis=0)
        m = {"x_in": np.ascontiguousarray(xb)}
        m.update(args)
        in_maps.append(m)
    res = run_bass_kernel_spmd(nc, in_maps, core_ids=list(range(8)))
    outp = np.empty((B, S, D), dtype=np.float32)
    for c in range(8):
        b, qh = c // 2, c % 2
        outp[b, qh * T:(qh + 1) * T] = res.results[c]["out"]
    return outp


if __name__ == "__main__":
    rng = np.random.default_rng(0)
    ins = {
        "L0_attn_X": rng.standard_normal((B, S, D), dtype=np.float32),
        "L0_attn_Wq": rng.standard_normal((D, D), dtype=np.float32) / 32,
        "L0_attn_bq": rng.standard_normal(D, dtype=np.float32) * 0.1,
        "L0_attn_Wk": rng.standard_normal((D, D), dtype=np.float32) / 32,
        "L0_attn_bk": rng.standard_normal(D, dtype=np.float32) * 0.1,
        "L0_attn_Wv": rng.standard_normal((D, D), dtype=np.float32) / 32,
        "L0_attn_bv": rng.standard_normal(D, dtype=np.float32) * 0.1,
        "L0_attn_Wo": rng.standard_normal((D, D), dtype=np.float32) / 32,
        "L0_attn_bo": rng.standard_normal(D, dtype=np.float32) * 0.1,
        "L0_ffn_w_gate": rng.integers(-127, 128, (HID, D)).astype(np.int32),
        "L0_ffn_w_up": rng.integers(-127, 128, (HID, D)).astype(np.int32),
        "L0_ffn_w_down": rng.integers(-127, 128, (D, HID)).astype(np.int32),
    }
    y = kernel(**ins)
    print("out", y.shape, y.dtype, np.abs(y).max())


# version 15
# speedup vs baseline: 2.1686x; 1.1835x over previous
import numpy as np
import concourse.bass as bass
import concourse.mybir as mybir
from concourse.bass_utils import run_bass_kernel_spmd

F32 = mybir.dt.float32
BF16 = mybir.dt.bfloat16
I32 = mybir.dt.int32
AF = mybir.ActivationFunctionType
OP = mybir.AluOpType
AX = mybir.AxisListType

B, S, D, H, DK, HID = 4, 4096, 1024, 16, 64, 3072
T = 2048
NQT = 4
EPS = 1e-5
DEQ = 1.0 / 127.0
DEQ2 = DEQ * DEQ

ENGS = ("sync", "gpsimd", "tensor", "scalar", "vector")
DMA_SEMS = ("ds", "dg")


class Sched:

    def __init__(self, limit=None):
        self.limit = limit
        self.total = 0
        self.q = {e: [] for e in ENGS}
        self.n = {s: 0 for s in ("ds", "dg", "pe", "act", "dve")}
        self.meta = {e: [] for e in ENGS}

    def check(self):
        ptr = {e: 0 for e in ENGS}
        cnt = {s: 0 for s in self.n}
        progress = True
        while progress:
            progress = False
            for e in ENGS:
                while ptr[e] < len(self.meta[e]):
                    waits, inc, tag = self.meta[e][ptr[e]]
                    if any(cnt[s] < t for (s, t) in waits):
                        break
                    if inc:
                        cnt[inc] += 16 if inc in DMA_SEMS else 1
                    ptr[e] += 1
                    progress = True
        stuck = {e: ptr[e] for e in ENGS if ptr[e] < len(self.meta[e])}
        if stuck:
            detail = {}
            for e in stuck:
                waits, inc, tag = self.meta[e][ptr[e]]
                detail[e] = (ptr[e], tag, [(s, t, cnt[s]) for (s, t) in waits if cnt[s] < t])
            raise RuntimeError(f"DEADLOCK: stuck steps {detail}")

    def step(self, eng, emit, waits=(), inc=None, tag=""):
        if self.limit is not None and self.total >= self.limit:
            return self.n.get(inc)
        self.total += 1
        waits = tuple((s, t) for (s, t) in waits if t)
        for s, t in waits:
            assert t <= self.n[s], f"wait target {s}={t} not yet emitted ({self.n[s]})"
        tgt = None
        if inc is not None:
            self.n[inc] += 16 if inc in DMA_SEMS else 1
            tgt = self.n[inc]

        def run(e, SM, waits=waits, emit=emit, inc=inc):
            for s, t in waits:
                e.wait_ge(SM[s], t)
            r = emit(e, SM)
            if inc is not None:
                assert r is not None, "inc requires a real instruction"
                r.then_inc(SM[inc], 16 if inc in DMA_SEMS else 1)
        self.q[eng].append(run)
        self.meta[eng].append((waits, inc, tag))
        return tgt


def build_nc(n_quads=4, n_heads=4, do_p3=True, do_p4=True, limit=None):
    nc = bass.Bass("TRN2", target_bir_lowering=False)

    x_in = nc.dram_tensor("x_in", [S, D], F32, kind="ExternalInput").ap()
    wq = nc.dram_tensor("wq", [D, D], F32, kind="ExternalInput").ap()
    wk = nc.dram_tensor("wk", [D, D], F32, kind="ExternalInput").ap()
    wv = nc.dram_tensor("wv", [D, D], F32, kind="ExternalInput").ap()
    wo = nc.dram_tensor("wo", [D, D], F32, kind="ExternalInput").ap()
    bq = nc.dram_tensor("bq", [D], F32, kind="ExternalInput").ap()
    bk = nc.dram_tensor("bk", [D], F32, kind="ExternalInput").ap()
    bv = nc.dram_tensor("bv", [D], F32, kind="ExternalInput").ap()
    bo = nc.dram_tensor("bo", [D], F32, kind="ExternalInput").ap()
    wg = nc.dram_tensor("wg", [HID, D], I32, kind="ExternalInput").ap()
    wu = nc.dram_tensor("wu", [HID, D], I32, kind="ExternalInput").ap()
    wd = nc.dram_tensor("wd", [D, HID], I32, kind="ExternalInput").ap()
    out = nc.dram_tensor("out", [T, D], F32, kind="ExternalOutput").ap()

    s_xbf = nc.dram_tensor("s_xbf", [8, S, 128], BF16, kind="Internal").ap()
    s_wg = nc.dram_tensor("s_wg", [8, HID, 128], BF16, kind="Internal").ap()
    s_wu = nc.dram_tensor("s_wu", [8, HID, 128], BF16, kind="Internal").ap()
    s_wd = nc.dram_tensor("s_wd", [24, D, 128], BF16, kind="Internal").ap()
    s_x1b = nc.dram_tensor("s_x1b", [8, T, 128], BF16, kind="Internal").ap()
    s_x1f = nc.dram_tensor("s_x1f", [T, D], F32, kind="Internal").ap()

    ident_dram = nc.inline_tensor(np.eye(128, dtype=np.float32), name="ident128")
    import ml_dtypes
    _psel = np.zeros((64, 128), dtype=ml_dtypes.bfloat16)
    for _p in range(64):
        _psel[_p, 64 + _p] = 1.0
    psel_dram = nc.inline_tensor(_psel, name="psel64")

    SBASE = 16512
    A = lambda name, shape, dt, off: nc.alloc_sbuf_tensor_at(name, shape, dt, offset=SBASE + off)
    XT = [A(f"XT{c}", [128, S], BF16, 8192 * c) for c in range(8)]
    WQq = A("WQq", [128, 2048], BF16, 65536)
    WKq = A("WKq", [128, 2048], BF16, 69632)
    WVq = A("WVq", [128, 2048], BF16, 73728)
    KTq = [A(f"KTq{p}", [128, S], BF16, 77824 + 8192 * p) for p in range(2)]
    QTq = [A(f"QTq{p}", [128, T], BF16, 94208 + 4096 * p) for p in range(2)]
    Vq = [A(f"Vq{t}", [128, 272], BF16, 102400 + 544 * t) for t in range(32)]
    SCBr = [A(f"SCBr{r}", [128, 1024], BF16, 119808 + 2048 * r) for r in range(4)]
    Kf32 = A("Kf32", [128, S], F32, 128000)
    AOT = [A(f"AOT{c}", [128, T], BF16, 151552 + 4096 * c) for c in range(8)]
    av_sb = A("av_sb", [128, 512], F32, 184320)
    rv = A("rv", [65, 512], F32, 186368)
    rvh = A("rvh", [65, 512], BF16, 188416)
    rvl = A("rvl", [65, 512], BF16, 189440)
    rvd = A("rvd", [65, 512], F32, 190464)
    rvh2 = A("rvh2", [65, 512], BF16, 192512)
    rvl2 = A("rvl2", [65, 512], BF16, 193536)
    av2_sb = A("av2_sb", [64, 512], F32, 194560)
    mo_sb = A("mo_sb", [128, 512], BF16, 196608)
    Wo_sb = [A(f"Wo{c}", [128, 1024], BF16, 2048 * c) for c in range(8)]
    o_sb = [A(f"osb{i}", [128, 512], F32, 16384 + 2048 * i) for i in range(8)]
    xres = [A(f"xres{i}", [128, 1024], F32, 32768 + 4096 * i) for i in range(2)]
    xpre = [A(f"xpre{i}", [128, 1024], F32, 40960 + 4096 * i) for i in range(2)]
    sq1 = A("sq1", [128, 1024], F32, 49152)
    x1t = [A(f"x1t{i}", [128, 1024], F32, 53248 + 4096 * i) for i in range(2)]
    x1tb = [A(f"x1tb{i}", [128, 1024], BF16, 61440 + 2048 * i) for i in range(2)]
    X1F = [A(f"X1F{c}", [128, T], BF16, 49152 + 4096 * c) for c in range(8)]
    WGT = [A(f"WGTx{c}", [128, HID], BF16, 81920 + 6144 * c) for c in range(8)]
    WUT = [A(f"WUTx{c}", [128, HID], BF16, 131072 + 6144 * c) for c in range(8)]
    WDT = [A(f"WDTx{j}", [128, 1024], BF16, 0 + 2048 * j) for j in range(24)]
    s_sb = A("s_sb", [128, 256], BF16, 180224)
    h_sb = [A(f"hsb{i}", [128, 256], BF16, 180736 + 512 * i) for i in range(2)]
    dn_sb = A("dn_sb", [128, 2048], F32, 181760)
    x2pre = [A(f"x2pre{i}", [128, 1024], F32, 189952 + 4096 * i) for i in range(2)]
    sq2 = A("sq2", [128, 1024], F32, 198144)
    x1re = A("x1re", [128, 1024], F32, 202240)
    P = lambda name, shape, dt: nc.alloc_sbuf_tensor(name, shape, dt, side="right")
    ident = P("ident", [128, 128], F32)
    psel = P("psel", [64, 128], BF16)
    ones_col = P("ones_col", [128, 1], BF16)
    ones_bc = P("ones_bc", [65, 64], BF16)
    ones_row = P("ones_row", [1, 128], BF16)
    zeros_row = P("zeros_row", [1, 128], BF16)
    bvrow = P("bvrow", [1, 1024], BF16)
    bq_sb = P("bq_sb", [128, 8], F32)
    bk_sb = P("bk_sb", [128, 8], F32)
    bo_sb = P("bo_sb", [128, 8], F32)
    ln_sum = P("ln_sum", [128, 1], F32)
    ln_ssq = P("ln_ssq", [128, 1], F32)
    ln_mean = P("ln_mean", [128, 1], F32)
    ln_var = P("ln_var", [128, 1], F32)
    ln_m2 = P("ln_m2", [128, 1], F32)
    ln_sd = P("ln_sd", [128, 1], F32)
    ln_rstd = P("ln_rstd", [128, 1], F32)
    kmean = P("kmean", [128, 1], F32)

    ps_big = nc.alloc_psum_tensor("ps_big", [128, 2048], F32)
    ps_b5 = nc.alloc_psum_tensor("ps_b5", [128, 512], F32)
    ps_bc = nc.alloc_psum_tensor("ps_bc", [128, 512], F32)
    ps_tok = nc.alloc_psum_tensor("ps_tok", [128, 1024], F32)

    sc = Sched(limit)
    st = sc.step

    m_xc = []
    for rq in range(4):
        m = 0
        for c8 in range(8):
            m = st("gpsimd", lambda e, M, rq=rq, c8=c8: e.dma_start(
                out=s_xbf[c8, rq * 1024:(rq + 1) * 1024, :],
                in_=x_in[rq * 1024:(rq + 1) * 1024, c8 * 128:(c8 + 1) * 128]), inc="dg")
        m_xc.append(m)
    m_bv = st("gpsimd", lambda e, M: e.dma_start(out=bvrow[0:1, :], in_=bv[None, :]), inc="dg")
    st("sync", lambda e, M: e.dma_start(out=ident[:], in_=ident_dram.ap()), inc="ds")
    st("sync", lambda e, M: e.dma_start(out=psel[:], in_=psel_dram.ap()), inc="ds")
    for c in range(8):
        st("sync", lambda e, M, c=c: e.dma_start(out=bq_sb[:, c], in_=bq[c * 128:(c + 1) * 128]), inc="ds")
        st("sync", lambda e, M, c=c: e.dma_start(out=bk_sb[:, c], in_=bk[c * 128:(c + 1) * 128]), inc="ds")
        st("sync", lambda e, M, c=c: e.dma_start(out=bo_sb[:, c], in_=bo[c * 128:(c + 1) * 128]), inc="ds")
    m_bias = sc.n["ds"]
    st("vector", lambda e, M: e.memset(ones_col[:], 1.0), inc="dve")
    for t32 in range(32):
        st("vector", lambda e, M, t32=t32: e.memset(
            Vq[t32][:, 0:260].rearrange("p (h x) -> p h x", h=4)[:, :, 64:65], 1.0), inc="dve")
    st("vector", lambda e, M: e.memset(ones_bc[:], 1.0), inc="dve")
    st("vector", lambda e, M: e.memset(ones_row[:], 1.0), inc="dve")
    m_ones = st("vector", lambda e, M: e.memset(zeros_row[:], 0.0), inc="dve")
    m_xt = 0
    m_xtp = [[0, 0] for _ in range(8)]
    for half in range(2):
        for c in range(8):
            m_xt = st("sync", lambda e, M, c=c, half=half: e.dma_start_transpose(
                XT[c][:, half * 2048:(half + 1) * 2048],
                s_xbf[c, half * 2048:(half + 1) * 2048, :]),
                waits=[("dg", m_xc[2 * half + 1])], inc="ds")
            m_xtp[c][half] = m_xt

    quarter_free = [0, 0, 0, 0]
    m_exp = [0] * 8
    m_b5_free = []
    m_bc_read = 0
    m_bc_done = 0
    m_avsb = 0
    m_mult = 0
    m_av_done = 0
    m_kcent_all = 0
    m_vq = [0] * 32
    m_wslice_read = 0
    m_rvchain = 0
    m_avmm_prev = {}
    m_exp_prev = {}
    m_oddevac = 0
    m_norm_prev = 0

    m_wgc = m_wuc = m_wdc = 0
    for g in range(n_quads):
        if g == 1:
            for c8 in range(8):
                m_wgc = st("gpsimd", lambda e, M, c8=c8: e.dma_start(
                    out=s_wg[c8], in_=wg[:, c8 * 128:(c8 + 1) * 128]), inc="dg")
            for c8 in range(8):
                m_wuc = st("gpsimd", lambda e, M, c8=c8: e.dma_start(
                    out=s_wu[c8], in_=wu[:, c8 * 128:(c8 + 1) * 128]), inc="dg")
            for j24 in range(24):
                m_wdc = st("gpsimd", lambda e, M, j24=j24: e.dma_start(
                    out=s_wd[j24], in_=wd[:, j24 * 128:(j24 + 1) * 128]), inc="dg")
        cols = slice(g * 256, (g + 1) * 256)
        m_wqs = st("gpsimd", lambda e, M, v=wq.rearrange("(c p) n -> p c n", p=128)[:, :, cols]:
                   e.dma_start(out=WQq.ap().rearrange("p (c n) -> p c n", c=8), in_=v),
                   waits=[("pe", m_wslice_read)], inc="dg")
        m_wks = st("gpsimd", lambda e, M, v=wk.rearrange("(c p) n -> p c n", p=128)[:, :, cols]:
                   e.dma_start(out=WKq.ap().rearrange("p (c n) -> p c n", c=8), in_=v),
                   waits=[("pe", m_wslice_read)], inc="dg")
        m_wvs = st("gpsimd", lambda e, M, v=wv.rearrange("(c p) n -> p c n", p=128)[:, :, cols]:
                   e.dma_start(out=WVq.ap().rearrange("p (c n) -> p c n", c=8), in_=v),
                   waits=[("pe", m_wslice_read)], inc="dg")

        m_kcent = [0, 0]
        for pc in range(2):
            m_kevac = [0] * 8
            for tcol in range(8):
                qb = tcol % 4
                w0 = [("ds", m_xt), ("dg", m_wks), ("act", quarter_free[qb]),
                      ("pe", m_av_done), ("dve", m_mult)]
                mm_last = None
                for ci in range(8):
                    def mmk(e, M, ci=ci, pc=pc, tcol=tcol, qb=qb):
                        return e.matmul(ps_big[:, qb * 512:qb * 512 + 512],
                                        WKq[:, ci * 256 + pc * 128: ci * 256 + pc * 128 + 128],
                                        XT[ci][:, tcol * 512:(tcol + 1) * 512],
                                        start=(ci == 0), stop=(ci == 7))
                    mm_last = st("tensor", mmk, waits=(w0 if ci == 0 else ()), inc=("pe" if ci == 7 else None))
                def kev(e, M, pc=pc, tcol=tcol, qb=qb, g=g):
                    return e.activation(Kf32[:, tcol * 512:(tcol + 1) * 512], ps_big[:, qb * 512:qb * 512 + 512],
                                        AF.Identity, bias=bk_sb[:, 2 * g + pc:2 * g + pc + 1])
                m_kevac[tcol] = st("scalar", kev,
                                   waits=[("pe", mm_last), ("ds", m_bias),
                                          ("dve", m_kcent[0] if pc == 1 else m_kcent_all)],
                                   inc="act")
                quarter_free[qb] = m_kevac[tcol]
            m_red = st("vector", lambda e, M: e.tensor_reduce(kmean[:], Kf32[:], AX.X, OP.add),
                       waits=[("act", m_kevac[7]), ("dve", m_kcent_all)], inc="dve")
            m_scl = st("vector", lambda e, M: e.tensor_scalar(kmean[:], kmean[:], 1.0 / S, None, OP.mult),
                       waits=[("dve", m_red)], inc="dve")
            m_kcent[pc] = st("vector", lambda e, M, pc=pc: e.tensor_scalar(
                KTq[pc][:], Kf32[:], kmean[:, 0:1], None, OP.subtract),
                waits=[("dve", m_scl), ("pe", m_av_done)], inc="dve")
        m_kcent_all = m_kcent[1]

        m_qevac = [[0] * NQT, [0] * NQT]
        for pc in range(2):
            for qt in range(NQT):
                qb = qt % 4
                w0 = [("act", quarter_free[qb]), ("dg", m_wqs)]
                mm_last = None
                for ci in range(8):
                    def mmq(e, M, ci=ci, pc=pc, qt=qt, qb=qb):
                        return e.matmul(ps_big[:, qb * 512:qb * 512 + 512],
                                        WQq[:, ci * 256 + pc * 128: ci * 256 + pc * 128 + 128],
                                        XT[ci][:, qt * 512:(qt + 1) * 512],
                                        start=(ci == 0), stop=(ci == 7))
                    mm_last = st("tensor", mmq, waits=(w0 if ci == 0 else ()), inc=("pe" if ci == 7 else None))
                def qev(e, M, pc=pc, qt=qt, qb=qb, g=g):
                    return e.activation(QTq[pc][:, qt * 512:(qt + 1) * 512], ps_big[:, qb * 512:qb * 512 + 512],
                                        AF.Identity, bias=bq_sb[:, 2 * g + pc:2 * g + pc + 1])
                m_qevac[pc][qt] = st("scalar", qev, waits=[("pe", mm_last)], inc="act")
                quarter_free[qb] = m_qevac[pc][qt]

        for tt in range(32):
            pjv = ps_b5 if tt % 2 == 0 else ps_bc
            w0 = [("dg", m_wvs), ("dve", m_ones), ("dg", m_bv)]
            if tt >= 2:
                w0.append(("dve", m_vq[tt - 2]))
            else:
                w0 += m_b5_free
            mm_last = None
            for ci in range(8):
                def mmv(e, M, ci=ci, tt=tt, pjv=pjv):
                    return e.matmul(pjv[:, 0:256],
                                    XT[ci][:, tt * 128:(tt + 1) * 128],
                                    WVq[:, ci * 256:(ci + 1) * 256],
                                    start=(ci == 0), stop=False)
                st("tensor", mmv, waits=(w0 if ci == 0 else ()))
            m_vmm = st("tensor", lambda e, M, pjv=pjv, g=g: e.matmul(
                pjv[:, 0:256], ones_row[:], bvrow[0:1, g * 256:(g + 1) * 256],
                start=False, stop=True), inc="pe")
            m_vq[tt] = st("vector", lambda e, M, tt=tt, pjv=pjv: e.tensor_copy(
                Vq[tt][:, 0:260].rearrange("p (h x) -> p h x", h=4)[:, :, 0:64],
                pjv[:, 0:256].rearrange("p (h x) -> p h x", h=4)), waits=[("pe", m_vmm)], inc="dve")
        m_wslice_read = sc.n["pe"]
        m_b5_free = [("dve", m_vq[30]), ("dve", m_vq[31])]

        for pc in range(2):
            ch = 2 * g + pc
            for qt in range(NQT):
                m_avmm = dict(m_avmm_prev)
                m_e = {c - 32: v for c, v in m_exp_prev.items()}

                def emit_scores(c):
                    i2 = c % 2
                    r4 = c % 4
                    w0 = [("dve", m_kcent[pc]), ("act", m_qevac[pc][qt]),
                          ("act", quarter_free[2 * i2]), ("act", quarter_free[2 * i2 + 1])]
                    def mmse(e, M, c=c, i2=i2, pc=pc, qt=qt):
                        return e.matmul(ps_big[:, i2 * 1024: i2 * 1024 + 512],
                                        KTq[pc][0:64, c * 128:(c + 1) * 128],
                                        QTq[pc][0:64, qt * 512:(qt + 1) * 512],
                                        start=True, stop=True, tile_position=(0, 0))
                    st("tensor", mmse, waits=w0)
                    def mmso(e, M, c=c, i2=i2, pc=pc, qt=qt):
                        return e.matmul(ps_big[:, i2 * 1024 + 512: i2 * 1024 + 1024],
                                        KTq[pc][64:128, c * 128:(c + 1) * 128],
                                        QTq[pc][64:128, qt * 512:(qt + 1) * 512],
                                        start=True, stop=True, tile_position=(64, 0))
                    m_smm = st("tensor", mmso, inc="pe")
                    m_e[c] = st("scalar", lambda e, M, c=c, i2=i2, r4=r4: e.activation(
                        SCBr[r4][:], ps_big[:, i2 * 1024:(i2 + 1) * 1024], AF.Exp, scale=0.125),
                        waits=[("pe", m_smm), ("pe", m_avmm.get(c - 4, 0))], inc="act")
                    quarter_free[2 * i2] = m_e[c]
                    quarter_free[2 * i2 + 1] = m_e[c]

                def emit_av(c):
                    r4 = c % 4
                    strt, stop = (c == 0), (c == 31)
                    wav = [("act", m_e[c])]
                    if strt:
                        wav += [("dve", m_norm_prev), ("dve", m_vq[31])]
                    def av_e(e, M, c=c, r4=r4, pc=pc, strt=strt, stop=stop):
                        return e.matmul(ps_b5[0:65, :], Vq[c][:, (2 * pc) * 65:(2 * pc) * 65 + 65],
                                        SCBr[r4][:, 0:512], start=strt, stop=stop)
                    st("tensor", av_e, waits=wav)
                    def av_o(e, M, c=c, r4=r4, pc=pc, strt=strt, stop=stop):
                        return e.matmul(ps_bc[0:65, :], Vq[c][:, (2 * pc + 1) * 65:(2 * pc + 1) * 65 + 65],
                                        SCBr[r4][:, 512:1024], start=strt, stop=stop)
                    m_avmm[c] = st("tensor", av_o, inc="pe")

                for c in range(32):
                    emit_scores(c)
                    if c >= 2:
                        emit_av(c - 2)
                emit_av(30)
                emit_av(31)
                m_avmm_prev = {c - 32: m_avmm[c] for c in range(28, 32)}
                m_exp_prev = {c: m_e[c] for c in range(30, 32)}
                m_av = m_avmm[31]
                m_r1 = st("vector", lambda e, M: e.reciprocal(rv[64:65, :], ps_b5[64:65, :]),
                          waits=[("pe", m_av), ("pe", m_bc_done)], inc="dve")
                m_r1b = st("vector", lambda e, M: e.reciprocal(rvd[64:65, :], ps_bc[64:65, :]),
                           waits=[("dve", m_r1)], inc="dve")
                m_ae = st("vector", lambda e, M: e.tensor_copy(av_sb[0:64, :], ps_b5[0:64, :]),
                          waits=[("dve", m_r1b), ("dve", m_mult)], inc="dve")
                m_ao = st("vector", lambda e, M: e.tensor_copy(av2_sb[:], ps_bc[0:64, :]),
                          waits=[("dve", m_ae)], inc="dve")
                m_r2 = st("vector", lambda e, M: e.tensor_copy(rvh[64:65, :], rv[64:65, :]),
                          waits=[("dve", m_ao), ("pe", m_bc_done)], inc="dve")
                m_r2b = st("vector", lambda e, M: e.tensor_copy(rvh2[64:65, :], rvd[64:65, :]),
                           waits=[("dve", m_r2)], inc="dve")
                m_r4a = st("vector", lambda e, M: e.tensor_tensor(rvl[64:65, :], rv[64:65, :], rvh[64:65, :], OP.subtract),
                           waits=[("dve", m_r2b)], inc="dve")
                m_r4_ = st("vector", lambda e, M: e.tensor_tensor(rvl2[64:65, :], rvd[64:65, :], rvh2[64:65, :], OP.subtract),
                           waits=[("dve", m_r4a)], inc="dve")
                st("tensor", lambda e, M: e.matmul(ps_tok[0:64, 0:512], ones_bc[64:65, 0:64], rvh[64:65, :],
                                                   start=True, stop=False, tile_position=(64, 0)),
                   waits=[("dve", m_r4_), ("dve", m_oddevac)])
                st("tensor", lambda e, M: e.matmul(ps_tok[0:64, 0:512], ones_bc[64:65, 0:64], rvl[64:65, :],
                                                   start=False, stop=True, tile_position=(64, 0)))
                st("tensor", lambda e, M: e.matmul(ps_tok[0:64, 512:1024], ones_bc[64:65, 0:64], rvh2[64:65, :],
                                                   start=True, stop=False, tile_position=(64, 0)))
                m_bc_done = st("tensor", lambda e, M: e.matmul(ps_tok[0:64, 512:1024], ones_bc[64:65, 0:64], rvl2[64:65, :],
                                                               start=False, stop=True, tile_position=(64, 0)), inc="pe")
                m_me = st("vector", lambda e, M, ch=ch, qt=qt: e.tensor_tensor(
                    AOT[ch][0:64, qt * 512:(qt + 1) * 512], av_sb[0:64, :], ps_tok[0:64, 0:512], OP.mult),
                    waits=[("pe", m_bc_done)], inc="dve")
                m_mo = st("vector", lambda e, M: e.tensor_tensor(
                    mo_sb[0:64, :], av2_sb[:], ps_tok[0:64, 512:1024], OP.mult),
                    waits=[("dve", m_me)], inc="dve")
                m_mover = st("tensor", lambda e, M: e.matmul(
                    ps_tok[:, 0:512], psel[:], mo_sb[0:64, :], start=True, stop=True),
                    waits=[("dve", m_mo)], inc="pe")
                m_mult = st("vector", lambda e, M, ch=ch, qt=qt: e.tensor_copy(
                    AOT[ch][64:128, qt * 512:(qt + 1) * 512], ps_tok[64:128, 0:512]),
                    waits=[("pe", m_mover)], inc="dve")
                m_oddevac = m_mult
                m_norm_prev = m_ao
                m_norm_prev = m_ao
                m_av_done = m_av

    p2_pe_done = sc.n["pe"]
    p2_mult_done = m_mult
    p2_act_done = sc.n["act"]
    p2w_pre = [("pe", p2_pe_done), ("act", p2_act_done), ("dve", p2_mult_done)]
    m_wgt = 0
    for c in range(8):
        m_wgt = st("sync", lambda e, M, c=c: e.dma_start_transpose(
            WGT[c][:], s_wg[c]), waits=p2w_pre + [("dg", m_wgc)], inc="ds")
    m_wut = 0
    for c in range(3):
        m_wut = st("sync", lambda e, M, c=c: e.dma_start_transpose(
            WUT[c][:], s_wu[c]), waits=p2w_pre + [("dg", m_wuc)], inc="ds")

    if not do_p3:
        do_p4 = False
    m_wo = 0
    for c in (range(8) if do_p3 else range(0)):
        m_wo = st("gpsimd", lambda e, M, c=c: e.dma_start(
            out=Wo_sb[c].ap(), in_=wo.rearrange("(c p) n -> p c n", p=128)[:, c, :]),
            waits=[("pe", p2_pe_done)], inc="dg")
    m_oev = [0] * 8
    m_tr_prev = 0
    m_xadd = [0, 0]
    m_sq_prev = 0
    m_ln_prev = 0
    m_var0_prev = 0
    m_x1store = [0, 0]
    m_xres_ld = [0, 0]
    m_ps_tok_free = 0
    for qt in (range(4) if do_p3 else range(0)):
        for outc in range(8):
            ps_o = ps_b5 if outc % 2 == 0 else ps_bc
            w0 = [("dg", m_wo), ("dve", p2_mult_done), ("act", m_oev[outc - 2] if outc >= 2 else 0),
                  ("pe", m_tr_prev)]
            if qt == 0 and outc < 2:
                w0 += [("dve", p2_mult_done)]
            mm_last = None
            for ci in range(8):
                def mmo(e, M, ci=ci, outc=outc, qt=qt, ps_o=ps_o):
                    return e.matmul(ps_o[:], Wo_sb[ci][:, outc * 128:(outc + 1) * 128],
                                    AOT[ci][:, qt * 512:(qt + 1) * 512],
                                    start=(ci == 0), stop=(ci == 7))
                mm_last = st("tensor", mmo, waits=(w0 if ci == 0 else ()), inc=("pe" if ci == 7 else None))
            m_oev[outc] = st("scalar", lambda e, M, outc=outc, ps_o=ps_o: e.activation(
                o_sb[outc][:], ps_o[:], AF.Identity, bias=bo_sb[:, outc:outc + 1]),
                waits=[("pe", mm_last), ("pe", m_tr_prev)], inc="act")
        if qt == 3:
            for c in range(3, 8):
                m_wut = st("sync", lambda e, M, c=c: e.dma_start_transpose(
                    WUT[c][:], s_wu[c]), waits=[("pe", sc.n["pe"]), ("dg", m_wuc)], inc="ds")
        for qs in range(4):
            i2 = qs % 2
            tok0 = qt * 512 + qs * 128
            m_xres_ld[i2] = st("sync", lambda e, M, i2=i2, tok0=tok0: e.dma_start(
                out=xres[i2][:], in_=x_in[tok0:tok0 + 128, :]),
                waits=[("dve", m_xadd[i2]), ("pe", p2_pe_done)], inc="ds")
            m_tr = None
            for outc in range(8):
                def trp(e, M, outc=outc, qs=qs):
                    return e.transpose(ps_tok[:, outc * 128:(outc + 1) * 128],
                                      o_sb[outc][:, qs * 128:(qs + 1) * 128], ident[:])
                m_tr = st("tensor", trp,
                          waits=([("act", m_oev[7]), ("dve", m_ps_tok_free), ("dve", p2_mult_done)] if outc == 0 else ()),
                          inc=("pe" if outc == 7 else None))
            m_tr_prev = m_tr
            m_add = st("vector", lambda e, M, i2=i2: e.tensor_tensor(
                xpre[i2][:], ps_tok[:], xres[i2][:], OP.add),
                waits=[("pe", m_tr), ("ds", m_xres_ld[i2]), ("act", m_sq_prev), ("dve", m_ln_prev)], inc="dve")
            m_ps_tok_free = m_add
            m_xadd[i2] = m_add
            m_red = st("vector", lambda e, M, i2=i2: e.tensor_reduce(
                ln_sum[:], xpre[i2][:], AX.X, OP.add), waits=[("dve", m_add), ("dve", m_ln_prev)], inc="dve")
            m_sq = st("scalar", lambda e, M, i2=i2: e.activation(
                sq1[:], xpre[i2][:], AF.Square, accum_out=ln_ssq[:, 0:1]),
                waits=[("dve", m_add), ("dve", m_var0_prev)], inc="act")
            m_sq_prev = m_sq
            m_mean = st("vector", lambda e, M: e.tensor_scalar(
                ln_mean[:], ln_sum[:], 1.0 / D, None, OP.mult), waits=[("dve", m_red)], inc="dve")
            m_m2 = st("vector", lambda e, M: e.tensor_tensor(
                ln_m2[:], ln_mean[:], ln_mean[:], OP.mult), waits=[("dve", m_mean)], inc="dve")
            m_var0 = st("vector", lambda e, M: e.tensor_scalar(
                ln_var[:], ln_ssq[:], 1.0 / D, EPS, OP.mult, OP.add),
                waits=[("act", m_sq), ("dve", m_m2)], inc="dve")
            m_var0_prev = m_var0
            m_var = st("vector", lambda e, M: e.tensor_tensor(
                ln_var[:], ln_var[:], ln_m2[:], OP.subtract), waits=[("dve", m_var0)], inc="dve")
            m_sd = st("scalar", lambda e, M: e.activation(
                ln_sd[:], ln_var[:], AF.Sqrt), waits=[("dve", m_var)], inc="act")
            m_rstd = st("vector", lambda e, M: e.reciprocal(
                ln_rstd[:], ln_sd[:]), waits=[("act", m_sd)], inc="dve")
            m_app = st("vector", lambda e, M, i2=i2: e.tensor_scalar(
                x1t[i2][:], xpre[i2][:], ln_mean[:, 0:1], ln_rstd[:, 0:1], OP.subtract, OP.mult),
                waits=[("dve", m_rstd), ("dg", m_x1store[i2])], inc="dve")
            m_cast = st("vector", lambda e, M, i2=i2: e.tensor_copy(
                x1tb[i2][:], x1t[i2][:]), waits=[("dve", m_app)], inc="dve")
            m_ln_prev = m_app
            st("gpsimd", lambda e, M, i2=i2, tok0=tok0: e.dma_start(
                out=s_x1f[tok0:tok0 + 128, :], in_=x1t[i2][:]), waits=[("dve", m_app)], inc="dg")
            for c8 in range(8):
                m_x1store[i2] = st("gpsimd", lambda e, M, i2=i2, tok0=tok0, c8=c8: e.dma_start(
                    out=s_x1b[c8, tok0:tok0 + 128, :],
                    in_=x1tb[i2][:, c8 * 128:(c8 + 1) * 128]),
                    waits=([("dve", m_cast)] if c8 == 0 else ()), inc="dg")

    p3_pe = sc.n["pe"]
    p3_act = sc.n["act"]
    p3_dve = sc.n["dve"]
    p3_dg = sc.n["dg"]
    R8 = range(8) if do_p4 else range(0)
    R24 = range(24) if do_p4 else range(0)

    p3w = [("pe", p3_pe), ("act", p3_act), ("dve", p3_dve), ("dg", p3_dg)]
    m_x1f = 0
    for c in R8:
        m_x1f = st("sync", lambda e, M, c=c: e.dma_start_transpose(
            X1F[c][:], s_x1b[c]), waits=[("dg", p3_dg), ("pe", p3_pe)], inc="ds")
    m_wdt = 0
    m_wdtj = []
    for j in R24:
        m_wdt = st("sync", lambda e, M, j=j: e.dma_start_transpose(
            WDT[j][:], s_wd[j]), waits=p3w + [("dg", m_wdc)], inc="ds")
        m_wdtj.append(m_wdt)

    m_si = [0] * 24
    m_h = [0] * 24
    m_dnj = [0] * 24
    m_dnev = [0] * 8
    m_x2add = [0, 0]
    m_x2store = [0, 0]
    m_tr2_prev = 0
    m_x1re_read = 0
    m_dnev_last = 0
    for qt in (range(8) if do_p4 else range(0)):
        def emit_gu(j):
            pj = (ps_b5, ps_bc, ps_tok)[j % 3]
            w0 = [("ds", m_wgt), ("ds", m_wut), ("ds", m_x1f)]
            if j >= 3:
                w0 += [("act", m_si[j - 3]), ("dve", m_h[j - 3])]
            elif qt > 0:
                w0 += [("act", m_si[21 + j]), ("dve", m_h[21 + j])]
            else:
                w0 += [("dve", p2_mult_done)]
            mm_last = None
            for ci in range(8):
                def mmg(e, M, ci=ci, j=j, qt=qt, pj=pj):
                    return e.matmul(pj[:, 0:256], WGT[ci][:, j * 128:(j + 1) * 128],
                                    X1F[ci][:, qt * 256:(qt + 1) * 256],
                                    start=(ci == 0), stop=(ci == 7))
                st("tensor", mmg, waits=(w0 if ci == 0 else ()))
            for ci in range(8):
                def mmu(e, M, ci=ci, j=j, qt=qt, pj=pj):
                    return e.matmul(pj[:, 256:512], WUT[ci][:, j * 128:(j + 1) * 128],
                                    X1F[ci][:, qt * 256:(qt + 1) * 256],
                                    start=(ci == 0), stop=(ci == 7))
                mm_last = st("tensor", mmu, inc=("pe" if ci == 7 else None))
            m_si[j] = st("scalar", lambda e, M, pj=pj: e.activation(
                s_sb[:], pj[:, 0:256], AF.Silu, scale=DEQ),
                waits=[("pe", mm_last)] + ([("dve", m_h[j - 1])] if j >= 1 else
                                           ([("dve", m_h[23])] if qt > 0 else [])), inc="act")
            m_h[j] = st("vector", lambda e, M, j=j, pj=pj: e.tensor_tensor(
                h_sb[j % 2][:], s_sb[:], pj[:, 256:512], OP.mult),
                waits=[("act", m_si[j])] + ([("pe", m_dnj[j - 2])] if j >= 2 else
                                            ([("pe", m_dnj[22 + j])] if qt > 0 else [])), inc="dve")

        def emit_dn(j):
            w_dn = [("dve", m_h[j]), ("ds", m_wdtj[j])]
            if j == 0:
                w_dn += [("dve", m_x2add[0]), ("dve", m_x2add[1]), ("act", m_dnev_last)]
                for bank in range(4):
                    def mmz(e, M, bank=bank):
                        return e.matmul(ps_big[:, bank * 512:(bank + 1) * 512],
                                        zeros_row[:], bvrow[0:1, 0:512], start=True, stop=False)
                    st("tensor", mmz, waits=(w_dn if bank == 0 else ()))
                w_dn = []
            mm_last = None
            for outc in range(8):
                def mmdn(e, M, outc=outc, j=j):
                    return e.matmul(ps_big[:, outc * 256:(outc + 1) * 256],
                                    WDT[j][:, outc * 128:(outc + 1) * 128], h_sb[j % 2][:],
                                    start=False, stop=(j == 23))
                mm_last = st("tensor", mmdn, waits=(w_dn if outc == 0 else ()), inc=("pe" if outc == 7 else None))
            m_dnj[j] = mm_last

        for j in range(24):
            emit_gu(j)
            if j >= 1:
                emit_dn(j - 1)
        emit_dn(23)
        for outc in range(8):
            m_dnev[outc] = st("scalar", lambda e, M, outc=outc: e.activation(
                dn_sb[:, outc * 256:(outc + 1) * 256], ps_big[:, outc * 256:(outc + 1) * 256],
                AF.Identity, scale=DEQ2),
                waits=[("pe", m_dnj[23]), ("pe", m_tr2_prev)], inc="act")
        m_dnev_last = m_dnev[7]
        for qs in range(2):
            i2 = qs % 2
            tok0 = qt * 256 + qs * 128
            m_ld = st("sync", lambda e, M, tok0=tok0: e.dma_start(
                out=x1re[:], in_=s_x1f[tok0:tok0 + 128, :]),
                waits=[("dve", m_x1re_read), ("dg", p3_dg)], inc="ds")
            m_tr = None
            for outc in range(8):
                def trp2(e, M, outc=outc, qs=qs):
                    return e.transpose(ps_big[:, qs * 1024 + outc * 128: qs * 1024 + (outc + 1) * 128],
                                      dn_sb[:, outc * 256 + qs * 128: outc * 256 + (qs + 1) * 128], ident[:])
                m_tr = st("tensor", trp2,
                          waits=([("act", m_dnev[7]), ("dve", m_x2add[i2])] if outc == 0 else ()),
                          inc=("pe" if outc == 7 else None))
            m_tr2_prev = m_tr
            m_add = st("vector", lambda e, M, i2=i2, qs=qs: e.tensor_tensor(
                x2pre[i2][:], ps_big[:, qs * 1024:(qs + 1) * 1024], x1re[:], OP.add),
                waits=[("pe", m_tr), ("ds", m_ld), ("dg", m_x2store[i2]),
                       ("act", m_sq_prev), ("dve", m_ln_prev)], inc="dve")
            m_x1re_read = m_add
            m_x2add[i2] = m_add
            m_red = st("vector", lambda e, M, i2=i2: e.tensor_reduce(
                ln_sum[:], x2pre[i2][:], AX.X, OP.add), waits=[("dve", m_add), ("dve", m_ln_prev)], inc="dve")
            m_sq = st("scalar", lambda e, M, i2=i2: e.activation(
                sq2[:], x2pre[i2][:], AF.Square, accum_out=ln_ssq[:, 0:1]),
                waits=[("dve", m_add), ("dve", m_var0_prev)], inc="act")
            m_sq_prev = m_sq
            m_mean = st("vector", lambda e, M: e.tensor_scalar(
                ln_mean[:], ln_sum[:], 1.0 / D, None, OP.mult), waits=[("dve", m_red)], inc="dve")
            m_m2 = st("vector", lambda e, M: e.tensor_tensor(
                ln_m2[:], ln_mean[:], ln_mean[:], OP.mult), waits=[("dve", m_mean)], inc="dve")
            m_var0 = st("vector", lambda e, M: e.tensor_scalar(
                ln_var[:], ln_ssq[:], 1.0 / D, EPS, OP.mult, OP.add),
                waits=[("act", m_sq), ("dve", m_m2)], inc="dve")
            m_var0_prev = m_var0
            m_var = st("vector", lambda e, M: e.tensor_tensor(
                ln_var[:], ln_var[:], ln_m2[:], OP.subtract), waits=[("dve", m_var0)], inc="dve")
            m_sd = st("scalar", lambda e, M: e.activation(
                ln_sd[:], ln_var[:], AF.Sqrt), waits=[("dve", m_var)], inc="act")
            m_rstd = st("vector", lambda e, M: e.reciprocal(
                ln_rstd[:], ln_sd[:]), waits=[("act", m_sd)], inc="dve")
            m_app = st("vector", lambda e, M, i2=i2: e.tensor_scalar(
                x2pre[i2][:], x2pre[i2][:], ln_mean[:, 0:1], ln_rstd[:, 0:1], OP.subtract, OP.mult),
                waits=[("dve", m_rstd)], inc="dve")
            m_ln_prev = m_app
            m_x2store[i2] = st("gpsimd", lambda e, M, i2=i2, tok0=tok0: e.dma_start(
                out=out[tok0:tok0 + 128, :], in_=x2pre[i2][:]), waits=[("dve", m_app)], inc="dg")

    final_dg = sc.n["dg"]
    sc.q["gpsimd"].append(lambda e, SM: e.wait_ge(SM["dg"], final_dg))
    sc.meta["gpsimd"].append(((("dg", final_dg),), None, "final"))
    sc.check()

    from contextlib import ExitStack
    es = ExitStack()
    SM = {s: es.enter_context(nc.semaphore(s)) for s in ("ds", "dg", "pe", "act", "dve")}
    block = es.enter_context(nc.Block())

    @block.sync
    def _(e):
        for f in sc.q["sync"]:
            f(e, SM)

    @block.gpsimd
    def _(e):
        for f in sc.q["gpsimd"]:
            f(e, SM)

    @block.tensor
    def _(e):
        for f in sc.q["tensor"]:
            f(e, SM)

    @block.scalar
    def _(e):
        for f in sc.q["scalar"]:
            f(e, SM)

    @block.vector
    def _(e):
        for f in sc.q["vector"]:
            f(e, SM)

    es.close()
    return nc


_NC_CACHE = None


def kernel(**inputs):
    global _NC_CACHE
    X = np.asarray(inputs["L0_attn_X"], dtype=np.float32)
    args = {
        "wq": np.ascontiguousarray(inputs["L0_attn_Wq"], dtype=np.float32),
        "wk": np.ascontiguousarray(inputs["L0_attn_Wk"], dtype=np.float32),
        "wv": np.ascontiguousarray(inputs["L0_attn_Wv"], dtype=np.float32),
        "wo": np.ascontiguousarray(inputs["L0_attn_Wo"], dtype=np.float32),
        "bq": np.ascontiguousarray(inputs["L0_attn_bq"], dtype=np.float32),
        "bk": np.ascontiguousarray(inputs["L0_attn_bk"], dtype=np.float32),
        "bv": np.ascontiguousarray(inputs["L0_attn_bv"], dtype=np.float32),
        "bo": np.ascontiguousarray(inputs["L0_attn_bo"], dtype=np.float32),
        "wg": np.ascontiguousarray(inputs["L0_ffn_w_gate"], dtype=np.int32),
        "wu": np.ascontiguousarray(inputs["L0_ffn_w_up"], dtype=np.int32),
        "wd": np.ascontiguousarray(inputs["L0_ffn_w_down"], dtype=np.int32),
    }
    if _NC_CACHE is None:
        _NC_CACHE = build_nc()
    nc = _NC_CACHE
    in_maps = []
    for c in range(8):
        b, qh = c // 2, c % 2
        xb = X[b]
        if qh:
            xb = np.roll(xb, -T, axis=0)
        m = {"x_in": np.ascontiguousarray(xb)}
        m.update(args)
        in_maps.append(m)
    res = run_bass_kernel_spmd(nc, in_maps, core_ids=list(range(8)))
    outp = np.empty((B, S, D), dtype=np.float32)
    for c in range(8):
        b, qh = c // 2, c % 2
        outp[b, qh * T:(qh + 1) * T] = res.results[c]["out"]
    return outp


if __name__ == "__main__":
    rng = np.random.default_rng(0)
    ins = {
        "L0_attn_X": rng.standard_normal((B, S, D), dtype=np.float32),
        "L0_attn_Wq": rng.standard_normal((D, D), dtype=np.float32) / 32,
        "L0_attn_bq": rng.standard_normal(D, dtype=np.float32) * 0.1,
        "L0_attn_Wk": rng.standard_normal((D, D), dtype=np.float32) / 32,
        "L0_attn_bk": rng.standard_normal(D, dtype=np.float32) * 0.1,
        "L0_attn_Wv": rng.standard_normal((D, D), dtype=np.float32) / 32,
        "L0_attn_bv": rng.standard_normal(D, dtype=np.float32) * 0.1,
        "L0_attn_Wo": rng.standard_normal((D, D), dtype=np.float32) / 32,
        "L0_attn_bo": rng.standard_normal(D, dtype=np.float32) * 0.1,
        "L0_ffn_w_gate": rng.integers(-127, 128, (HID, D)).astype(np.int32),
        "L0_ffn_w_up": rng.integers(-127, 128, (HID, D)).astype(np.int32),
        "L0_ffn_w_down": rng.integers(-127, 128, (D, HID)).astype(np.int32),
    }
    y = kernel(**ins)
    print("out", y.shape, y.dtype, np.abs(y).max())


# version 16
# speedup vs baseline: 2.3208x; 1.0702x over previous
import numpy as np
import concourse.bass as bass
import concourse.mybir as mybir
from concourse.bass_utils import run_bass_kernel_spmd

F32 = mybir.dt.float32
BF16 = mybir.dt.bfloat16
I32 = mybir.dt.int32
AF = mybir.ActivationFunctionType
OP = mybir.AluOpType
AX = mybir.AxisListType

B, S, D, H, DK, HID = 4, 4096, 1024, 16, 64, 3072
T = 2048
NQT = 4
EPS = 1e-5
DEQ = 1.0 / 127.0
DEQ2 = DEQ * DEQ

ENGS = ("sync", "gpsimd", "tensor", "scalar", "vector")
DMA_SEMS = ("ds", "dg")


class Sched:

    def __init__(self, limit=None):
        self.limit = limit
        self.total = 0
        self.q = {e: [] for e in ENGS}
        self.n = {s: 0 for s in ("ds", "dg", "pe", "act", "dve")}
        self.meta = {e: [] for e in ENGS}

    def check(self):
        ptr = {e: 0 for e in ENGS}
        cnt = {s: 0 for s in self.n}
        progress = True
        while progress:
            progress = False
            for e in ENGS:
                while ptr[e] < len(self.meta[e]):
                    waits, inc, tag = self.meta[e][ptr[e]]
                    if any(cnt[s] < t for (s, t) in waits):
                        break
                    if inc:
                        cnt[inc] += 16 if inc in DMA_SEMS else 1
                    ptr[e] += 1
                    progress = True
        stuck = {e: ptr[e] for e in ENGS if ptr[e] < len(self.meta[e])}
        if stuck:
            detail = {}
            for e in stuck:
                waits, inc, tag = self.meta[e][ptr[e]]
                detail[e] = (ptr[e], tag, [(s, t, cnt[s]) for (s, t) in waits if cnt[s] < t])
            raise RuntimeError(f"DEADLOCK: stuck steps {detail}")

    def step(self, eng, emit, waits=(), inc=None, tag=""):
        if self.limit is not None and self.total >= self.limit:
            return self.n.get(inc)
        self.total += 1
        waits = tuple((s, t) for (s, t) in waits if t)
        for s, t in waits:
            assert t <= self.n[s], f"wait target {s}={t} not yet emitted ({self.n[s]})"
        tgt = None
        if inc is not None:
            self.n[inc] += 16 if inc in DMA_SEMS else 1
            tgt = self.n[inc]

        def run(e, SM, waits=waits, emit=emit, inc=inc):
            for s, t in waits:
                e.wait_ge(SM[s], t)
            r = emit(e, SM)
            if inc is not None:
                assert r is not None, "inc requires a real instruction"
                r.then_inc(SM[inc], 16 if inc in DMA_SEMS else 1)
        self.q[eng].append(run)
        self.meta[eng].append((waits, inc, tag))
        return tgt


def build_nc(n_quads=4, n_heads=4, do_p3=True, do_p4=True, limit=None):
    nc = bass.Bass("TRN2", target_bir_lowering=False)

    x_in = nc.dram_tensor("x_in", [S, D], F32, kind="ExternalInput").ap()
    wq = nc.dram_tensor("wq", [D, D], F32, kind="ExternalInput").ap()
    wk = nc.dram_tensor("wk", [D, D], F32, kind="ExternalInput").ap()
    wv = nc.dram_tensor("wv", [D, D], F32, kind="ExternalInput").ap()
    wo = nc.dram_tensor("wo", [D, D], F32, kind="ExternalInput").ap()
    bq = nc.dram_tensor("bq", [D], F32, kind="ExternalInput").ap()
    bk = nc.dram_tensor("bk", [D], F32, kind="ExternalInput").ap()
    bv = nc.dram_tensor("bv", [D], F32, kind="ExternalInput").ap()
    bo = nc.dram_tensor("bo", [D], F32, kind="ExternalInput").ap()
    wg = nc.dram_tensor("wg", [HID, D], I32, kind="ExternalInput").ap()
    wu = nc.dram_tensor("wu", [HID, D], I32, kind="ExternalInput").ap()
    wd = nc.dram_tensor("wd", [D, HID], I32, kind="ExternalInput").ap()
    out = nc.dram_tensor("out", [T, D], F32, kind="ExternalOutput").ap()

    s_xbf = nc.dram_tensor("s_xbf", [8, S, 128], BF16, kind="Internal").ap()
    s_wg = nc.dram_tensor("s_wg", [8, HID, 128], BF16, kind="Internal").ap()
    s_wu = nc.dram_tensor("s_wu", [8, HID, 128], BF16, kind="Internal").ap()
    s_wd = nc.dram_tensor("s_wd", [24, D, 128], BF16, kind="Internal").ap()
    s_x1b = nc.dram_tensor("s_x1b", [8, T, 128], BF16, kind="Internal").ap()
    s_x1f = nc.dram_tensor("s_x1f", [T, D], F32, kind="Internal").ap()

    ident_dram = nc.inline_tensor(np.eye(128, dtype=np.float32), name="ident128")
    import ml_dtypes
    _psel = np.zeros((64, 128), dtype=ml_dtypes.bfloat16)
    for _p in range(64):
        _psel[_p, 64 + _p] = 1.0
    psel_dram = nc.inline_tensor(_psel, name="psel64")

    SBASE = 16512
    A = lambda name, shape, dt, off: nc.alloc_sbuf_tensor_at(name, shape, dt, offset=SBASE + off)
    XT = [A(f"XT{c}", [128, S], BF16, 8192 * c) for c in range(8)]
    WQq = A("WQq", [128, 2048], BF16, 65536)
    WKq = A("WKq", [128, 2048], BF16, 69632)
    WVq = A("WVq", [128, 2048], BF16, 73728)
    KTq = [A(f"KTq{p}", [128, S], BF16, 77824 + 8192 * p) for p in range(2)]
    QTq = [A(f"QTq{p}", [128, T], BF16, 94208 + 4096 * p) for p in range(2)]
    Vq = [A(f"Vq{t}", [128, 272], BF16, 102400 + 544 * t) for t in range(32)]
    SCBr = [A(f"SCBr{r}", [128, 1024], BF16, 119808 + 2048 * r) for r in range(4)]
    Kf32 = A("Kf32", [128, S], F32, 128000)
    AOT = [A(f"AOT{c}", [128, T], BF16, 151552 + 4096 * c) for c in range(8)]
    av_sb = A("av_sb", [128, 512], F32, 184320)
    rv = A("rv", [65, 512], F32, 186368)
    rvh = A("rvh", [65, 512], BF16, 188416)
    rvl = A("rvl", [65, 512], BF16, 189440)
    rvd = A("rvd", [65, 512], F32, 190464)
    rvh2 = A("rvh2", [65, 512], BF16, 192512)
    rvl2 = A("rvl2", [65, 512], BF16, 193536)
    av2_sb = A("av2_sb", [64, 512], F32, 194560)
    mo_sb = A("mo_sb", [128, 512], BF16, 196608)
    Wo_sb = [A(f"Wo{c}", [128, 1024], BF16, 2048 * c) for c in range(8)]
    o_sb = [A(f"osb{i}", [128, 512], F32, 16384 + 2048 * i) for i in range(8)]
    xres = [A(f"xres{i}", [128, 1024], F32, 32768 + 4096 * i) for i in range(2)]
    xpre = [A(f"xpre{i}", [128, 1024], F32, 40960 + 4096 * i) for i in range(2)]
    sq1 = A("sq1", [128, 1024], F32, 49152)
    x1t = [A(f"x1t{i}", [128, 1024], F32, 53248 + 4096 * i) for i in range(2)]
    x1tb = [A(f"x1tb{i}", [128, 1024], BF16, 61440 + 2048 * i) for i in range(2)]
    X1F = [A(f"X1F{c}", [128, T], BF16, 49152 + 4096 * c) for c in range(8)]
    WGT = [A(f"WGTx{c}", [128, HID], BF16, 81920 + 6144 * c) for c in range(8)]
    WUT = [A(f"WUTx{c}", [128, HID], BF16, 131072 + 6144 * c) for c in range(8)]
    WDT = [A(f"WDTx{j}", [128, 1024], BF16, 0 + 2048 * j) for j in range(24)]
    s_sb = A("s_sb", [128, 256], BF16, 180224)
    h_sb = [A(f"hsb{i}", [128, 256], BF16, 180736 + 512 * i) for i in range(2)]
    dn_sb = A("dn_sb", [128, 2048], F32, 181760)
    x2pre = [A(f"x2pre{i}", [128, 1024], F32, 189952 + 4096 * i) for i in range(2)]
    sq2 = A("sq2", [128, 1024], F32, 198144)
    x1re = A("x1re", [128, 1024], F32, 202240)
    P = lambda name, shape, dt: nc.alloc_sbuf_tensor(name, shape, dt, side="right")
    ident = P("ident", [128, 128], F32)
    psel = P("psel", [64, 128], BF16)
    ones_col = P("ones_col", [128, 1], BF16)
    ones_bc = P("ones_bc", [65, 64], BF16)
    ones_row = P("ones_row", [1, 128], BF16)
    zeros_row = P("zeros_row", [1, 128], BF16)
    bvrow = P("bvrow", [1, 1024], BF16)
    bq_sb = P("bq_sb", [128, 8], F32)
    bk_sb = P("bk_sb", [128, 8], F32)
    bo_sb = P("bo_sb", [128, 8], F32)
    ln_sum = P("ln_sum", [128, 1], F32)
    ln_ssq = P("ln_ssq", [128, 1], F32)
    ln_mean = P("ln_mean", [128, 1], F32)
    ln_var = P("ln_var", [128, 1], F32)
    ln_m2 = P("ln_m2", [128, 1], F32)
    ln_sd = P("ln_sd", [128, 1], F32)
    ln_rstd = P("ln_rstd", [128, 1], F32)
    kmean = P("kmean", [128, 1], F32)

    ps_big = nc.alloc_psum_tensor("ps_big", [128, 2048], F32)
    ps_b5 = nc.alloc_psum_tensor("ps_b5", [128, 512], F32)
    ps_bc = nc.alloc_psum_tensor("ps_bc", [128, 512], F32)
    ps_tok = nc.alloc_psum_tensor("ps_tok", [128, 1024], F32)

    sc = Sched(limit)
    st = sc.step

    m_xc = []
    for rq in range(4):
        m = 0
        for c8 in range(8):
            m = st("gpsimd", lambda e, M, rq=rq, c8=c8: e.dma_start(
                out=s_xbf[c8, rq * 1024:(rq + 1) * 1024, :],
                in_=x_in[rq * 1024:(rq + 1) * 1024, c8 * 128:(c8 + 1) * 128]), inc="dg")
        m_xc.append(m)
    m_bv = st("gpsimd", lambda e, M: e.dma_start(out=bvrow[0:1, :], in_=bv[None, :]), inc="dg")
    st("sync", lambda e, M: e.dma_start(out=ident[:], in_=ident_dram.ap()), inc="ds")
    st("sync", lambda e, M: e.dma_start(out=psel[:], in_=psel_dram.ap()), inc="ds")
    for c in range(8):
        st("sync", lambda e, M, c=c: e.dma_start(out=bq_sb[:, c], in_=bq[c * 128:(c + 1) * 128]), inc="ds")
        st("sync", lambda e, M, c=c: e.dma_start(out=bk_sb[:, c], in_=bk[c * 128:(c + 1) * 128]), inc="ds")
        st("sync", lambda e, M, c=c: e.dma_start(out=bo_sb[:, c], in_=bo[c * 128:(c + 1) * 128]), inc="ds")
    m_bias = sc.n["ds"]
    st("vector", lambda e, M: e.memset(ones_col[:], 1.0), inc="dve")
    for t32 in range(32):
        st("vector", lambda e, M, t32=t32: e.memset(
            Vq[t32][:, 0:260].rearrange("p (h x) -> p h x", h=4)[:, :, 64:65], 1.0), inc="dve")
    st("vector", lambda e, M: e.memset(ones_bc[:], 1.0), inc="dve")
    st("vector", lambda e, M: e.memset(ones_row[:], 1.0), inc="dve")
    m_ones = st("vector", lambda e, M: e.memset(zeros_row[:], 0.0), inc="dve")
    m_xt = 0
    m_xtp = [[0, 0] for _ in range(8)]
    for half in range(2):
        for c in range(8):
            m_xt = st("sync", lambda e, M, c=c, half=half: e.dma_start_transpose(
                XT[c][:, half * 2048:(half + 1) * 2048],
                s_xbf[c, half * 2048:(half + 1) * 2048, :]),
                waits=[("dg", m_xc[2 * half + 1])], inc="ds")
            m_xtp[c][half] = m_xt

    quarter_free = [0, 0, 0, 0]
    m_exp = [0] * 8
    m_b5_free = []
    m_bc_read = 0
    m_bc_done = 0
    m_avsb = 0
    m_mult = 0
    m_av_done = 0
    m_kcent_all = 0
    m_vq = [0] * 32
    m_wslice_read = 0
    m_rvchain = 0
    m_avmm_prev = {}
    m_exp_prev = {}
    m_oddevac = 0
    m_lnread_prev = 0
    m_norm_prev = 0

    m_wgc = m_wuc = m_wdc = 0
    for g in range(n_quads):
        if g == 1:
            for c8 in range(8):
                m_wgc = st("gpsimd", lambda e, M, c8=c8: e.dma_start(
                    out=s_wg[c8], in_=wg[:, c8 * 128:(c8 + 1) * 128]), inc="dg")
            for c8 in range(8):
                m_wuc = st("gpsimd", lambda e, M, c8=c8: e.dma_start(
                    out=s_wu[c8], in_=wu[:, c8 * 128:(c8 + 1) * 128]), inc="dg")
            for j24 in range(24):
                m_wdc = st("gpsimd", lambda e, M, j24=j24: e.dma_start(
                    out=s_wd[j24], in_=wd[:, j24 * 128:(j24 + 1) * 128]), inc="dg")
        cols = slice(g * 256, (g + 1) * 256)
        m_wqs = st("gpsimd", lambda e, M, v=wq.rearrange("(c p) n -> p c n", p=128)[:, :, cols]:
                   e.dma_start(out=WQq.ap().rearrange("p (c n) -> p c n", c=8), in_=v),
                   waits=[("pe", m_wslice_read)], inc="dg")
        m_wks = st("gpsimd", lambda e, M, v=wk.rearrange("(c p) n -> p c n", p=128)[:, :, cols]:
                   e.dma_start(out=WKq.ap().rearrange("p (c n) -> p c n", c=8), in_=v),
                   waits=[("pe", m_wslice_read)], inc="dg")
        m_wvs = st("gpsimd", lambda e, M, v=wv.rearrange("(c p) n -> p c n", p=128)[:, :, cols]:
                   e.dma_start(out=WVq.ap().rearrange("p (c n) -> p c n", c=8), in_=v),
                   waits=[("pe", m_wslice_read)], inc="dg")

        m_kcent = [0, 0]
        for pc in range(2):
            m_kevac = [0] * 8
            for tcol in range(8):
                qb = tcol % 4
                w0 = [("ds", m_xt), ("dg", m_wks), ("act", quarter_free[qb]),
                      ("pe", m_av_done), ("dve", m_mult)]
                mm_last = None
                for ci in range(8):
                    def mmk(e, M, ci=ci, pc=pc, tcol=tcol, qb=qb):
                        return e.matmul(ps_big[:, qb * 512:qb * 512 + 512],
                                        WKq[:, ci * 256 + pc * 128: ci * 256 + pc * 128 + 128],
                                        XT[ci][:, tcol * 512:(tcol + 1) * 512],
                                        start=(ci == 0), stop=(ci == 7))
                    mm_last = st("tensor", mmk, waits=(w0 if ci == 0 else ()), inc=("pe" if ci == 7 else None))
                def kev(e, M, pc=pc, tcol=tcol, qb=qb, g=g):
                    return e.activation(Kf32[:, tcol * 512:(tcol + 1) * 512], ps_big[:, qb * 512:qb * 512 + 512],
                                        AF.Identity, bias=bk_sb[:, 2 * g + pc:2 * g + pc + 1])
                m_kevac[tcol] = st("scalar", kev,
                                   waits=[("pe", mm_last), ("ds", m_bias),
                                          ("dve", m_kcent[0] if pc == 1 else m_kcent_all)],
                                   inc="act")
                quarter_free[qb] = m_kevac[tcol]
            m_red = st("vector", lambda e, M: e.tensor_reduce(kmean[:], Kf32[:], AX.X, OP.add),
                       waits=[("act", m_kevac[7]), ("dve", m_kcent_all)], inc="dve")
            m_scl = st("vector", lambda e, M: e.tensor_scalar(kmean[:], kmean[:], 1.0 / S, None, OP.mult),
                       waits=[("dve", m_red)], inc="dve")
            m_kcent[pc] = st("vector", lambda e, M, pc=pc: e.tensor_scalar(
                KTq[pc][:], Kf32[:], kmean[:, 0:1], None, OP.subtract),
                waits=[("dve", m_scl), ("pe", m_av_done)], inc="dve")
        m_kcent_all = m_kcent[1]

        m_qevac = [[0] * NQT, [0] * NQT]
        for pc in range(2):
            for qt in range(NQT):
                qb = qt % 4
                w0 = [("act", quarter_free[qb]), ("dg", m_wqs)]
                mm_last = None
                for ci in range(8):
                    def mmq(e, M, ci=ci, pc=pc, qt=qt, qb=qb):
                        return e.matmul(ps_big[:, qb * 512:qb * 512 + 512],
                                        WQq[:, ci * 256 + pc * 128: ci * 256 + pc * 128 + 128],
                                        XT[ci][:, qt * 512:(qt + 1) * 512],
                                        start=(ci == 0), stop=(ci == 7))
                    mm_last = st("tensor", mmq, waits=(w0 if ci == 0 else ()), inc=("pe" if ci == 7 else None))
                def qev(e, M, pc=pc, qt=qt, qb=qb, g=g):
                    return e.activation(QTq[pc][:, qt * 512:(qt + 1) * 512], ps_big[:, qb * 512:qb * 512 + 512],
                                        AF.Identity, bias=bq_sb[:, 2 * g + pc:2 * g + pc + 1])
                m_qevac[pc][qt] = st("scalar", qev, waits=[("pe", mm_last)], inc="act")
                quarter_free[qb] = m_qevac[pc][qt]

        for tt in range(32):
            pjv = ps_b5 if tt % 2 == 0 else ps_bc
            w0 = [("dg", m_wvs), ("dve", m_ones), ("dg", m_bv)]
            if tt >= 2:
                w0.append(("dve", m_vq[tt - 2]))
            else:
                w0 += m_b5_free
            mm_last = None
            for ci in range(8):
                def mmv(e, M, ci=ci, tt=tt, pjv=pjv):
                    return e.matmul(pjv[:, 0:256],
                                    XT[ci][:, tt * 128:(tt + 1) * 128],
                                    WVq[:, ci * 256:(ci + 1) * 256],
                                    start=(ci == 0), stop=False)
                st("tensor", mmv, waits=(w0 if ci == 0 else ()))
            m_vmm = st("tensor", lambda e, M, pjv=pjv, g=g: e.matmul(
                pjv[:, 0:256], ones_row[:], bvrow[0:1, g * 256:(g + 1) * 256],
                start=False, stop=True), inc="pe")
            m_vq[tt] = st("vector", lambda e, M, tt=tt, pjv=pjv: e.tensor_copy(
                Vq[tt][:, 0:260].rearrange("p (h x) -> p h x", h=4)[:, :, 0:64],
                pjv[:, 0:256].rearrange("p (h x) -> p h x", h=4)), waits=[("pe", m_vmm)], inc="dve")
        m_wslice_read = sc.n["pe"]
        m_b5_free = [("dve", m_vq[30]), ("dve", m_vq[31])]

        for pc in range(2):
            ch = 2 * g + pc
            for qt in range(NQT):
                m_avmm = dict(m_avmm_prev)
                m_e = {c - 32: v for c, v in m_exp_prev.items()}

                def emit_scores(c):
                    i2 = c % 2
                    r4 = c % 4
                    w0 = [("dve", m_kcent[pc]), ("act", m_qevac[pc][qt]),
                          ("act", quarter_free[2 * i2]), ("act", quarter_free[2 * i2 + 1])]
                    def mmse(e, M, c=c, i2=i2, pc=pc, qt=qt):
                        return e.matmul(ps_big[:, i2 * 1024: i2 * 1024 + 512],
                                        KTq[pc][0:64, c * 128:(c + 1) * 128],
                                        QTq[pc][0:64, qt * 512:(qt + 1) * 512],
                                        start=True, stop=True, tile_position=(0, 0))
                    st("tensor", mmse, waits=w0)
                    def mmso(e, M, c=c, i2=i2, pc=pc, qt=qt):
                        return e.matmul(ps_big[:, i2 * 1024 + 512: i2 * 1024 + 1024],
                                        KTq[pc][64:128, c * 128:(c + 1) * 128],
                                        QTq[pc][64:128, qt * 512:(qt + 1) * 512],
                                        start=True, stop=True, tile_position=(64, 0))
                    m_smm = st("tensor", mmso, inc="pe")
                    m_e[c] = st("scalar", lambda e, M, c=c, i2=i2, r4=r4: e.activation(
                        SCBr[r4][:], ps_big[:, i2 * 1024:(i2 + 1) * 1024], AF.Exp, scale=0.125),
                        waits=[("pe", m_smm), ("pe", m_avmm.get(c - 4, 0))], inc="act")
                    quarter_free[2 * i2] = m_e[c]
                    quarter_free[2 * i2 + 1] = m_e[c]

                def emit_av(c):
                    r4 = c % 4
                    strt, stop = (c == 0), (c == 31)
                    wav = [("act", m_e[c])]
                    if strt:
                        wav += [("dve", m_norm_prev), ("dve", m_vq[31]), ("act", m_lnread_prev)]
                    def av_e(e, M, c=c, r4=r4, pc=pc, strt=strt, stop=stop):
                        return e.matmul(ps_b5[0:65, :], Vq[c][:, (2 * pc) * 65:(2 * pc) * 65 + 65],
                                        SCBr[r4][:, 0:512], start=strt, stop=stop)
                    st("tensor", av_e, waits=wav)
                    def av_o(e, M, c=c, r4=r4, pc=pc, strt=strt, stop=stop):
                        return e.matmul(ps_bc[0:65, :], Vq[c][:, (2 * pc + 1) * 65:(2 * pc + 1) * 65 + 65],
                                        SCBr[r4][:, 512:1024], start=strt, stop=stop)
                    m_avmm[c] = st("tensor", av_o, inc="pe")

                for c in range(32):
                    emit_scores(c)
                    if c >= 2:
                        emit_av(c - 2)
                emit_av(30)
                emit_av(31)
                m_avmm_prev = {c - 32: m_avmm[c] for c in range(28, 32)}
                m_exp_prev = {c: m_e[c] for c in range(30, 32)}
                m_av = m_avmm[31]
                m_ae = st("vector", lambda e, M: e.tensor_copy(av_sb[0:64, :], ps_b5[0:64, :]),
                          waits=[("pe", m_av), ("dve", m_mult)], inc="dve")
                m_ao = st("vector", lambda e, M: e.tensor_copy(av2_sb[:], ps_bc[0:64, :]),
                          waits=[("dve", m_ae)], inc="dve")
                m_l1 = st("scalar", lambda e, M: e.activation(rv[64:65, :], ps_b5[64:65, :], AF.Ln),
                          waits=[("pe", m_av)], inc="act")
                m_l2 = st("scalar", lambda e, M: e.activation(rvd[64:65, :], ps_bc[64:65, :], AF.Ln),
                          waits=[("act", m_l1)], inc="act")
                m_e1 = st("scalar", lambda e, M: e.activation(rv[64:65, :], rv[64:65, :], AF.Exp, scale=-1.0),
                          waits=[("act", m_l2)], inc="act")
                m_e2 = st("scalar", lambda e, M: e.activation(rvd[64:65, :], rvd[64:65, :], AF.Exp, scale=-1.0),
                          waits=[("act", m_e1)], inc="act")
                m_r2 = st("vector", lambda e, M: e.tensor_copy(rvh[64:65, :], rv[64:65, :]),
                          waits=[("act", m_e1), ("pe", m_bc_done), ("dve", m_ao)], inc="dve")
                m_r2b = st("vector", lambda e, M: e.tensor_copy(rvh2[64:65, :], rvd[64:65, :]),
                           waits=[("act", m_e2), ("dve", m_r2)], inc="dve")
                m_r4a = st("vector", lambda e, M: e.tensor_tensor(rvl[64:65, :], rv[64:65, :], rvh[64:65, :], OP.subtract),
                           waits=[("dve", m_r2b)], inc="dve")
                m_r4_ = st("vector", lambda e, M: e.tensor_tensor(rvl2[64:65, :], rvd[64:65, :], rvh2[64:65, :], OP.subtract),
                           waits=[("dve", m_r4a)], inc="dve")
                st("tensor", lambda e, M: e.matmul(ps_tok[0:64, 0:512], ones_bc[64:65, 0:64], rvh[64:65, :],
                                                   start=True, stop=False, tile_position=(64, 0)),
                   waits=[("dve", m_r4_), ("dve", m_oddevac)])
                st("tensor", lambda e, M: e.matmul(ps_tok[0:64, 0:512], ones_bc[64:65, 0:64], rvl[64:65, :],
                                                   start=False, stop=True, tile_position=(64, 0)))
                st("tensor", lambda e, M: e.matmul(ps_tok[0:64, 512:1024], ones_bc[64:65, 0:64], rvh2[64:65, :],
                                                   start=True, stop=False, tile_position=(64, 0)))
                m_bc_done = st("tensor", lambda e, M: e.matmul(ps_tok[0:64, 512:1024], ones_bc[64:65, 0:64], rvl2[64:65, :],
                                                               start=False, stop=True, tile_position=(64, 0)), inc="pe")
                m_me = st("vector", lambda e, M, ch=ch, qt=qt: e.tensor_tensor(
                    AOT[ch][0:64, qt * 512:(qt + 1) * 512], av_sb[0:64, :], ps_tok[0:64, 0:512], OP.mult),
                    waits=[("pe", m_bc_done)], inc="dve")
                m_mo = st("vector", lambda e, M: e.tensor_tensor(
                    mo_sb[0:64, :], av2_sb[:], ps_tok[0:64, 512:1024], OP.mult),
                    waits=[("dve", m_me)], inc="dve")
                m_mover = st("tensor", lambda e, M: e.matmul(
                    ps_tok[:, 0:512], psel[:], mo_sb[0:64, :], start=True, stop=True),
                    waits=[("dve", m_mo)], inc="pe")
                m_mult = st("vector", lambda e, M, ch=ch, qt=qt: e.tensor_copy(
                    AOT[ch][64:128, qt * 512:(qt + 1) * 512], ps_tok[64:128, 0:512]),
                    waits=[("pe", m_mover)], inc="dve")
                m_oddevac = m_mult
                m_norm_prev = m_ao
                m_lnread_prev = m_l2
                m_norm_prev = m_ao
                m_lnread_prev = m_l2
                m_av_done = m_av

    p2_pe_done = sc.n["pe"]
    p2_mult_done = m_mult
    p2_act_done = sc.n["act"]
    p2w_pre = [("pe", p2_pe_done), ("act", p2_act_done), ("dve", p2_mult_done)]
    m_wgt = 0
    for c in range(8):
        m_wgt = st("sync", lambda e, M, c=c: e.dma_start_transpose(
            WGT[c][:], s_wg[c]), waits=p2w_pre + [("dg", m_wgc)], inc="ds")
    m_wut = 0
    for c in range(3):
        m_wut = st("sync", lambda e, M, c=c: e.dma_start_transpose(
            WUT[c][:], s_wu[c]), waits=p2w_pre + [("dg", m_wuc)], inc="ds")

    if not do_p3:
        do_p4 = False
    m_wo = 0
    for c in (range(8) if do_p3 else range(0)):
        m_wo = st("gpsimd", lambda e, M, c=c: e.dma_start(
            out=Wo_sb[c].ap(), in_=wo.rearrange("(c p) n -> p c n", p=128)[:, c, :]),
            waits=[("pe", p2_pe_done)], inc="dg")
    m_oev = [0] * 8
    m_tr_prev = 0
    m_xadd = [0, 0]
    m_sq_prev = 0
    m_ln_prev = 0
    m_var0_prev = 0
    m_x1store = [0, 0]
    m_xres_ld = [0, 0]
    m_ps_tok_free = 0
    for qt in (range(4) if do_p3 else range(0)):
        for outc in range(8):
            ps_o = ps_b5 if outc % 2 == 0 else ps_bc
            w0 = [("dg", m_wo), ("dve", p2_mult_done), ("act", m_oev[outc - 2] if outc >= 2 else 0),
                  ("pe", m_tr_prev)]
            if qt == 0 and outc < 2:
                w0 += [("dve", p2_mult_done)]
            mm_last = None
            for ci in range(8):
                def mmo(e, M, ci=ci, outc=outc, qt=qt, ps_o=ps_o):
                    return e.matmul(ps_o[:], Wo_sb[ci][:, outc * 128:(outc + 1) * 128],
                                    AOT[ci][:, qt * 512:(qt + 1) * 512],
                                    start=(ci == 0), stop=(ci == 7))
                mm_last = st("tensor", mmo, waits=(w0 if ci == 0 else ()), inc=("pe" if ci == 7 else None))
            m_oev[outc] = st("scalar", lambda e, M, outc=outc, ps_o=ps_o: e.activation(
                o_sb[outc][:], ps_o[:], AF.Identity, bias=bo_sb[:, outc:outc + 1]),
                waits=[("pe", mm_last), ("pe", m_tr_prev)], inc="act")
        if qt == 3:
            for c in range(3, 8):
                m_wut = st("sync", lambda e, M, c=c: e.dma_start_transpose(
                    WUT[c][:], s_wu[c]), waits=[("pe", sc.n["pe"]), ("dg", m_wuc)], inc="ds")
        for qs in range(4):
            i2 = qs % 2
            tok0 = qt * 512 + qs * 128
            m_xres_ld[i2] = st("sync", lambda e, M, i2=i2, tok0=tok0: e.dma_start(
                out=xres[i2][:], in_=x_in[tok0:tok0 + 128, :]),
                waits=[("dve", m_xadd[i2]), ("pe", p2_pe_done)], inc="ds")
            m_tr = None
            for outc in range(8):
                def trp(e, M, outc=outc, qs=qs):
                    return e.transpose(ps_tok[:, outc * 128:(outc + 1) * 128],
                                      o_sb[outc][:, qs * 128:(qs + 1) * 128], ident[:])
                m_tr = st("tensor", trp,
                          waits=([("act", m_oev[7]), ("dve", m_ps_tok_free), ("dve", p2_mult_done)] if outc == 0 else ()),
                          inc=("pe" if outc == 7 else None))
            m_tr_prev = m_tr
            m_add = st("vector", lambda e, M, i2=i2: e.tensor_tensor(
                xpre[i2][:], ps_tok[:], xres[i2][:], OP.add),
                waits=[("pe", m_tr), ("ds", m_xres_ld[i2]), ("act", m_sq_prev), ("dve", m_ln_prev)], inc="dve")
            m_ps_tok_free = m_add
            m_xadd[i2] = m_add
            m_red = st("vector", lambda e, M, i2=i2: e.tensor_reduce(
                ln_sum[:], xpre[i2][:], AX.X, OP.add), waits=[("dve", m_add), ("dve", m_ln_prev)], inc="dve")
            m_sq = st("scalar", lambda e, M, i2=i2: e.activation(
                sq1[:], xpre[i2][:], AF.Square, accum_out=ln_ssq[:, 0:1]),
                waits=[("dve", m_add), ("dve", m_var0_prev)], inc="act")
            m_sq_prev = m_sq
            m_mean = st("vector", lambda e, M: e.tensor_scalar(
                ln_mean[:], ln_sum[:], 1.0 / D, None, OP.mult), waits=[("dve", m_red)], inc="dve")
            m_m2 = st("vector", lambda e, M: e.tensor_tensor(
                ln_m2[:], ln_mean[:], ln_mean[:], OP.mult), waits=[("dve", m_mean)], inc="dve")
            m_var0 = st("vector", lambda e, M: e.tensor_scalar(
                ln_var[:], ln_ssq[:], 1.0 / D, EPS, OP.mult, OP.add),
                waits=[("act", m_sq), ("dve", m_m2)], inc="dve")
            m_var0_prev = m_var0
            m_var = st("vector", lambda e, M: e.tensor_tensor(
                ln_var[:], ln_var[:], ln_m2[:], OP.subtract), waits=[("dve", m_var0)], inc="dve")
            m_sd = st("scalar", lambda e, M: e.activation(
                ln_sd[:], ln_var[:], AF.Sqrt), waits=[("dve", m_var)], inc="act")
            m_rstd = st("vector", lambda e, M: e.reciprocal(
                ln_rstd[:], ln_sd[:]), waits=[("act", m_sd)], inc="dve")
            m_app = st("vector", lambda e, M, i2=i2: e.tensor_scalar(
                x1t[i2][:], xpre[i2][:], ln_mean[:, 0:1], ln_rstd[:, 0:1], OP.subtract, OP.mult),
                waits=[("dve", m_rstd), ("dg", m_x1store[i2])], inc="dve")
            m_cast = st("vector", lambda e, M, i2=i2: e.tensor_copy(
                x1tb[i2][:], x1t[i2][:]), waits=[("dve", m_app)], inc="dve")
            m_ln_prev = m_app
            st("gpsimd", lambda e, M, i2=i2, tok0=tok0: e.dma_start(
                out=s_x1f[tok0:tok0 + 128, :], in_=x1t[i2][:]), waits=[("dve", m_app)], inc="dg")
            for c8 in range(8):
                m_x1store[i2] = st("gpsimd", lambda e, M, i2=i2, tok0=tok0, c8=c8: e.dma_start(
                    out=s_x1b[c8, tok0:tok0 + 128, :],
                    in_=x1tb[i2][:, c8 * 128:(c8 + 1) * 128]),
                    waits=([("dve", m_cast)] if c8 == 0 else ()), inc="dg")

    p3_pe = sc.n["pe"]
    p3_act = sc.n["act"]
    p3_dve = sc.n["dve"]
    p3_dg = sc.n["dg"]
    R8 = range(8) if do_p4 else range(0)
    R24 = range(24) if do_p4 else range(0)

    p3w = [("pe", p3_pe), ("act", p3_act), ("dve", p3_dve), ("dg", p3_dg)]
    m_x1f = 0
    for c in R8:
        m_x1f = st("sync", lambda e, M, c=c: e.dma_start_transpose(
            X1F[c][:], s_x1b[c]), waits=[("dg", p3_dg), ("pe", p3_pe)], inc="ds")
    m_wdt = 0
    m_wdtj = []
    for j in R24:
        m_wdt = st("sync", lambda e, M, j=j: e.dma_start_transpose(
            WDT[j][:], s_wd[j]), waits=p3w + [("dg", m_wdc)], inc="ds")
        m_wdtj.append(m_wdt)

    m_si = [0] * 24
    m_h = [0] * 24
    m_dnj = [0] * 24
    m_dnev = [0] * 8
    m_x2add = [0, 0]
    m_x2store = [0, 0]
    m_tr2_prev = 0
    m_x1re_read = 0
    m_dnev_last = 0
    for qt in (range(8) if do_p4 else range(0)):
        def emit_gu(j):
            pj = (ps_b5, ps_bc, ps_tok)[j % 3]
            w0 = [("ds", m_wgt), ("ds", m_wut), ("ds", m_x1f)]
            if j >= 3:
                w0 += [("act", m_si[j - 3]), ("dve", m_h[j - 3])]
            elif qt > 0:
                w0 += [("act", m_si[21 + j]), ("dve", m_h[21 + j])]
            else:
                w0 += [("dve", p2_mult_done)]
            mm_last = None
            for ci in range(8):
                def mmg(e, M, ci=ci, j=j, qt=qt, pj=pj):
                    return e.matmul(pj[:, 0:256], WGT[ci][:, j * 128:(j + 1) * 128],
                                    X1F[ci][:, qt * 256:(qt + 1) * 256],
                                    start=(ci == 0), stop=(ci == 7))
                st("tensor", mmg, waits=(w0 if ci == 0 else ()))
            for ci in range(8):
                def mmu(e, M, ci=ci, j=j, qt=qt, pj=pj):
                    return e.matmul(pj[:, 256:512], WUT[ci][:, j * 128:(j + 1) * 128],
                                    X1F[ci][:, qt * 256:(qt + 1) * 256],
                                    start=(ci == 0), stop=(ci == 7))
                mm_last = st("tensor", mmu, inc=("pe" if ci == 7 else None))
            m_si[j] = st("scalar", lambda e, M, pj=pj: e.activation(
                s_sb[:], pj[:, 0:256], AF.Silu, scale=DEQ),
                waits=[("pe", mm_last)] + ([("dve", m_h[j - 1])] if j >= 1 else
                                           ([("dve", m_h[23])] if qt > 0 else [])), inc="act")
            m_h[j] = st("vector", lambda e, M, j=j, pj=pj: e.tensor_tensor(
                h_sb[j % 2][:], s_sb[:], pj[:, 256:512], OP.mult),
                waits=[("act", m_si[j])] + ([("pe", m_dnj[j - 2])] if j >= 2 else
                                            ([("pe", m_dnj[22 + j])] if qt > 0 else [])), inc="dve")

        def emit_dn(j):
            w_dn = [("dve", m_h[j]), ("ds", m_wdtj[j])]
            if j == 0:
                w_dn += [("dve", m_x2add[0]), ("dve", m_x2add[1]), ("act", m_dnev_last)]
                for bank in range(4):
                    def mmz(e, M, bank=bank):
                        return e.matmul(ps_big[:, bank * 512:(bank + 1) * 512],
                                        zeros_row[:], bvrow[0:1, 0:512], start=True, stop=False)
                    st("tensor", mmz, waits=(w_dn if bank == 0 else ()))
                w_dn = []
            mm_last = None
            for outc in range(8):
                def mmdn(e, M, outc=outc, j=j):
                    return e.matmul(ps_big[:, outc * 256:(outc + 1) * 256],
                                    WDT[j][:, outc * 128:(outc + 1) * 128], h_sb[j % 2][:],
                                    start=False, stop=(j == 23))
                mm_last = st("tensor", mmdn, waits=(w_dn if outc == 0 else ()), inc=("pe" if outc == 7 else None))
            m_dnj[j] = mm_last

        for j in range(24):
            emit_gu(j)
            if j >= 1:
                emit_dn(j - 1)
        emit_dn(23)
        for outc in range(8):
            m_dnev[outc] = st("scalar", lambda e, M, outc=outc: e.activation(
                dn_sb[:, outc * 256:(outc + 1) * 256], ps_big[:, outc * 256:(outc + 1) * 256],
                AF.Identity, scale=DEQ2),
                waits=[("pe", m_dnj[23]), ("pe", m_tr2_prev)], inc="act")
        m_dnev_last = m_dnev[7]
        for qs in range(2):
            i2 = qs % 2
            tok0 = qt * 256 + qs * 128
            m_ld = st("sync", lambda e, M, tok0=tok0: e.dma_start(
                out=x1re[:], in_=s_x1f[tok0:tok0 + 128, :]),
                waits=[("dve", m_x1re_read), ("dg", p3_dg)], inc="ds")
            m_tr = None
            for outc in range(8):
                def trp2(e, M, outc=outc, qs=qs):
                    return e.transpose(ps_big[:, qs * 1024 + outc * 128: qs * 1024 + (outc + 1) * 128],
                                      dn_sb[:, outc * 256 + qs * 128: outc * 256 + (qs + 1) * 128], ident[:])
                m_tr = st("tensor", trp2,
                          waits=([("act", m_dnev[7]), ("dve", m_x2add[i2])] if outc == 0 else ()),
                          inc=("pe" if outc == 7 else None))
            m_tr2_prev = m_tr
            m_add = st("vector", lambda e, M, i2=i2, qs=qs: e.tensor_tensor(
                x2pre[i2][:], ps_big[:, qs * 1024:(qs + 1) * 1024], x1re[:], OP.add),
                waits=[("pe", m_tr), ("ds", m_ld), ("dg", m_x2store[i2]),
                       ("act", m_sq_prev), ("dve", m_ln_prev)], inc="dve")
            m_x1re_read = m_add
            m_x2add[i2] = m_add
            m_red = st("vector", lambda e, M, i2=i2: e.tensor_reduce(
                ln_sum[:], x2pre[i2][:], AX.X, OP.add), waits=[("dve", m_add), ("dve", m_ln_prev)], inc="dve")
            m_sq = st("scalar", lambda e, M, i2=i2: e.activation(
                sq2[:], x2pre[i2][:], AF.Square, accum_out=ln_ssq[:, 0:1]),
                waits=[("dve", m_add), ("dve", m_var0_prev)], inc="act")
            m_sq_prev = m_sq
            m_mean = st("vector", lambda e, M: e.tensor_scalar(
                ln_mean[:], ln_sum[:], 1.0 / D, None, OP.mult), waits=[("dve", m_red)], inc="dve")
            m_m2 = st("vector", lambda e, M: e.tensor_tensor(
                ln_m2[:], ln_mean[:], ln_mean[:], OP.mult), waits=[("dve", m_mean)], inc="dve")
            m_var0 = st("vector", lambda e, M: e.tensor_scalar(
                ln_var[:], ln_ssq[:], 1.0 / D, EPS, OP.mult, OP.add),
                waits=[("act", m_sq), ("dve", m_m2)], inc="dve")
            m_var0_prev = m_var0
            m_var = st("vector", lambda e, M: e.tensor_tensor(
                ln_var[:], ln_var[:], ln_m2[:], OP.subtract), waits=[("dve", m_var0)], inc="dve")
            m_sd = st("scalar", lambda e, M: e.activation(
                ln_sd[:], ln_var[:], AF.Sqrt), waits=[("dve", m_var)], inc="act")
            m_rstd = st("vector", lambda e, M: e.reciprocal(
                ln_rstd[:], ln_sd[:]), waits=[("act", m_sd)], inc="dve")
            m_app = st("vector", lambda e, M, i2=i2: e.tensor_scalar(
                x2pre[i2][:], x2pre[i2][:], ln_mean[:, 0:1], ln_rstd[:, 0:1], OP.subtract, OP.mult),
                waits=[("dve", m_rstd)], inc="dve")
            m_ln_prev = m_app
            m_x2store[i2] = st("gpsimd", lambda e, M, i2=i2, tok0=tok0: e.dma_start(
                out=out[tok0:tok0 + 128, :], in_=x2pre[i2][:]), waits=[("dve", m_app)], inc="dg")

    final_dg = sc.n["dg"]
    sc.q["gpsimd"].append(lambda e, SM: e.wait_ge(SM["dg"], final_dg))
    sc.meta["gpsimd"].append(((("dg", final_dg),), None, "final"))
    sc.check()

    from contextlib import ExitStack
    es = ExitStack()
    SM = {s: es.enter_context(nc.semaphore(s)) for s in ("ds", "dg", "pe", "act", "dve")}
    block = es.enter_context(nc.Block())

    @block.sync
    def _(e):
        for f in sc.q["sync"]:
            f(e, SM)

    @block.gpsimd
    def _(e):
        for f in sc.q["gpsimd"]:
            f(e, SM)

    @block.tensor
    def _(e):
        for f in sc.q["tensor"]:
            f(e, SM)

    @block.scalar
    def _(e):
        for f in sc.q["scalar"]:
            f(e, SM)

    @block.vector
    def _(e):
        for f in sc.q["vector"]:
            f(e, SM)

    es.close()
    return nc


_NC_CACHE = None


def kernel(**inputs):
    global _NC_CACHE
    X = np.asarray(inputs["L0_attn_X"], dtype=np.float32)
    args = {
        "wq": np.ascontiguousarray(inputs["L0_attn_Wq"], dtype=np.float32),
        "wk": np.ascontiguousarray(inputs["L0_attn_Wk"], dtype=np.float32),
        "wv": np.ascontiguousarray(inputs["L0_attn_Wv"], dtype=np.float32),
        "wo": np.ascontiguousarray(inputs["L0_attn_Wo"], dtype=np.float32),
        "bq": np.ascontiguousarray(inputs["L0_attn_bq"], dtype=np.float32),
        "bk": np.ascontiguousarray(inputs["L0_attn_bk"], dtype=np.float32),
        "bv": np.ascontiguousarray(inputs["L0_attn_bv"], dtype=np.float32),
        "bo": np.ascontiguousarray(inputs["L0_attn_bo"], dtype=np.float32),
        "wg": np.ascontiguousarray(inputs["L0_ffn_w_gate"], dtype=np.int32),
        "wu": np.ascontiguousarray(inputs["L0_ffn_w_up"], dtype=np.int32),
        "wd": np.ascontiguousarray(inputs["L0_ffn_w_down"], dtype=np.int32),
    }
    if _NC_CACHE is None:
        _NC_CACHE = build_nc()
    nc = _NC_CACHE
    in_maps = []
    for c in range(8):
        b, qh = c // 2, c % 2
        xb = X[b]
        if qh:
            xb = np.roll(xb, -T, axis=0)
        m = {"x_in": np.ascontiguousarray(xb)}
        m.update(args)
        in_maps.append(m)
    res = run_bass_kernel_spmd(nc, in_maps, core_ids=list(range(8)))
    outp = np.empty((B, S, D), dtype=np.float32)
    for c in range(8):
        b, qh = c // 2, c % 2
        outp[b, qh * T:(qh + 1) * T] = res.results[c]["out"]
    return outp


if __name__ == "__main__":
    rng = np.random.default_rng(0)
    ins = {
        "L0_attn_X": rng.standard_normal((B, S, D), dtype=np.float32),
        "L0_attn_Wq": rng.standard_normal((D, D), dtype=np.float32) / 32,
        "L0_attn_bq": rng.standard_normal(D, dtype=np.float32) * 0.1,
        "L0_attn_Wk": rng.standard_normal((D, D), dtype=np.float32) / 32,
        "L0_attn_bk": rng.standard_normal(D, dtype=np.float32) * 0.1,
        "L0_attn_Wv": rng.standard_normal((D, D), dtype=np.float32) / 32,
        "L0_attn_bv": rng.standard_normal(D, dtype=np.float32) * 0.1,
        "L0_attn_Wo": rng.standard_normal((D, D), dtype=np.float32) / 32,
        "L0_attn_bo": rng.standard_normal(D, dtype=np.float32) * 0.1,
        "L0_ffn_w_gate": rng.integers(-127, 128, (HID, D)).astype(np.int32),
        "L0_ffn_w_up": rng.integers(-127, 128, (HID, D)).astype(np.int32),
        "L0_ffn_w_down": rng.integers(-127, 128, (D, HID)).astype(np.int32),
    }
    y = kernel(**ins)
    print("out", y.shape, y.dtype, np.abs(y).max())


# version 17
# speedup vs baseline: 2.3989x; 1.0337x over previous
import numpy as np
import concourse.bass as bass
import concourse.mybir as mybir
from concourse.bass_utils import run_bass_kernel_spmd

F32 = mybir.dt.float32
BF16 = mybir.dt.bfloat16
I32 = mybir.dt.int32
AF = mybir.ActivationFunctionType
OP = mybir.AluOpType
AX = mybir.AxisListType

B, S, D, H, DK, HID = 4, 4096, 1024, 16, 64, 3072
T = 2048
NQT = 4
EPS = 1e-5
DEQ = 1.0 / 127.0
DEQ2 = DEQ * DEQ

ENGS = ("sync", "gpsimd", "tensor", "scalar", "vector")
DMA_SEMS = ("ds", "dg")


class Sched:

    def __init__(self, limit=None):
        self.limit = limit
        self.total = 0
        self.q = {e: [] for e in ENGS}
        self.n = {s: 0 for s in ("ds", "dg", "pe", "act", "dve")}
        self.meta = {e: [] for e in ENGS}

    def check(self):
        ptr = {e: 0 for e in ENGS}
        cnt = {s: 0 for s in self.n}
        progress = True
        while progress:
            progress = False
            for e in ENGS:
                while ptr[e] < len(self.meta[e]):
                    waits, inc, tag = self.meta[e][ptr[e]]
                    if any(cnt[s] < t for (s, t) in waits):
                        break
                    if inc:
                        cnt[inc] += 16 if inc in DMA_SEMS else 1
                    ptr[e] += 1
                    progress = True
        stuck = {e: ptr[e] for e in ENGS if ptr[e] < len(self.meta[e])}
        if stuck:
            detail = {}
            for e in stuck:
                waits, inc, tag = self.meta[e][ptr[e]]
                detail[e] = (ptr[e], tag, [(s, t, cnt[s]) for (s, t) in waits if cnt[s] < t])
            raise RuntimeError(f"DEADLOCK: stuck steps {detail}")

    def step(self, eng, emit, waits=(), inc=None, tag=""):
        if self.limit is not None and self.total >= self.limit:
            return self.n.get(inc)
        self.total += 1
        waits = tuple((s, t) for (s, t) in waits if t)
        for s, t in waits:
            assert t <= self.n[s], f"wait target {s}={t} not yet emitted ({self.n[s]})"
        tgt = None
        if inc is not None:
            self.n[inc] += 16 if inc in DMA_SEMS else 1
            tgt = self.n[inc]

        def run(e, SM, waits=waits, emit=emit, inc=inc):
            for s, t in waits:
                e.wait_ge(SM[s], t)
            r = emit(e, SM)
            if inc is not None:
                assert r is not None, "inc requires a real instruction"
                r.then_inc(SM[inc], 16 if inc in DMA_SEMS else 1)
        self.q[eng].append(run)
        self.meta[eng].append((waits, inc, tag))
        return tgt


def build_nc(n_quads=4, n_heads=4, do_p3=True, do_p4=True, limit=None):
    nc = bass.Bass("TRN2", target_bir_lowering=False)

    x_in = nc.dram_tensor("x_in", [S, D], F32, kind="ExternalInput").ap()
    wq = nc.dram_tensor("wq", [D, D], F32, kind="ExternalInput").ap()
    wk = nc.dram_tensor("wk", [D, D], F32, kind="ExternalInput").ap()
    wv = nc.dram_tensor("wv", [D, D], F32, kind="ExternalInput").ap()
    wo = nc.dram_tensor("wo", [D, D], F32, kind="ExternalInput").ap()
    bq = nc.dram_tensor("bq", [D], F32, kind="ExternalInput").ap()
    bk = nc.dram_tensor("bk", [D], F32, kind="ExternalInput").ap()
    bv = nc.dram_tensor("bv", [D], F32, kind="ExternalInput").ap()
    bo = nc.dram_tensor("bo", [D], F32, kind="ExternalInput").ap()
    wg = nc.dram_tensor("wg", [HID, D], I32, kind="ExternalInput").ap()
    wu = nc.dram_tensor("wu", [HID, D], I32, kind="ExternalInput").ap()
    wd = nc.dram_tensor("wd", [D, HID], I32, kind="ExternalInput").ap()
    out = nc.dram_tensor("out", [T, D], F32, kind="ExternalOutput").ap()

    s_xbf = nc.dram_tensor("s_xbf", [8, S, 128], BF16, kind="Internal").ap()
    s_wg = nc.dram_tensor("s_wg", [8, HID, 128], BF16, kind="Internal").ap()
    s_wu = nc.dram_tensor("s_wu", [8, HID, 128], BF16, kind="Internal").ap()
    s_wd = nc.dram_tensor("s_wd", [24, D, 128], BF16, kind="Internal").ap()
    s_x1b = nc.dram_tensor("s_x1b", [8, T, 128], BF16, kind="Internal").ap()
    s_x1f = nc.dram_tensor("s_x1f", [T, D], F32, kind="Internal").ap()

    ident_dram = nc.inline_tensor(np.eye(128, dtype=np.float32), name="ident128")
    import ml_dtypes
    _psel = np.zeros((64, 128), dtype=ml_dtypes.bfloat16)
    for _p in range(64):
        _psel[_p, 64 + _p] = 1.0
    psel_dram = nc.inline_tensor(_psel, name="psel64")

    SBASE = 16512
    A = lambda name, shape, dt, off: nc.alloc_sbuf_tensor_at(name, shape, dt, offset=SBASE + off)
    XT = [A(f"XT{c}", [128, S], BF16, 8192 * c) for c in range(8)]
    WQq = A("WQq", [128, 2048], BF16, 65536)
    WKq = A("WKq", [128, 2048], BF16, 69632)
    WVq = A("WVq", [128, 2048], BF16, 73728)
    KTq = [A(f"KTq{p}", [128, S], BF16, 77824 + 8192 * p) for p in range(2)]
    QTq = [A(f"QTq{p}", [128, T], BF16, 94208 + 4096 * p) for p in range(2)]
    Vq = [A(f"Vq{t}", [128, 272], BF16, 102400 + 544 * t) for t in range(32)]
    SCBr = [A(f"SCBr{r}", [128, 1024], BF16, 119808 + 2048 * r) for r in range(4)]
    Kf32 = A("Kf32", [128, S], F32, 128000)
    AOT = [A(f"AOT{c}", [128, T], BF16, 151552 + 4096 * c) for c in range(8)]
    av_sb = A("av_sb", [128, 512], F32, 184320)
    rv = A("rv", [65, 512], F32, 186368)
    rvh = A("rvh", [65, 512], BF16, 188416)
    rvl = A("rvl", [65, 512], BF16, 189440)
    rvd = A("rvd", [65, 512], F32, 190464)
    rvh2 = A("rvh2", [65, 512], BF16, 192512)
    rvl2 = A("rvl2", [65, 512], BF16, 193536)
    av2_sb = A("av2_sb", [64, 512], F32, 194560)
    mo_sb = A("mo_sb", [128, 512], BF16, 196608)
    Wo_sb = [A(f"Wo{c}", [128, 1024], BF16, 2048 * c) for c in range(8)]
    o_sb = [A(f"osb{i}", [128, 512], F32, 16384 + 2048 * i) for i in range(8)]
    xres = [A(f"xres{i}", [128, 1024], F32, 32768 + 4096 * i) for i in range(2)]
    xpre = [A(f"xpre{i}", [128, 1024], F32, 40960 + 4096 * i) for i in range(2)]
    sq1 = A("sq1", [128, 1024], F32, 49152)
    x1t = [A(f"x1t{i}", [128, 1024], F32, 53248 + 4096 * i) for i in range(2)]
    x1tb = [A(f"x1tb{i}", [128, 1024], BF16, 61440 + 2048 * i) for i in range(2)]
    X1F = [A(f"X1F{c}", [128, T], BF16, 49152 + 4096 * c) for c in range(8)]
    WGT = [A(f"WGTx{c}", [128, HID], BF16, 81920 + 6144 * c) for c in range(8)]
    WUT = [A(f"WUTx{c}", [128, HID], BF16, 131072 + 6144 * c) for c in range(8)]
    WDT = [A(f"WDTx{j}", [128, 1024], BF16, 0 + 2048 * j) for j in range(24)]
    s_sb = A("s_sb", [128, 256], BF16, 180224)
    h_sb = [A(f"hsb{i}", [128, 256], BF16, 180736 + 512 * i) for i in range(2)]
    dn_sb = A("dn_sb", [128, 2048], F32, 181760)
    x2pre = [A(f"x2pre{i}", [128, 1024], F32, 189952 + 4096 * i) for i in range(2)]
    sq2 = A("sq2", [128, 1024], F32, 198144)
    x1re = A("x1re", [128, 1024], F32, 202240)
    P = lambda name, shape, dt: nc.alloc_sbuf_tensor(name, shape, dt, side="right")
    ident = P("ident", [128, 128], F32)
    psel = P("psel", [64, 128], BF16)
    ones_col = P("ones_col", [128, 1], BF16)
    ones_bc = P("ones_bc", [65, 64], BF16)
    ones_row = P("ones_row", [1, 128], BF16)
    zeros_row = P("zeros_row", [1, 128], BF16)
    bvrow = P("bvrow", [1, 1024], BF16)
    bq_sb = P("bq_sb", [128, 8], F32)
    bk_sb = P("bk_sb", [128, 8], F32)
    bo_sb = P("bo_sb", [128, 8], F32)
    ln_sum = P("ln_sum", [128, 1], F32)
    ln_ssq = P("ln_ssq", [128, 1], F32)
    ln_mean = P("ln_mean", [128, 1], F32)
    ln_var = P("ln_var", [128, 1], F32)
    ln_m2 = P("ln_m2", [128, 1], F32)
    ln_sd = P("ln_sd", [128, 1], F32)
    ln_rstd = P("ln_rstd", [128, 1], F32)
    kmean = P("kmean", [128, 1], F32)

    ps_big = nc.alloc_psum_tensor("ps_big", [128, 2048], F32)
    ps_b5 = nc.alloc_psum_tensor("ps_b5", [128, 512], F32)
    ps_bc = nc.alloc_psum_tensor("ps_bc", [128, 512], F32)
    ps_tok = nc.alloc_psum_tensor("ps_tok", [128, 1024], F32)

    sc = Sched(limit)
    st = sc.step

    m_xc = []
    for rq in range(4):
        m = 0
        for c8 in range(8):
            m = st("gpsimd", lambda e, M, rq=rq, c8=c8: e.dma_start(
                out=s_xbf[c8, rq * 1024:(rq + 1) * 1024, :],
                in_=x_in[rq * 1024:(rq + 1) * 1024, c8 * 128:(c8 + 1) * 128]), inc="dg")
        m_xc.append(m)
    m_bv = st("gpsimd", lambda e, M: e.dma_start(out=bvrow[0:1, :], in_=bv[None, :]), inc="dg")
    st("sync", lambda e, M: e.dma_start(out=ident[:], in_=ident_dram.ap()), inc="ds")
    st("sync", lambda e, M: e.dma_start(out=psel[:], in_=psel_dram.ap()), inc="ds")
    for c in range(8):
        st("sync", lambda e, M, c=c: e.dma_start(out=bq_sb[:, c], in_=bq[c * 128:(c + 1) * 128]), inc="ds")
        st("sync", lambda e, M, c=c: e.dma_start(out=bk_sb[:, c], in_=bk[c * 128:(c + 1) * 128]), inc="ds")
        st("sync", lambda e, M, c=c: e.dma_start(out=bo_sb[:, c], in_=bo[c * 128:(c + 1) * 128]), inc="ds")
    m_bias = sc.n["ds"]
    st("vector", lambda e, M: e.memset(ones_col[:], 1.0), inc="dve")
    for t32 in range(32):
        st("vector", lambda e, M, t32=t32: e.memset(
            Vq[t32][:, 0:260].rearrange("p (h x) -> p h x", h=4)[:, :, 64:65], 1.0), inc="dve")
    st("vector", lambda e, M: e.memset(ones_bc[:], 1.0), inc="dve")
    st("vector", lambda e, M: e.memset(ones_row[:], 1.0), inc="dve")
    m_ones = st("vector", lambda e, M: e.memset(zeros_row[:], 0.0), inc="dve")
    m_xt = 0
    m_xtp = [[0, 0] for _ in range(8)]
    for half in range(2):
        for c in range(8):
            m_xt = st("sync", lambda e, M, c=c, half=half: e.dma_start_transpose(
                XT[c][:, half * 2048:(half + 1) * 2048],
                s_xbf[c, half * 2048:(half + 1) * 2048, :]),
                waits=[("dg", m_xc[2 * half + 1])], inc="ds")
            m_xtp[c][half] = m_xt

    quarter_free = [0, 0, 0, 0]
    m_exp = [0] * 8
    m_b5_free = []
    m_bc_read = 0
    m_bc_done = 0
    m_avsb = 0
    m_mult = 0
    m_av_done = 0
    m_kcent_all = 0
    m_vq = [0] * 32
    m_wslice_read = 0
    m_rvchain = 0
    m_avmm_prev = {}
    m_exp_prev = {}
    m_oddevac = 0
    m_lnread_prev = 0
    pending_tail = []
    m_norm_prev = 0

    m_wgc = m_wuc = m_wdc = 0
    for g in range(n_quads):
        if g == 1:
            for c8 in range(8):
                m_wgc = st("gpsimd", lambda e, M, c8=c8: e.dma_start(
                    out=s_wg[c8], in_=wg[:, c8 * 128:(c8 + 1) * 128]), inc="dg")
            for c8 in range(8):
                m_wuc = st("gpsimd", lambda e, M, c8=c8: e.dma_start(
                    out=s_wu[c8], in_=wu[:, c8 * 128:(c8 + 1) * 128]), inc="dg")
            for j24 in range(24):
                m_wdc = st("gpsimd", lambda e, M, j24=j24: e.dma_start(
                    out=s_wd[j24], in_=wd[:, j24 * 128:(j24 + 1) * 128]), inc="dg")
        cols = slice(g * 256, (g + 1) * 256)
        m_wqs = st("gpsimd", lambda e, M, v=wq.rearrange("(c p) n -> p c n", p=128)[:, :, cols]:
                   e.dma_start(out=WQq.ap().rearrange("p (c n) -> p c n", c=8), in_=v),
                   waits=[("pe", m_wslice_read)], inc="dg")
        m_wks = st("gpsimd", lambda e, M, v=wk.rearrange("(c p) n -> p c n", p=128)[:, :, cols]:
                   e.dma_start(out=WKq.ap().rearrange("p (c n) -> p c n", c=8), in_=v),
                   waits=[("pe", m_wslice_read)], inc="dg")
        m_wvs = st("gpsimd", lambda e, M, v=wv.rearrange("(c p) n -> p c n", p=128)[:, :, cols]:
                   e.dma_start(out=WVq.ap().rearrange("p (c n) -> p c n", c=8), in_=v),
                   waits=[("pe", m_wslice_read)], inc="dg")

        m_kcent = [0, 0]
        for pc in range(2):
            m_kevac = [0] * 8
            for tcol in range(8):
                qb = tcol % 4
                w0 = [("ds", m_xt), ("dg", m_wks), ("act", quarter_free[qb]),
                      ("pe", m_av_done), ("dve", m_mult)]
                mm_last = None
                for ci in range(8):
                    def mmk(e, M, ci=ci, pc=pc, tcol=tcol, qb=qb):
                        return e.matmul(ps_big[:, qb * 512:qb * 512 + 512],
                                        WKq[:, ci * 256 + pc * 128: ci * 256 + pc * 128 + 128],
                                        XT[ci][:, tcol * 512:(tcol + 1) * 512],
                                        start=(ci == 0), stop=(ci == 7))
                    mm_last = st("tensor", mmk, waits=(w0 if ci == 0 else ()), inc=("pe" if ci == 7 else None))
                def kev(e, M, pc=pc, tcol=tcol, qb=qb, g=g):
                    return e.activation(Kf32[:, tcol * 512:(tcol + 1) * 512], ps_big[:, qb * 512:qb * 512 + 512],
                                        AF.Identity, bias=bk_sb[:, 2 * g + pc:2 * g + pc + 1])
                m_kevac[tcol] = st("scalar", kev,
                                   waits=[("pe", mm_last), ("ds", m_bias),
                                          ("dve", m_kcent[0] if pc == 1 else m_kcent_all)],
                                   inc="act")
                quarter_free[qb] = m_kevac[tcol]
            m_red = st("vector", lambda e, M: e.tensor_reduce(kmean[:], Kf32[:], AX.X, OP.add),
                       waits=[("act", m_kevac[7]), ("dve", m_kcent_all)], inc="dve")
            m_scl = st("vector", lambda e, M: e.tensor_scalar(kmean[:], kmean[:], 1.0 / S, None, OP.mult),
                       waits=[("dve", m_red)], inc="dve")
            m_kcent[pc] = st("vector", lambda e, M, pc=pc: e.tensor_scalar(
                KTq[pc][:], Kf32[:], kmean[:, 0:1], None, OP.subtract),
                waits=[("dve", m_scl), ("pe", m_av_done)], inc="dve")
        m_kcent_all = m_kcent[1]

        m_qevac = [[0] * NQT, [0] * NQT]
        for pc in range(2):
            for qt in range(NQT):
                qb = qt % 4
                w0 = [("act", quarter_free[qb]), ("dg", m_wqs)]
                mm_last = None
                for ci in range(8):
                    def mmq(e, M, ci=ci, pc=pc, qt=qt, qb=qb):
                        return e.matmul(ps_big[:, qb * 512:qb * 512 + 512],
                                        WQq[:, ci * 256 + pc * 128: ci * 256 + pc * 128 + 128],
                                        XT[ci][:, qt * 512:(qt + 1) * 512],
                                        start=(ci == 0), stop=(ci == 7))
                    mm_last = st("tensor", mmq, waits=(w0 if ci == 0 else ()), inc=("pe" if ci == 7 else None))
                def qev(e, M, pc=pc, qt=qt, qb=qb, g=g):
                    return e.activation(QTq[pc][:, qt * 512:(qt + 1) * 512], ps_big[:, qb * 512:qb * 512 + 512],
                                        AF.Identity, bias=bq_sb[:, 2 * g + pc:2 * g + pc + 1])
                m_qevac[pc][qt] = st("scalar", qev, waits=[("pe", mm_last)], inc="act")
                quarter_free[qb] = m_qevac[pc][qt]

        for tt in range(32):
            pjv = ps_b5 if tt % 2 == 0 else ps_bc
            w0 = [("dg", m_wvs), ("dve", m_ones), ("dg", m_bv)]
            if tt >= 2:
                w0.append(("dve", m_vq[tt - 2]))
            else:
                w0 += m_b5_free
            mm_last = None
            for ci in range(8):
                def mmv(e, M, ci=ci, tt=tt, pjv=pjv):
                    return e.matmul(pjv[:, 0:256],
                                    XT[ci][:, tt * 128:(tt + 1) * 128],
                                    WVq[:, ci * 256:(ci + 1) * 256],
                                    start=(ci == 0), stop=False)
                st("tensor", mmv, waits=(w0 if ci == 0 else ()))
            m_vmm = st("tensor", lambda e, M, pjv=pjv, g=g: e.matmul(
                pjv[:, 0:256], ones_row[:], bvrow[0:1, g * 256:(g + 1) * 256],
                start=False, stop=True), inc="pe")
            m_vq[tt] = st("vector", lambda e, M, tt=tt, pjv=pjv: e.tensor_copy(
                Vq[tt][:, 0:260].rearrange("p (h x) -> p h x", h=4)[:, :, 0:64],
                pjv[:, 0:256].rearrange("p (h x) -> p h x", h=4)), waits=[("pe", m_vmm)], inc="dve")
        m_wslice_read = sc.n["pe"]
        m_b5_free = [("dve", m_vq[30]), ("dve", m_vq[31])]

        for pc in range(2):
            ch = 2 * g + pc
            for qt in range(NQT):
                m_avmm = dict(m_avmm_prev)
                m_e = {c - 32: v for c, v in m_exp_prev.items()}

                def emit_scores(c):
                    i2 = c % 2
                    r4 = c % 4
                    w0 = [("dve", m_kcent[pc]), ("act", m_qevac[pc][qt]),
                          ("act", quarter_free[2 * i2]), ("act", quarter_free[2 * i2 + 1])]
                    def mmse(e, M, c=c, i2=i2, pc=pc, qt=qt):
                        return e.matmul(ps_big[:, i2 * 1024: i2 * 1024 + 512],
                                        KTq[pc][0:64, c * 128:(c + 1) * 128],
                                        QTq[pc][0:64, qt * 512:(qt + 1) * 512],
                                        start=True, stop=True, tile_position=(0, 0))
                    st("tensor", mmse, waits=w0)
                    def mmso(e, M, c=c, i2=i2, pc=pc, qt=qt):
                        return e.matmul(ps_big[:, i2 * 1024 + 512: i2 * 1024 + 1024],
                                        KTq[pc][64:128, c * 128:(c + 1) * 128],
                                        QTq[pc][64:128, qt * 512:(qt + 1) * 512],
                                        start=True, stop=True, tile_position=(64, 0))
                    m_smm = st("tensor", mmso, inc="pe")
                    m_e[c] = st("scalar", lambda e, M, c=c, i2=i2, r4=r4: e.activation(
                        SCBr[r4][:], ps_big[:, i2 * 1024:(i2 + 1) * 1024], AF.Exp, scale=0.125),
                        waits=[("pe", m_smm), ("pe", m_avmm.get(c - 4, 0))], inc="act")
                    quarter_free[2 * i2] = m_e[c]
                    quarter_free[2 * i2 + 1] = m_e[c]

                def emit_av(c):
                    r4 = c % 4
                    strt, stop = (c == 0), (c == 31)
                    wav = [("act", m_e[c])]
                    if strt:
                        wav += [("dve", m_norm_prev), ("dve", m_vq[31]), ("act", m_lnread_prev)]
                    def av_e(e, M, c=c, r4=r4, pc=pc, strt=strt, stop=stop):
                        return e.matmul(ps_b5[0:65, :], Vq[c][:, (2 * pc) * 65:(2 * pc) * 65 + 65],
                                        SCBr[r4][:, 0:512], start=strt, stop=stop)
                    st("tensor", av_e, waits=wav)
                    def av_o(e, M, c=c, r4=r4, pc=pc, strt=strt, stop=stop):
                        return e.matmul(ps_bc[0:65, :], Vq[c][:, (2 * pc + 1) * 65:(2 * pc + 1) * 65 + 65],
                                        SCBr[r4][:, 512:1024], start=strt, stop=stop)
                    m_avmm[c] = st("tensor", av_o, inc="pe")

                for c in range(32):
                    emit_scores(c)
                    if c == 2 and pending_tail:
                        f = pending_tail.pop(0)
                        m_bc_done, m_mover_, m_mult = f()
                        m_oddevac = m_mult
                    if c >= 2:
                        emit_av(c - 2)
                emit_av(30)
                emit_av(31)
                m_avmm_prev = {c - 32: m_avmm[c] for c in range(28, 32)}
                m_exp_prev = {c: m_e[c] for c in range(30, 32)}
                m_av = m_avmm[31]
                m_ae = st("vector", lambda e, M: e.tensor_copy(av_sb[0:64, :], ps_b5[0:64, :]),
                          waits=[("pe", m_av), ("dve", m_mult)], inc="dve")
                m_ao = st("vector", lambda e, M: e.tensor_copy(av2_sb[:], ps_bc[0:64, :]),
                          waits=[("dve", m_ae)], inc="dve")
                m_l1 = st("scalar", lambda e, M: e.activation(rv[64:65, :], ps_b5[64:65, :], AF.Ln),
                          waits=[("pe", m_av)], inc="act")
                m_l2 = st("scalar", lambda e, M: e.activation(rvd[64:65, :], ps_bc[64:65, :], AF.Ln),
                          waits=[("act", m_l1)], inc="act")
                m_e1 = st("scalar", lambda e, M: e.activation(rv[64:65, :], rv[64:65, :], AF.Exp, scale=-1.0),
                          waits=[("act", m_l2)], inc="act")
                m_e2 = st("scalar", lambda e, M: e.activation(rvd[64:65, :], rvd[64:65, :], AF.Exp, scale=-1.0),
                          waits=[("act", m_e1)], inc="act")
                m_r2 = st("vector", lambda e, M: e.tensor_copy(rvh[64:65, :], rv[64:65, :]),
                          waits=[("act", m_e1), ("pe", m_bc_done), ("dve", m_ao)], inc="dve")
                m_r2b = st("vector", lambda e, M: e.tensor_copy(rvh2[64:65, :], rvd[64:65, :]),
                           waits=[("act", m_e2), ("dve", m_r2)], inc="dve")
                m_r4a = st("vector", lambda e, M: e.tensor_tensor(rvl[64:65, :], rv[64:65, :], rvh[64:65, :], OP.subtract),
                           waits=[("dve", m_r2b)], inc="dve")
                m_r4_ = st("vector", lambda e, M: e.tensor_tensor(rvl2[64:65, :], rvd[64:65, :], rvh2[64:65, :], OP.subtract),
                           waits=[("dve", m_r4a)], inc="dve")
                def norm_tail(ch=ch, qt=qt, m_r4_=m_r4_, m_oddevac_w=m_oddevac):
                    global _unused
                    st("tensor", lambda e, M: e.matmul(ps_tok[0:64, 0:512], ones_bc[64:65, 0:64], rvh[64:65, :],
                                                       start=True, stop=False, tile_position=(64, 0)),
                       waits=[("dve", m_r4_), ("dve", m_oddevac_w)])
                    st("tensor", lambda e, M: e.matmul(ps_tok[0:64, 0:512], ones_bc[64:65, 0:64], rvl[64:65, :],
                                                       start=False, stop=True, tile_position=(64, 0)))
                    st("tensor", lambda e, M: e.matmul(ps_tok[0:64, 512:1024], ones_bc[64:65, 0:64], rvh2[64:65, :],
                                                       start=True, stop=False, tile_position=(64, 0)))
                    bc_done = st("tensor", lambda e, M: e.matmul(ps_tok[0:64, 512:1024], ones_bc[64:65, 0:64], rvl2[64:65, :],
                                                                 start=False, stop=True, tile_position=(64, 0)), inc="pe")
                    me = st("vector", lambda e, M, ch=ch, qt=qt: e.tensor_tensor(
                        AOT[ch][0:64, qt * 512:(qt + 1) * 512], av_sb[0:64, :], ps_tok[0:64, 0:512], OP.mult),
                        waits=[("pe", bc_done)], inc="dve")
                    mo = st("vector", lambda e, M: e.tensor_tensor(
                        mo_sb[0:64, :], av2_sb[:], ps_tok[0:64, 512:1024], OP.mult),
                        waits=[("dve", me)], inc="dve")
                    mover = st("tensor", lambda e, M: e.matmul(
                        ps_tok[:, 0:512], psel[:], mo_sb[0:64, :], start=True, stop=True),
                        waits=[("dve", mo)], inc="pe")
                    mult = st("vector", lambda e, M, ch=ch, qt=qt: e.tensor_copy(
                        AOT[ch][64:128, qt * 512:(qt + 1) * 512], ps_tok[64:128, 0:512]),
                        waits=[("pe", mover)], inc="dve")
                    return bc_done, mover, mult
                pending_tail.append(norm_tail)
                m_norm_prev = m_ao
                m_lnread_prev = m_l2
                m_b5_free = [("dve", m_norm_prev), ("act", m_lnread_prev)]
                m_norm_prev = m_ao
                m_lnread_prev = m_l2
                m_av_done = m_av

    while pending_tail:
        f = pending_tail.pop(0)
        m_bc_done, m_mover_, m_mult = f()
        m_oddevac = m_mult
    p2_pe_done = sc.n["pe"]
    p2_mult_done = m_mult
    p2_act_done = sc.n["act"]
    p2w_pre = [("pe", p2_pe_done), ("act", p2_act_done), ("dve", p2_mult_done)]
    m_wgt = 0
    for c in range(8):
        m_wgt = st("sync", lambda e, M, c=c: e.dma_start_transpose(
            WGT[c][:], s_wg[c]), waits=p2w_pre + [("dg", m_wgc)], inc="ds")
    m_wut = 0
    for c in range(3):
        m_wut = st("sync", lambda e, M, c=c: e.dma_start_transpose(
            WUT[c][:], s_wu[c]), waits=p2w_pre + [("dg", m_wuc)], inc="ds")

    if not do_p3:
        do_p4 = False
    m_wo = 0
    for c in (range(8) if do_p3 else range(0)):
        m_wo = st("gpsimd", lambda e, M, c=c: e.dma_start(
            out=Wo_sb[c].ap(), in_=wo.rearrange("(c p) n -> p c n", p=128)[:, c, :]),
            waits=[("pe", p2_pe_done)], inc="dg")
    m_oev = [0] * 8
    m_tr_prev = 0
    m_xadd = [0, 0]
    m_sq_prev = 0
    m_ln_prev = 0
    m_var0_prev = 0
    m_x1store = [0, 0]
    m_xres_ld = [0, 0]
    m_ps_tok_free = 0
    for qt in (range(4) if do_p3 else range(0)):
        for outc in range(8):
            ps_o = ps_b5 if outc % 2 == 0 else ps_bc
            w0 = [("dg", m_wo), ("dve", p2_mult_done), ("act", m_oev[outc - 2] if outc >= 2 else 0),
                  ("pe", m_tr_prev)]
            if qt == 0 and outc < 2:
                w0 += [("dve", p2_mult_done)]
            mm_last = None
            for ci in range(8):
                def mmo(e, M, ci=ci, outc=outc, qt=qt, ps_o=ps_o):
                    return e.matmul(ps_o[:], Wo_sb[ci][:, outc * 128:(outc + 1) * 128],
                                    AOT[ci][:, qt * 512:(qt + 1) * 512],
                                    start=(ci == 0), stop=(ci == 7))
                mm_last = st("tensor", mmo, waits=(w0 if ci == 0 else ()), inc=("pe" if ci == 7 else None))
            m_oev[outc] = st("scalar", lambda e, M, outc=outc, ps_o=ps_o: e.activation(
                o_sb[outc][:], ps_o[:], AF.Identity, bias=bo_sb[:, outc:outc + 1]),
                waits=[("pe", mm_last), ("pe", m_tr_prev)], inc="act")
        if qt == 3:
            for c in range(3, 8):
                m_wut = st("sync", lambda e, M, c=c: e.dma_start_transpose(
                    WUT[c][:], s_wu[c]), waits=[("pe", sc.n["pe"]), ("dg", m_wuc)], inc="ds")
        for qs in range(4):
            i2 = qs % 2
            tok0 = qt * 512 + qs * 128
            m_xres_ld[i2] = st("sync", lambda e, M, i2=i2, tok0=tok0: e.dma_start(
                out=xres[i2][:], in_=x_in[tok0:tok0 + 128, :]),
                waits=[("dve", m_xadd[i2]), ("pe", p2_pe_done)], inc="ds")
            m_tr = None
            for outc in range(8):
                def trp(e, M, outc=outc, qs=qs):
                    return e.transpose(ps_tok[:, outc * 128:(outc + 1) * 128],
                                      o_sb[outc][:, qs * 128:(qs + 1) * 128], ident[:])
                m_tr = st("tensor", trp,
                          waits=([("act", m_oev[7]), ("dve", m_ps_tok_free), ("dve", p2_mult_done)] if outc == 0 else ()),
                          inc=("pe" if outc == 7 else None))
            m_tr_prev = m_tr
            m_add = st("vector", lambda e, M, i2=i2: e.tensor_tensor(
                xpre[i2][:], ps_tok[:], xres[i2][:], OP.add),
                waits=[("pe", m_tr), ("ds", m_xres_ld[i2]), ("act", m_sq_prev), ("dve", m_ln_prev)], inc="dve")
            m_ps_tok_free = m_add
            m_xadd[i2] = m_add
            m_red = st("vector", lambda e, M, i2=i2: e.tensor_reduce(
                ln_sum[:], xpre[i2][:], AX.X, OP.add), waits=[("dve", m_add), ("dve", m_ln_prev)], inc="dve")
            m_sq = st("scalar", lambda e, M, i2=i2: e.activation(
                sq1[:], xpre[i2][:], AF.Square, accum_out=ln_ssq[:, 0:1]),
                waits=[("dve", m_add), ("dve", m_var0_prev)], inc="act")
            m_sq_prev = m_sq
            m_mean = st("vector", lambda e, M: e.tensor_scalar(
                ln_mean[:], ln_sum[:], 1.0 / D, None, OP.mult), waits=[("dve", m_red)], inc="dve")
            m_m2 = st("vector", lambda e, M: e.tensor_tensor(
                ln_m2[:], ln_mean[:], ln_mean[:], OP.mult), waits=[("dve", m_mean)], inc="dve")
            m_var0 = st("vector", lambda e, M: e.tensor_scalar(
                ln_var[:], ln_ssq[:], 1.0 / D, EPS, OP.mult, OP.add),
                waits=[("act", m_sq), ("dve", m_m2)], inc="dve")
            m_var0_prev = m_var0
            m_var = st("vector", lambda e, M: e.tensor_tensor(
                ln_var[:], ln_var[:], ln_m2[:], OP.subtract), waits=[("dve", m_var0)], inc="dve")
            m_sd = st("scalar", lambda e, M: e.activation(
                ln_sd[:], ln_var[:], AF.Sqrt), waits=[("dve", m_var)], inc="act")
            m_rstd = st("vector", lambda e, M: e.reciprocal(
                ln_rstd[:], ln_sd[:]), waits=[("act", m_sd)], inc="dve")
            m_app = st("vector", lambda e, M, i2=i2: e.tensor_scalar(
                x1t[i2][:], xpre[i2][:], ln_mean[:, 0:1], ln_rstd[:, 0:1], OP.subtract, OP.mult),
                waits=[("dve", m_rstd), ("dg", m_x1store[i2])], inc="dve")
            m_cast = st("vector", lambda e, M, i2=i2: e.tensor_copy(
                x1tb[i2][:], x1t[i2][:]), waits=[("dve", m_app)], inc="dve")
            m_ln_prev = m_app
            st("gpsimd", lambda e, M, i2=i2, tok0=tok0: e.dma_start(
                out=s_x1f[tok0:tok0 + 128, :], in_=x1t[i2][:]), waits=[("dve", m_app)], inc="dg")
            for c8 in range(8):
                m_x1store[i2] = st("gpsimd", lambda e, M, i2=i2, tok0=tok0, c8=c8: e.dma_start(
                    out=s_x1b[c8, tok0:tok0 + 128, :],
                    in_=x1tb[i2][:, c8 * 128:(c8 + 1) * 128]),
                    waits=([("dve", m_cast)] if c8 == 0 else ()), inc="dg")

    p3_pe = sc.n["pe"]
    p3_act = sc.n["act"]
    p3_dve = sc.n["dve"]
    p3_dg = sc.n["dg"]
    R8 = range(8) if do_p4 else range(0)
    R24 = range(24) if do_p4 else range(0)

    p3w = [("pe", p3_pe), ("act", p3_act), ("dve", p3_dve), ("dg", p3_dg)]
    m_x1f = 0
    for c in R8:
        m_x1f = st("sync", lambda e, M, c=c: e.dma_start_transpose(
            X1F[c][:], s_x1b[c]), waits=[("dg", p3_dg), ("pe", p3_pe)], inc="ds")
    m_wdt = 0
    m_wdtj = []
    for j in R24:
        m_wdt = st("sync", lambda e, M, j=j: e.dma_start_transpose(
            WDT[j][:], s_wd[j]), waits=p3w + [("dg", m_wdc)], inc="ds")
        m_wdtj.append(m_wdt)

    m_si = [0] * 24
    m_h = [0] * 24
    m_dnj = [0] * 24
    m_dnev = [0] * 8
    m_x2add = [0, 0]
    m_x2store = [0, 0]
    m_tr2_prev = 0
    m_x1re_read = 0
    m_dnev_last = 0
    for qt in (range(8) if do_p4 else range(0)):
        def emit_gu(j):
            pj = (ps_b5, ps_bc, ps_tok)[j % 3]
            w0 = [("ds", m_wgt), ("ds", m_wut), ("ds", m_x1f)]
            if j >= 3:
                w0 += [("act", m_si[j - 3]), ("dve", m_h[j - 3])]
            elif qt > 0:
                w0 += [("act", m_si[21 + j]), ("dve", m_h[21 + j])]
            else:
                w0 += [("dve", p2_mult_done)]
            mm_last = None
            for ci in range(8):
                def mmg(e, M, ci=ci, j=j, qt=qt, pj=pj):
                    return e.matmul(pj[:, 0:256], WGT[ci][:, j * 128:(j + 1) * 128],
                                    X1F[ci][:, qt * 256:(qt + 1) * 256],
                                    start=(ci == 0), stop=(ci == 7))
                st("tensor", mmg, waits=(w0 if ci == 0 else ()))
            for ci in range(8):
                def mmu(e, M, ci=ci, j=j, qt=qt, pj=pj):
                    return e.matmul(pj[:, 256:512], WUT[ci][:, j * 128:(j + 1) * 128],
                                    X1F[ci][:, qt * 256:(qt + 1) * 256],
                                    start=(ci == 0), stop=(ci == 7))
                mm_last = st("tensor", mmu, inc=("pe" if ci == 7 else None))
            m_si[j] = st("scalar", lambda e, M, pj=pj: e.activation(
                s_sb[:], pj[:, 0:256], AF.Silu, scale=DEQ),
                waits=[("pe", mm_last)] + ([("dve", m_h[j - 1])] if j >= 1 else
                                           ([("dve", m_h[23])] if qt > 0 else [])), inc="act")
            m_h[j] = st("vector", lambda e, M, j=j, pj=pj: e.tensor_tensor(
                h_sb[j % 2][:], s_sb[:], pj[:, 256:512], OP.mult),
                waits=[("act", m_si[j])] + ([("pe", m_dnj[j - 2])] if j >= 2 else
                                            ([("pe", m_dnj[22 + j])] if qt > 0 else [])), inc="dve")

        def emit_dn(j):
            w_dn = [("dve", m_h[j]), ("ds", m_wdtj[j])]
            if j == 0:
                w_dn += [("dve", m_x2add[0]), ("dve", m_x2add[1]), ("act", m_dnev_last)]
                for bank in range(4):
                    def mmz(e, M, bank=bank):
                        return e.matmul(ps_big[:, bank * 512:(bank + 1) * 512],
                                        zeros_row[:], bvrow[0:1, 0:512], start=True, stop=False)
                    st("tensor", mmz, waits=(w_dn if bank == 0 else ()))
                w_dn = []
            mm_last = None
            for outc in range(8):
                def mmdn(e, M, outc=outc, j=j):
                    return e.matmul(ps_big[:, outc * 256:(outc + 1) * 256],
                                    WDT[j][:, outc * 128:(outc + 1) * 128], h_sb[j % 2][:],
                                    start=False, stop=(j == 23))
                mm_last = st("tensor", mmdn, waits=(w_dn if outc == 0 else ()), inc=("pe" if outc == 7 else None))
            m_dnj[j] = mm_last

        for j in range(24):
            emit_gu(j)
            if j >= 1:
                emit_dn(j - 1)
        emit_dn(23)
        for outc in range(8):
            m_dnev[outc] = st("scalar", lambda e, M, outc=outc: e.activation(
                dn_sb[:, outc * 256:(outc + 1) * 256], ps_big[:, outc * 256:(outc + 1) * 256],
                AF.Identity, scale=DEQ2),
                waits=[("pe", m_dnj[23]), ("pe", m_tr2_prev)], inc="act")
        m_dnev_last = m_dnev[7]
        for qs in range(2):
            i2 = qs % 2
            tok0 = qt * 256 + qs * 128
            m_ld = st("sync", lambda e, M, tok0=tok0: e.dma_start(
                out=x1re[:], in_=s_x1f[tok0:tok0 + 128, :]),
                waits=[("dve", m_x1re_read), ("dg", p3_dg)], inc="ds")
            m_tr = None
            for outc in range(8):
                def trp2(e, M, outc=outc, qs=qs):
                    return e.transpose(ps_big[:, qs * 1024 + outc * 128: qs * 1024 + (outc + 1) * 128],
                                      dn_sb[:, outc * 256 + qs * 128: outc * 256 + (qs + 1) * 128], ident[:])
                m_tr = st("tensor", trp2,
                          waits=([("act", m_dnev[7]), ("dve", m_x2add[i2])] if outc == 0 else ()),
                          inc=("pe" if outc == 7 else None))
            m_tr2_prev = m_tr
            m_add = st("vector", lambda e, M, i2=i2, qs=qs: e.tensor_tensor(
                x2pre[i2][:], ps_big[:, qs * 1024:(qs + 1) * 1024], x1re[:], OP.add),
                waits=[("pe", m_tr), ("ds", m_ld), ("dg", m_x2store[i2]),
                       ("act", m_sq_prev), ("dve", m_ln_prev)], inc="dve")
            m_x1re_read = m_add
            m_x2add[i2] = m_add
            m_red = st("vector", lambda e, M, i2=i2: e.tensor_reduce(
                ln_sum[:], x2pre[i2][:], AX.X, OP.add), waits=[("dve", m_add), ("dve", m_ln_prev)], inc="dve")
            m_sq = st("scalar", lambda e, M, i2=i2: e.activation(
                sq2[:], x2pre[i2][:], AF.Square, accum_out=ln_ssq[:, 0:1]),
                waits=[("dve", m_add), ("dve", m_var0_prev)], inc="act")
            m_sq_prev = m_sq
            m_mean = st("vector", lambda e, M: e.tensor_scalar(
                ln_mean[:], ln_sum[:], 1.0 / D, None, OP.mult), waits=[("dve", m_red)], inc="dve")
            m_m2 = st("vector", lambda e, M: e.tensor_tensor(
                ln_m2[:], ln_mean[:], ln_mean[:], OP.mult), waits=[("dve", m_mean)], inc="dve")
            m_var0 = st("vector", lambda e, M: e.tensor_scalar(
                ln_var[:], ln_ssq[:], 1.0 / D, EPS, OP.mult, OP.add),
                waits=[("act", m_sq), ("dve", m_m2)], inc="dve")
            m_var0_prev = m_var0
            m_var = st("vector", lambda e, M: e.tensor_tensor(
                ln_var[:], ln_var[:], ln_m2[:], OP.subtract), waits=[("dve", m_var0)], inc="dve")
            m_sd = st("scalar", lambda e, M: e.activation(
                ln_sd[:], ln_var[:], AF.Sqrt), waits=[("dve", m_var)], inc="act")
            m_rstd = st("vector", lambda e, M: e.reciprocal(
                ln_rstd[:], ln_sd[:]), waits=[("act", m_sd)], inc="dve")
            m_app = st("vector", lambda e, M, i2=i2: e.tensor_scalar(
                x2pre[i2][:], x2pre[i2][:], ln_mean[:, 0:1], ln_rstd[:, 0:1], OP.subtract, OP.mult),
                waits=[("dve", m_rstd)], inc="dve")
            m_ln_prev = m_app
            m_x2store[i2] = st("gpsimd", lambda e, M, i2=i2, tok0=tok0: e.dma_start(
                out=out[tok0:tok0 + 128, :], in_=x2pre[i2][:]), waits=[("dve", m_app)], inc="dg")

    final_dg = sc.n["dg"]
    sc.q["gpsimd"].append(lambda e, SM: e.wait_ge(SM["dg"], final_dg))
    sc.meta["gpsimd"].append(((("dg", final_dg),), None, "final"))
    sc.check()

    from contextlib import ExitStack
    es = ExitStack()
    SM = {s: es.enter_context(nc.semaphore(s)) for s in ("ds", "dg", "pe", "act", "dve")}
    block = es.enter_context(nc.Block())

    @block.sync
    def _(e):
        for f in sc.q["sync"]:
            f(e, SM)

    @block.gpsimd
    def _(e):
        for f in sc.q["gpsimd"]:
            f(e, SM)

    @block.tensor
    def _(e):
        for f in sc.q["tensor"]:
            f(e, SM)

    @block.scalar
    def _(e):
        for f in sc.q["scalar"]:
            f(e, SM)

    @block.vector
    def _(e):
        for f in sc.q["vector"]:
            f(e, SM)

    es.close()
    return nc


_NC_CACHE = None


def kernel(**inputs):
    global _NC_CACHE
    X = np.asarray(inputs["L0_attn_X"], dtype=np.float32)
    args = {
        "wq": np.ascontiguousarray(inputs["L0_attn_Wq"], dtype=np.float32),
        "wk": np.ascontiguousarray(inputs["L0_attn_Wk"], dtype=np.float32),
        "wv": np.ascontiguousarray(inputs["L0_attn_Wv"], dtype=np.float32),
        "wo": np.ascontiguousarray(inputs["L0_attn_Wo"], dtype=np.float32),
        "bq": np.ascontiguousarray(inputs["L0_attn_bq"], dtype=np.float32),
        "bk": np.ascontiguousarray(inputs["L0_attn_bk"], dtype=np.float32),
        "bv": np.ascontiguousarray(inputs["L0_attn_bv"], dtype=np.float32),
        "bo": np.ascontiguousarray(inputs["L0_attn_bo"], dtype=np.float32),
        "wg": np.ascontiguousarray(inputs["L0_ffn_w_gate"], dtype=np.int32),
        "wu": np.ascontiguousarray(inputs["L0_ffn_w_up"], dtype=np.int32),
        "wd": np.ascontiguousarray(inputs["L0_ffn_w_down"], dtype=np.int32),
    }
    if _NC_CACHE is None:
        _NC_CACHE = build_nc()
    nc = _NC_CACHE
    in_maps = []
    for c in range(8):
        b, qh = c // 2, c % 2
        xb = X[b]
        if qh:
            xb = np.roll(xb, -T, axis=0)
        m = {"x_in": np.ascontiguousarray(xb)}
        m.update(args)
        in_maps.append(m)
    res = run_bass_kernel_spmd(nc, in_maps, core_ids=list(range(8)))
    outp = np.empty((B, S, D), dtype=np.float32)
    for c in range(8):
        b, qh = c // 2, c % 2
        outp[b, qh * T:(qh + 1) * T] = res.results[c]["out"]
    return outp


if __name__ == "__main__":
    rng = np.random.default_rng(0)
    ins = {
        "L0_attn_X": rng.standard_normal((B, S, D), dtype=np.float32),
        "L0_attn_Wq": rng.standard_normal((D, D), dtype=np.float32) / 32,
        "L0_attn_bq": rng.standard_normal(D, dtype=np.float32) * 0.1,
        "L0_attn_Wk": rng.standard_normal((D, D), dtype=np.float32) / 32,
        "L0_attn_bk": rng.standard_normal(D, dtype=np.float32) * 0.1,
        "L0_attn_Wv": rng.standard_normal((D, D), dtype=np.float32) / 32,
        "L0_attn_bv": rng.standard_normal(D, dtype=np.float32) * 0.1,
        "L0_attn_Wo": rng.standard_normal((D, D), dtype=np.float32) / 32,
        "L0_attn_bo": rng.standard_normal(D, dtype=np.float32) * 0.1,
        "L0_ffn_w_gate": rng.integers(-127, 128, (HID, D)).astype(np.int32),
        "L0_ffn_w_up": rng.integers(-127, 128, (HID, D)).astype(np.int32),
        "L0_ffn_w_down": rng.integers(-127, 128, (D, HID)).astype(np.int32),
    }
    y = kernel(**ins)
    print("out", y.shape, y.dtype, np.abs(y).max())
